# version 05
# baseline (speedup 1.0000x reference)
import numpy as np
import ml_dtypes
from contextlib import ExitStack

import concourse.bass as bass
import concourse.mybir as mybir
from concourse.bass_utils import run_bass_kernel_spmd

F32 = mybir.dt.float32
BF16 = mybir.dt.bfloat16
F16 = mybir.dt.float16
F8 = mybir.dt.float8e4
ALU = mybir.AluOpType
AF = mybir.ActivationFunctionType
AX = mybir.AxisListType.X

NCORES = 8
B, L, D = 32, 2048, 1024
SPC = B // NCORES
T = 1024
NPASS = SPC * (L // T)
NTC = T // 128
NFC = D // 128
H, DH = 4, 256
FFH = 4096
NHC = FFH // 128
PADV = 24
PADQ = 4
WPV = PADV + T
WPQ = PADQ + T

ENGINES = ("tensor", "vector", "scalar", "gpsimd", "sync")

AUXW = 464
AXF = 352


class Task:
    __slots__ = ("eng", "fn", "deps", "n_dma", "label", "milestone", "needed")

    def __init__(self, eng, fn, deps, n_dma, label):
        self.eng, self.fn, self.deps, self.n_dma, self.label = eng, fn, deps, n_dma, label
        self.milestone = None
        self.needed = False


class Bufr:
    __slots__ = ("w", "readers", "name")

    def __init__(self, name=""):
        self.w = None
        self.readers = []
        self.name = name


class Builder:
    def __init__(self, serialize=False):
        self.tasks = {e: [] for e in ENGINES}
        self.all = []
        self.arenas = {}
        self.serialize = serialize

    def mk(self, eng, fn, reads=(), writes=(), extra=(), fresh=(), n_dma=0, label=""):
        deps = [d for d in extra if d is not None]
        if self.serialize and self.all:
            deps.append(self.all[-1])
        for a in fresh:
            for b in self.arenas[a]:
                deps.extend(b.readers)
                if b.w is not None:
                    deps.append(b.w)
        for b in reads:
            if b.w is not None:
                deps.append(b.w)
        for b in writes:
            deps.extend(b.readers)
            if b.w is not None:
                deps.append(b.w)
        t = Task(eng, fn, deps, n_dma, label)
        self.tasks[eng].append(t)
        self.all.append(t)
        for b in reads:
            b.readers.append(t)
        for b in writes:
            b.w = t
            b.readers = []
        return t

    def finalize(self):
        for t in self.all:
            for d in t.deps:
                if d.eng != t.eng:
                    d.needed = True
        for e in ENGINES:
            cum = 0
            for t in self.tasks[e]:
                if t.needed:
                    cum += 16 * t.n_dma if t.n_dma else 1
                    t.milestone = cum

    def emitter(self, ename, sems):
        def run(engh):
            hw = dict.fromkeys(ENGINES, 0)
            for t in self.tasks[ename]:
                for d in t.deps:
                    if d.eng == ename:
                        continue
                    v = d.milestone
                    assert v is not None, f"{t.label}: dep {d.label} lacks milestone"
                    if v > hw[d.eng]:
                        engh.wait_ge(sems[d.eng], v)
                        hw[d.eng] = v
                insts = t.fn(engh)
                if t.needed:
                    if t.n_dma:
                        assert isinstance(insts, (list, tuple)) and len(insts) == t.n_dma, t.label
                        for i in insts:
                            i.then_inc(sems[ename], 16)
                    else:
                        last = insts[-1] if isinstance(insts, (list, tuple)) else insts
                        last.then_inc(sems[ename], 1)
        return run


def host_consts():
    b = ml_dtypes.bfloat16
    eye = np.eye(128, dtype=np.float32)
    lm = np.tril(np.ones((128, 128), np.float32), -1)
    um = np.triu(np.ones((128, 128), np.float32), 0)
    bc = np.zeros((128, 169), np.float32)
    bc[:, 112] = 1.0
    s8 = np.zeros((8, 8 * 128), np.float32)
    for h in range(8):
        s8[h, h * 128:(h + 1) * 128] = 1.0
    s16 = np.zeros((16, 16 * 128), np.float32)
    for h in range(16):
        s16[h, h * 128:(h + 1) * 128] = 1.0
    rm = np.zeros((128, 2), np.float32)
    for k in range(14):
        for h in range(4):
            rm[k * 4 + h, 1 if (k < 8 and k % 2 == 1) else 0] = 1.0
    return {"c_eye": eye.astype(b), "c_lmask": lm.astype(b),
            "c_umaski": um.astype(b), "c_bigcol": bc.astype(b),
            "c_sel8": s8.astype(b), "c_sel16": s16.astype(b),
            "c_rowmask": rm}


def build_program(debug_taps=(), serialize=False):
    nc = bass.Bass("TRN2", target_bir_lowering=False, detect_race_conditions=False)
    BD = Builder(serialize=serialize)
    mk = BD.mk

    dx = nc.dram_tensor("x", [SPC, L, D], F32, kind="ExternalInput").ap()
    dout = nc.dram_tensor("out", [SPC, L, D], F32, kind="ExternalOutput").ap()
    dacc = nc.dram_tensor("s_acc", [T, D], F32, kind="Internal").ap()
    dw = {}
    for nm, shp in [("Wq", [D, D]), ("Wk", [D, D]), ("Wv", [D, D]),
                    ("r_w1", [1080, 2160]), ("r_w2", [2160, 16]), ("Wo", [D, D]),
                    ("w1", [FFH, D]), ("w2", [D, FFH]), ("b2", [D]),
                    ("aux", [128, AUXW])]:
        dw[nm] = nc.dram_tensor(nm, shp, F32, kind="ExternalInput").ap()
    dcn = {nm: nc.dram_tensor(nm, shp, BF16, kind="ExternalInput").ap()
           for nm, shp in [("c_eye", [128, 128]), ("c_lmask", [128, 128]),
                           ("c_umaski", [128, 128]), ("c_bigcol", [128, 169]),
                           ("c_sel8", [8, 1024]), ("c_sel16", [16, 2048])]}
    dcn["c_rowmask"] = nc.dram_tensor("c_rowmask", [128, 2], F32, kind="ExternalInput").ap()
    s_w = {nm: nc.dram_tensor(f"s_{nm}", shp, dt_, kind="Internal").ap()
           for nm, shp, dt_ in [("wq", [128, 8192], BF16), ("wk", [128, 8192], BF16),
                                ("wv", [128, 8192], BF16), ("wo", [128, 8192], BF16),
                                ("rw1", [9, 128, 2304], BF16),
                                ("w1t", [4, 128, 8192], BF16),
                                ("w2t", [2, 128, 16384], F8)]}
    d_dbg = {nm: nc.dram_tensor(nm, list(shp), dt_, kind="ExternalOutput").ap()
             for nm, shp, dt_ in debug_taps}
    tap_tasks = []

    def tap(name, ap, reads=(), extra=()):
        if name not in d_dbg:
            return
        tap_tasks.append(mk("gpsimd",
                            lambda q, ap=ap, name=name: [q.dma_start(out=d_dbg[name][:], in_=ap)],
                            reads=list(reads), extra=[e for e in extra if e is not None],
                            n_dma=1, label=f"tap_{name}"))

    est = ExitStack()
    E = est.enter_context
    sems = {en: E(nc.semaphore(f"sm_{en}")) for en in ENGINES}

    BUFS = {}

    def sbuf(name, shape, dt_, nbuf=1):
        h = E(nc.sbuf_tensor(name, shape, dt_))
        BUFS[name] = Bufr(name) if nbuf == 1 else [Bufr(f"{name}{i}") for i in range(nbuf)]
        return h

    def vbuf(name, nbuf=1, arena=None):
        BUFS[name] = Bufr(name) if nbuf == 1 else [Bufr(f"{name}{i}") for i in range(nbuf)]
        if arena:
            bs = BUFS[name] if isinstance(BUFS[name], list) else [BUFS[name]]
            BD.arenas.setdefault(arena, []).extend(bs)

    def psum(name):
        h = E(nc.psum_tensor(name, [128, 512], F32))
        BUFS[name] = Bufr(name)
        return h

    def bf(name, i=None):
        b = BUFS[name]
        return b if i is None else b[i]

    x_t = sbuf("x_t", [128, 1024], F32)
    xn_fm = sbuf("xn_fm", [128, NFC * 1024], BF16)
    ak = sbuf("ak", [128, NFC * WPQ], BF16, nbuf=NFC)
    av = sbuf("av", [128, NFC * WPV], BF16, nbuf=NFC)
    tails = sbuf("tails", [128, 2 * NFC * PADQ + NFC * 3 + NFC * PADV], BF16)
    VTOF = 2 * NFC * PADQ + NFC * 3
    loc_h = sbuf("loc_h", [128, NFC * 512], BF16)
    mid_h = sbuf("mid_h", [128, NFC * 512], BF16)
    del_fm = sbuf("del_fm", [128, NFC * 1024], BF16)
    beta_tm = sbuf("beta_tm", [128, NTC * H], F32)
    stat_t = sbuf("stat_t", [128, 4 * NTC], F32)
    s_state = sbuf("s_state", [128, 2 * H * 256], F32)
    s_bf = sbuf("s_bf", [128, 2 * H * 256], BF16)
    rf_tail = sbuf("rf_tail", [128, 512], BF16)
    h1t = sbuf("h1t", [128, 2 * 512], BF16)
    p_tm = sbuf("p_tm", [128, 16], F32)
    p_tmb = sbuf("p_tmb", [128, 16], BF16)
    p_fm = sbuf("p_fm", [16, 512], BF16)
    rs_bf2 = sbuf("rs_bf2", [4, 512], BF16)
    eye = sbuf("eye", [128, 128], BF16)
    lmask = sbuf("lmask", [128, 128], BF16)
    umaski = sbuf("umaski", [128, 128], BF16)
    bigcol = sbuf("bigcol", [128, 169], BF16)
    sel8 = sbuf("sel8", [8, 1024], BF16)
    sel16 = sbuf("sel16", [16, 2048], BF16)
    rowmask = sbuf("rowmask", [128, 2], F32)
    wb_r = sbuf("wb_r", [128, NFC * H], BF16)
    auxb = sbuf("auxb", [128, AXF], F32)
    auxf = sbuf("auxf", [128, AUXW - AXF], F32)
    rw2_r = sbuf("rw2_r", [128, 17 * 16], BF16)
    g_col = sbuf("g_col", [128, NFC], F32)
    w2col = sbuf("w2col", [128, NFC], F32)
    b1_r = auxf[:, 0:32]
    rb1_r = auxf[:, 32:49]
    mix8 = auxf[:, 49:57]
    onorm2 = auxf[:, 57:59]
    rb2_bc = auxf[:, 59:75]
    wbf32 = auxf[:, 75:107]
    bq_row = sbuf("bq_row", [1, 1024], F32)
    ones32 = sbuf("ones32", [1, 128], F32)
    smallb = sbuf("smallb", [128, 16], F32)

    u4 = sbuf("u4_raw", [128, 24608], BF16)
    u5 = sbuf("u5_raw", [128, 8192], BF16)
    u6 = sbuf("u6_raw", [128, 9216], BF16)
    u7 = sbuf("u7_raw", [128, 2048], BF16)
    u8 = sbuf("u8_raw", [128, 2048], BF16)
    for un in ("u4_raw", "u5_raw", "u6_raw", "u7_raw", "u8_raw"):
        del BUFS[un]

    wq_v = u4[:, 0:8192]
    wk_v = u4[:, 8192:16384]
    wv_v = u4[:, 0:8192]
    aq_v = u4[:, 16384:16384 + NFC * WPQ]
    w1t_v = [u4[:, 0:8192], u4[:, 8192:16384]]
    hq_v = u4[:, 16384:24576].bitcast(F8)
    pc_in = u4[:, 0:8192].bitcast(F32)
    pc_bf = u4[:, 8192:12288]
    pc_f8 = u4[:, 12288:14336].bitcast(F8)
    pc_t = u4[:, 14336:18432]
    pc_t8 = u4[:, 18432:20480].bitcast(F8)
    vbuf("wq_s", arena="A4"); vbuf("wk_s", arena="A4"); vbuf("wv_s", arena="A4")
    vbuf("aq", nbuf=NFC, arena="A4")
    vbuf("w1t", nbuf=2, arena="A4"); vbuf("hq", arena="A4")
    vbuf("pc_in", arena="A4"); vbuf("pc_bf", arena="A4"); vbuf("pc_f8", arena="A4")
    vbuf("pc_t", arena="A4"); vbuf("pc_t8", arena="A4")

    rw1_v = [u5[:, 0:2304], u5[:, 2304:4608]]
    wo_v = u5[:, 0:8192]
    w2t_v = u5[:, 0:8192].bitcast(F8)
    vbuf("rw1", nbuf=2, arena="A5"); vbuf("wo_s", arena="A5"); vbuf("w2t", arena="A5")

    powtf_v = u6[:, 0:3584].bitcast(F32)
    powuf_v = u6[:, 3584:4096].bitcast(F32)
    xchf_v = u6[:, 4096:5120].bitcast(F32)
    xchb_v = u6[:, 5120:5632]
    abuf_v = u6[:, 5632:5888]
    attnb_v = u6[:, 5888:6144]
    ktm_v = u6[:, 6144:6656]
    vtm_v = u6[:, 6656:7168]
    otm_v = u6[:, 7168:7680]
    pb_v = u6[:, 0:2048]
    mixed_v = u6[:, 2048:6144]
    vbuf("pow_t", nbuf=2, arena="A6"); vbuf("pow_u", nbuf=2, arena="A6")
    vbuf("abuf", nbuf=2, arena="A6")
    vbuf("attnb", nbuf=2, arena="A6")
    vbuf("ktm", nbuf=2, arena="A6"); vbuf("vtm", nbuf=2, arena="A6")
    vbuf("xch", nbuf=2, arena="A6"); vbuf("xchb", nbuf=2, arena="A6")
    vbuf("otm", nbuf=2, arena="A6")
    vbuf("pb", nbuf=4, arena="A6"); vbuf("mixed", nbuf=NFC, arena="A6")

    ln_scr = u7[:, 0:2048].bitcast(F32)
    rsq_v = u7[:, 0:1024].bitcast(F32)
    rsb_v = u7[:, 1024:1536]
    rsA = u7[:, 0:1024].bitcast(F32)
    rsB = u7[:, 1024:2048].bitcast(F32)
    vbuf("ln_scr", arena="A7"); vbuf("rsq", arena="A7"); vbuf("rsb", arena="A7")
    vbuf("rsA", arena="A7"); vbuf("rsB", arena="A7")

    xn_t_v = u8[:, 0:2048]
    prod_v = [u8[:, 0:512], u8[:, 512:1024]]
    sq_v = [u8[:, 1024:1536], u8[:, 1536:2048]]
    c16_v = [u8[:, 0:512].bitcast(F16), u8[:, 512:1024].bitcast(F16)]
    bqf16_v = u8[:, 0:1024].bitcast(F16)
    cacc = [u8[:, 0:1024].bitcast(F32), u8[:, 1024:2048].bitcast(F32)]
    vbuf("xn_t", nbuf=2, arena="A8")
    vbuf("prod", nbuf=2, arena="A8"); vbuf("sq", nbuf=2, arena="A8")
    vbuf("c16", nbuf=2, arena="A8"); vbuf("bqf16", arena="A8")
    vbuf("cacc", nbuf=2, arena="A8")
    vbuf("dacc", nbuf=NTC)

    def xnt(i):
        return xn_t_v[:, i * 1024:(i + 1) * 1024]

    ps_mm = [psum("ps_mm0"), psum("ps_mm1")]
    ps_pw = psum("ps_pw")
    ps_ms = psum("ps_ms")
    ps_ch = psum("ps_ch")
    ps_su = psum("ps_su")
    ps_ft = psum("ps_ft")
    ps_bc = psum("ps_bc")
    PSB = ["ps_mm0", "ps_mm1", "ps_pw", "ps_ms", "ps_ch", "ps_su", "ps_ft", "ps_bc"]
    PSH = {"ps_mm0": ps_mm[0], "ps_mm1": ps_mm[1], "ps_pw": ps_pw, "ps_ms": ps_ms,
           "ps_ch": ps_ch, "ps_su": ps_su, "ps_ft": ps_ft, "ps_bc": ps_bc}
    mmb0 = ps_mm[0][:].bitcast(BF16)
    msb = ps_ms[:].bitcast(BF16)
    sub = ps_su[:].bitcast(BF16)
    bcb = ps_bc[:].bitcast(BF16)

    CWO = {"q": 0, "k": 4, "v": 8, "mid": 12}
    LOCO = 37
    CWW = 44

    mk("sync", lambda q: [
        q.dma_start(out=eye[:], in_=dcn["c_eye"][:]),
        q.dma_start(out=lmask[:], in_=dcn["c_lmask"][:]),
        q.dma_start(out=umaski[:], in_=dcn["c_umaski"][:]),
        q.dma_start(out=bigcol[:], in_=dcn["c_bigcol"][:]),
        q.dma_start(out=sel8[:], in_=dcn["c_sel8"][:]),
        q.dma_start(out=sel16[:], in_=dcn["c_sel16"][:]),
        q.dma_start(out=rowmask[:], in_=dcn["c_rowmask"][:]),
    ], writes=[bf("eye"), bf("lmask"), bf("umaski"), bf("bigcol"), bf("sel8"), bf("sel16"),
               bf("rowmask")], n_dma=7, label="c")

    t_auxc = mk("sync", lambda q: [
        q.dma_start(out=auxb[:], in_=dw["aux"][:, 0:AXF]),
        q.dma_start(out=auxf[:], in_=dw["aux"][:, AXF:AUXW]),
        q.dma_start(out=bq_row[:], in_=dw["b2"].unsqueeze(0)),
    ], writes=[bf("auxb"), bf("auxf"), bf("bq_row")], n_dma=3, label="auxc")

    def fn_prep(v):
        on8 = onorm2.unsqueeze(1).broadcast_to([128, 4, 2])
        v.tensor_tensor(g_col[:].rearrange("p (a c) -> p a c", c=2),
                        mix8.rearrange("p (a c) -> p a c", c=2),
                        on8, op=ALU.mult)
        v.tensor_tensor(w2col[:], mix8, mix8, op=ALU.mult)
        v.memset(smallb[:, 2:3], 1e-5)
        v.memset(smallb[:, 3:4], -0.5)
        v.memset(smallb[:, 4:5], 0.5)
        v.memset(smallb[:, 5:6], 1e-12)
        v.memset(ones32[:], 1.0)
        v.memset(rf_tail[:], 0.0)
        v.tensor_copy(wb_r[:], wbf32)
        v.tensor_copy(bqf16_v[0:1, :], bq_row[:])
        return v.tensor_copy(bq_row[:], bqf16_v[0:1, :])
    t_prep = mk("vector", fn_prep, reads=[bf("auxf"), bf("bq_row")],
                writes=[bf("g_col"), bf("w2col"), bf("bq_row"), bf("smallb"),
                        bf("ones32"), bf("wb_r"), bf("rf_tail"), bf("bqf16")],
                extra=[t_auxc], label="prep")

    t_rw2l = mk("sync", lambda q: [
        q.dma_start(out=ln_scr[0:(128 if k < 16 else 112), k * 16:(k + 1) * 16],
                    in_=dw["r_w2"][k * 128:k * 128 + (128 if k < 16 else 112), :])
        for k in range(17)],
        writes=[bf("ln_scr")], n_dma=17, label="rw2l")

    def fn_rw2(v):
        v.memset(rw2_r[:], 0.0)
        return v.tensor_copy(rw2_r[:, 0:17 * 16], ln_scr[:, 0:17 * 16])
    mk("vector", fn_rw2, reads=[bf("ln_scr")], writes=[bf("rw2_r")],
       extra=[t_rw2l, t_prep], label="rw2")

    pc = {}

    def pc_direct(name, dram, nk, ncols, store_fn, n_store):
        last = None
        for kc in range(nk):
            rows = min(128, dram.shape[0] - kc * 128)
            mk("sync", lambda q, r=rows, c=ncols, k=kc, d=dram: [
                q.dma_start(out=pc_in[0:r, 0:c], in_=d[k * 128:k * 128 + r, :])],
               writes=[bf("pc_in")], fresh=("A4",) if (name == "wq" and kc == 0) else (),
               n_dma=1, label=f"pcl_{name}{kc}")
            mk("vector", lambda v, r=rows, c=ncols: v.tensor_copy(pc_bf[0:r, 0:c], pc_in[0:r, 0:c]),
               reads=[bf("pc_in")], writes=[bf("pc_bf")], label=f"pcc_{name}{kc}")
            last = mk("gpsimd", lambda q, k=kc, r=rows: store_fn(q, k, r),
                      reads=[bf("pc_bf")], n_dma=n_store, label=f"pcs_{name}{kc}")
        return last

    def st_w(s):
        return lambda q, k, r: [q.dma_start(out=s[0:r, k * 1024:(k + 1) * 1024],
                                            in_=pc_bf[0:r, 0:1024])]

    def st_rw1(q, k, r):
        return [q.dma_start(
            out=s_w["rw1"][mcg, 0:r, k * 256:k * 256 + min(256, 2160 - mcg * 256)],
            in_=pc_bf[0:r, mcg * 256:mcg * 256 + min(256, 2160 - mcg * 256)])
            for mcg in range(9)]

    pc["wq"] = pc_direct("wq", dw["Wq"], 8, 1024, st_w(s_w["wq"]), 1)
    pc["wk"] = pc_direct("wk", dw["Wk"], 8, 1024, st_w(s_w["wk"]), 1)
    pc["wv"] = pc_direct("wv", dw["Wv"], 8, 1024, st_w(s_w["wv"]), 1)
    pc["wo"] = pc_direct("wo", dw["Wo"], 8, 1024, st_w(s_w["wo"]), 1)
    pc["rw1"] = pc_direct("rw1", dw["r_w1"], 9, 2160, st_rw1, 9)

    for hc in range(NHC):
        mk("sync", lambda q, k=hc: [
            q.dma_start(out=pc_in[:, 0:1024], in_=dw["w1"][k * 128:(k + 1) * 128, :])],
           writes=[bf("pc_in")], n_dma=1, label=f"pcl_w1{hc}")
        mk("vector", lambda v: v.tensor_copy(pc_bf[:, 0:1024], pc_in[:, 0:1024]),
           reads=[bf("pc_in")], writes=[bf("pc_bf")], label=f"pcc_w1{hc}")

        def fn_tr(te):
            o = []
            for fc in range(8):
                o.append(te.transpose(mmb0[:, fc * 128:(fc + 1) * 128],
                                      pc_bf[:, fc * 128:(fc + 1) * 128], eye[:]))
            return o[-1]
        mk("tensor", fn_tr, reads=[bf("pc_bf"), bf("eye")],
           writes=[bf("ps_mm0")], label=f"pct_w1{hc}")

        def fn_ev(v):
            return v.tensor_copy(pc_t[:, 0:1024], mmb0[:, 0:1024])
        mk("vector", fn_ev, reads=[bf("ps_mm0")], writes=[bf("pc_t")],
           label=f"pce_w1{hc}")
        pc["w1t"] = mk("gpsimd", lambda q, k=hc: [
            q.dma_start(out=s_w["w1t"][k // 8, :,
                                       kc * 1024 + (k % 8) * 128:kc * 1024 + (k % 8) * 128 + 128],
                        in_=pc_t[:, kc * 128:(kc + 1) * 128]) for kc in range(8)],
            reads=[bf("pc_t")], n_dma=8, label=f"pcs_w1{hc}")

    for kc in range(8):
        for hf in range(2):
            mk("sync", lambda q, k=kc, h=hf: [
                q.dma_start(out=pc_in[:, 0:2048],
                            in_=dw["w2"][k * 128:(k + 1) * 128, h * 2048:(h + 1) * 2048])],
               writes=[bf("pc_in")], n_dma=1, label=f"pcl_w2{kc}_{hf}")

            def fn_q(v):
                v.tensor_copy(pc_f8[:, 0:2048], pc_in[:, 0:2048])
                return v.tensor_copy(pc_bf[:, 0:2048], pc_f8[:, 0:2048])
            mk("vector", fn_q, reads=[bf("pc_in")], writes=[bf("pc_bf"), bf("pc_f8")],
               label=f"pcq_w2{kc}_{hf}")
            for g in range(2):
                def fn_tr(te, g=g):
                    o = []
                    for j in range(8):
                        o.append(te.transpose(mmb0[:, j * 128:(j + 1) * 128],
                                              pc_bf[:, (g * 8 + j) * 128:(g * 8 + j + 1) * 128],
                                              eye[:]))
                    return o[-1]
                mk("tensor", fn_tr, reads=[bf("pc_bf"), bf("eye")],
                   writes=[bf("ps_mm0")], label=f"pct_w2{kc}_{hf}_{g}")

                def fn_ev(v, g=g):
                    return v.tensor_copy(pc_t8[:, g * 1024:(g + 1) * 1024], mmb0[:, 0:1024])
                mk("vector", fn_ev, reads=[bf("ps_mm0")], writes=[bf("pc_t8")],
                   label=f"pce_w2{kc}_{hf}_{g}")
                pc["w2t"] = mk("gpsimd", lambda q, k=kc, h=hf, g=g: [
                    q.dma_start(out=s_w["w2t"][h, :,
                                               (g * 8 + j) * 1024 + k * 128:(g * 8 + j) * 1024 + (k + 1) * 128],
                                in_=pc_t8[:, (g * 8 + j) * 128:(g * 8 + j + 1) * 128])
                    for j in range(8)],
                    reads=[bf("pc_t8")], n_dma=8, label=f"pcs_w2{kc}_{hf}_{g}")

    final_dma = []
    for p_i in range(NPASS):
        seq, half = p_i // 2, p_i % 2
        tok0 = half * T

        t_lwq = mk("sync", lambda q: [q.dma_start(out=wq_v, in_=s_w["wq"][:])],
                   writes=[bf("wq_s")], fresh=("A4",), extra=[pc["wq"]], n_dma=1,
                   label=f"lwq{p_i}")
        t_lwk = mk("sync", lambda q: [q.dma_start(out=wk_v, in_=s_w["wk"][:])],
                   writes=[bf("wk_s")], extra=[pc["wk"]], n_dma=1, label=f"lwk{p_i}")

        t_xn_done = None
        prev_ln = None
        for tc in range(NTC):
            pi = tc % 2
            t_xl = mk("sync", lambda q, s=seq, t0=tok0, tc=tc: [
                q.dma_start(out=x_t[:], in_=dx[s, t0 + tc * 128:t0 + (tc + 1) * 128, :])],
                writes=[bf("x_t")], n_dma=1, label=f"xl{p_i}_{tc}")
            if p_i == 0 and tc == 0:
                tap("t_xt", x_t[:], reads=[bf("x_t")])

            def fn_st(v, tc=tc):
                c0 = stat_t[:, tc * 4:tc * 4 + 1]
                c1 = stat_t[:, tc * 4 + 1:tc * 4 + 2]
                c2 = stat_t[:, tc * 4 + 2:tc * 4 + 3]
                c3 = stat_t[:, tc * 4 + 3:tc * 4 + 4]
                v.reduce_sum(c0, x_t[:], axis=AX)
                v.tensor_tensor(ln_scr[:, 0:1024], x_t[:], x_t[:], op=ALU.mult)
                v.tensor_scalar(c0, c0, 1.0 / 1024, None, op0=ALU.mult)
                v.reduce_sum(c1, ln_scr[:, 0:1024], axis=AX)
                v.tensor_tensor(c2, c0, c0, op=ALU.mult)
                v.tensor_copy(c3, c0)
                return v.scalar_tensor_tensor(c1, c1, 1.0 / 1024, c2,
                                              op0=ALU.mult, op1=ALU.subtract)
            t_st = mk("vector", fn_st, reads=[bf("x_t")], writes=[bf("stat_t"), bf("ln_scr")],
                      fresh=("A7",) if tc == 0 else (), extra=[t_prep, prev_ln],
                      label=f"lnst{p_i}_{tc}")
            if p_i == 0 and tc == 0:
                tap("t_lnscr", ln_scr[:, 0:1024], reads=[bf("ln_scr")], extra=[t_st])

            def fn_rs(sc, tc=tc):
                sc.activation(stat_t[:, tc * 4 + 2:tc * 4 + 3],
                              stat_t[:, tc * 4 + 1:tc * 4 + 2], AF.Ln,
                              bias=smallb[:, 2:3])
                return sc.activation(stat_t[:, tc * 4 + 3:tc * 4 + 4],
                                     stat_t[:, tc * 4 + 2:tc * 4 + 3], AF.Exp,
                                     scale=smallb[:, 3:4])
            t_rs = mk("scalar", fn_rs, reads=[bf("smallb")], writes=[bf("stat_t")],
                      extra=[t_st], label=f"lnrs{p_i}_{tc}")

            def fn_xn(v, tc=tc, pi=pi):
                return v.tensor_scalar(xnt(pi), x_t[:],
                                       stat_t[:, tc * 4:tc * 4 + 1],
                                       stat_t[:, tc * 4 + 3:tc * 4 + 4],
                                       op0=ALU.subtract, op1=ALU.mult)
            t_xn = mk("vector", fn_xn, reads=[bf("x_t"), bf("stat_t")],
                      writes=[bf("xn_t", pi)],
                      fresh=("A8",) if (tc == 0) else (), extra=[t_rs],
                      label=f"xn{p_i}_{tc}")
            prev_ln = t_xn

            def fn_tr(te, pi=pi):
                o = []
                for fc in range(NFC):
                    o.append(te.transpose(mmb0[:, fc * 128:(fc + 1) * 128],
                                          xnt(pi)[:, fc * 128:(fc + 1) * 128], eye[:]))
                return o[-1]
            t_tr = mk("tensor", fn_tr, reads=[bf("xn_t", pi), bf("eye")],
                      writes=[bf("ps_mm0")], label=f"xnT{p_i}_{tc}")

            def fn_ev(v, tc=tc):
                return v.tensor_copy(
                    xn_fm[:].rearrange("p (f c) -> p f c", c=1024)[:, :, tc * 128:(tc + 1) * 128],
                    mmb0[:, 0:1024].rearrange("p (f c) -> p f c", c=128))
            t_xn_done = mk("vector", fn_ev, reads=[bf("ps_mm0")],
                           writes=[bf("xn_fm")], extra=[t_tr], label=f"xnE{p_i}_{tc}")

        if p_i == 0:
            tap("t_stat", stat_t[:], reads=[bf("stat_t")])
            tap("t_xnfm", xn_fm[:], reads=[bf("xn_fm")])

        def fn_pad(v, half=half):
            last = None
            for bi, (dst, wp, pd) in enumerate(((aq_v, WPQ, PADQ), (ak, WPQ, PADQ),
                                                (av, WPV, PADV))):
                for fc in range(NFC):
                    if half == 0:
                        last = v.memset(dst[:, fc * wp:fc * wp + pd], 0.0)
                    elif bi < 2:
                        tof = bi * NFC * PADQ
                        last = v.tensor_copy(dst[:, fc * wp:fc * wp + pd],
                                             tails[:, tof + fc * pd:tof + (fc + 1) * pd])
                    else:
                        tof = 2 * NFC * PADQ
                        last = v.tensor_copy(dst[:, fc * wp + pd - 3:fc * wp + pd],
                                             tails[:, tof + fc * 3:tof + (fc + 1) * 3])
            return last
        t_pad = mk("vector", fn_pad, reads=[bf("tails")],
                   writes=[bf("aq", i) for i in range(NFC)] + [bf("ak", i) for i in range(NFC)]
                   + [bf("av", i) for i in range(NFC)], label=f"pad{p_i}")

        def proj(pname, wview, wname, dst, dname, wp, pd, dep):
            for mc in range(NFC):
                for nt in range(2):
                    bk = (mc * 2 + nt) % 2

                    def fn_mm(te, w=wview, mc=mc, nt=nt, bk=bk):
                        o = []
                        for kc in range(NFC):
                            o.append(te.matmul(
                                ps_mm[bk][:],
                                w[:, kc * 1024 + mc * 128:kc * 1024 + (mc + 1) * 128],
                                xn_fm[:, kc * 1024 + nt * 512:kc * 1024 + (nt + 1) * 512],
                                start=(kc == 0), stop=(kc == NFC - 1)))
                        return o[-1]
                    t_mm = mk("tensor", fn_mm, reads=[bf(wname), bf("xn_fm")],
                              writes=[bf(f"ps_mm{bk}")], extra=[dep, t_xn_done],
                              label=f"{pname}mm{p_i}_{mc}_{nt}")

                    def fn_ev(v, d=dst, mc=mc, nt=nt, bk=bk, wp=wp, pd=pd):
                        return v.tensor_copy(
                            d[:, mc * wp + pd + nt * 512:mc * wp + pd + (nt + 1) * 512],
                            ps_mm[bk][:])
                    mk("vector", fn_ev, reads=[bf(f"ps_mm{bk}")], writes=[bf(dname, mc)],
                       extra=[t_pad], label=f"{pname}ev{p_i}_{mc}_{nt}")

        proj("q", wq_v, "wq_s", aq_v, "aq", WPQ, PADQ, t_lwq)
        t_lwv = mk("sync", lambda q: [q.dma_start(out=wv_v, in_=s_w["wv"][:])],
                   writes=[bf("wv_s")], fresh=("A4",), extra=[pc["wv"]],
                   label=f"lwv{p_i}", n_dma=1)
        proj("k", wk_v, "wk_s", ak, "ak", WPQ, PADQ, t_lwk)
        proj("v", wv_v, "wv_s", av, "av", WPV, PADV, t_lwv)

        t_beta_fin = None
        for tc in range(NTC):
            def fn_bmm(te, tc=tc):
                o = []
                for kc in range(NFC):
                    o.append(te.matmul(ps_ft[:, 0:H],
                                       xn_fm[:, kc * 1024 + tc * 128:kc * 1024 + (tc + 1) * 128],
                                       wb_r[:, kc * H:(kc + 1) * H],
                                       start=(kc == 0), stop=(kc == NFC - 1)))
                return o[-1]
            t_bmm = mk("tensor", fn_bmm, reads=[bf("xn_fm"), bf("wb_r")], writes=[bf("ps_ft")],
                       extra=[t_xn_done], label=f"bmm{p_i}_{tc}")
            t_ba = mk("scalar", lambda sc: sc.activation(smallb[:, 8:12], ps_ft[:, 0:H],
                                                         AF.Tanh, scale=smallb[:, 4:5]),
                      reads=[bf("ps_ft")], writes=[bf("smallb")], extra=[t_bmm],
                      label=f"bact{p_i}_{tc}")
            t_beta_fin = mk("vector",
                            lambda v, tc=tc: v.tensor_scalar(beta_tm[:, tc * H:(tc + 1) * H],
                                                             smallb[:, 8:12], 0.5, 0.5,
                                                             op0=ALU.mult, op1=ALU.add),
                            reads=[bf("smallb")], writes=[bf("beta_tm")], extra=[t_ba],
                            label=f"bfin{p_i}_{tc}")

        def fn_tails(v):
            last = None
            for bi, (dst, wp, pd) in enumerate(((aq_v, WPQ, PADQ), (ak, WPQ, PADQ),
                                                (av, WPV, PADV))):
                for fc in range(NFC):
                    if bi < 2:
                        tof = bi * NFC * PADQ
                        last = v.tensor_copy(tails[:, tof + fc * pd:tof + (fc + 1) * pd],
                                             dst[:, fc * wp + 1024:fc * wp + pd + 1024])
                    else:
                        tof = 2 * NFC * PADQ
                        last = v.tensor_copy(tails[:, tof + fc * 3:tof + (fc + 1) * 3],
                                             dst[:, fc * wp + pd + 1021:fc * wp + pd + 1024])
            return last
        t_tails = mk("vector", fn_tails,
                     reads=[bf("aq", i) for i in range(NFC)] + [bf("ak", i) for i in range(NFC)]
                     + [bf("av", i) for i in range(NFC)],
                     writes=[bf("tails")], label=f"tails{p_i}")

        def conv_block(pname, src, sname, taps_ap, ktaps, dst, dst_off, dstname, act,
                       wp, pd, h2span=(0, 1), extra_dep=None):
            for fc in range(NFC):
                acc_tasks = []
                for ih, nt in enumerate(range(h2span[0], h2span[1] + 1)):
                    def fn_acc(v, fc=fc, nt=nt, ih=ih, taps_ap=taps_ap, ktaps=ktaps,
                               src=src, wp=wp, pd=pd):
                        o = None
                        for s_ in range(ktaps):
                            lag = ktaps - 1 - s_
                            base = fc * wp + pd + nt * 512 - lag
                            if s_ == 0:
                                o = v.tensor_scalar(cacc[ih][:], src[:, base:base + 512],
                                                    taps_ap(fc, s_), None, op0=ALU.mult)
                            else:
                                o = v.scalar_tensor_tensor(cacc[ih][:],
                                                           src[:, base:base + 512],
                                                           taps_ap(fc, s_), cacc[ih][:],
                                                           op0=ALU.mult, op1=ALU.add)
                        return o
                    acc_tasks.append(mk(
                        "vector", fn_acc, reads=[bf(sname, fc), bf("auxb")],
                        writes=[bf("cacc", ih)],
                        fresh=("A8",) if (pname in ("cq", "lo") and fc == 0 and ih == 0) else (),
                        extra=[t_tails, extra_dep], label=f"cva_{pname}{p_i}_{fc}_{ih}"))

                dwr = [bf(dstname, fc) if isinstance(BUFS[dstname], list) else bf(dstname)]
                nsp = h2span[1] - h2span[0] + 1

                def fn_evs(sc, fc=fc, nsp=nsp, act=act, dst=dst, dst_off=dst_off):
                    o = []
                    for ih in range(nsp):
                        o.append(sc.activation(
                            dst[:, dst_off(fc) + ih * 512:dst_off(fc) + (ih + 1) * 512],
                            cacc[ih][:], act if act is not None else AF.Copy))
                    return o[-1]
                mk("scalar", fn_evs,
                   reads=[bf("cacc", ih) for ih in range(nsp)], writes=dwr,
                   extra=acc_tasks, label=f"cve_{pname}{p_i}_{fc}")

        def cw_ap(off):
            return lambda fc, s_: auxb[:, fc * CWW + off + s_:fc * CWW + off + s_ + 1]

        conv_block("cq", aq_v, "aq", cw_ap(CWO["q"]), 4, aq_v,
                   lambda fc: fc * WPQ + PADQ, "aq", AF.Silu, WPQ, PADQ)
        conv_block("ck", ak, "ak", cw_ap(CWO["k"]), 4, ak,
                   lambda fc: fc * WPQ + PADQ, "ak", AF.Silu, WPQ, PADQ)
        conv_block("cv", av, "av", cw_ap(CWO["v"]), 4, av,
                   lambda fc: fc * WPV + PADV, "av", AF.Silu, WPV, PADV)

        def fn_vpad(v, half=half):
            last = None
            tof = VTOF
            for fc in range(NFC):
                reg = av[:, fc * WPV:fc * WPV + PADV]
                last = (v.memset(reg, 0.0) if half == 0 else
                        v.tensor_copy(reg, tails[:, tof + fc * PADV:tof + (fc + 1) * PADV]))
            return last
        t_vpad = mk("vector", fn_vpad, reads=[bf("tails")],
                    writes=[bf("av", i) for i in range(NFC)], label=f"vpad{p_i}")

        def fn_vtail(v):
            last = None
            tof = VTOF
            for fc in range(NFC):
                last = v.tensor_copy(tails[:, tof + fc * PADV:tof + (fc + 1) * PADV],
                                     av[:, fc * WPV + 1024:fc * WPV + PADV + 1024])
            return last
        t_vtail = mk("vector", fn_vtail, reads=[bf("av", i) for i in range(NFC)],
                     writes=[bf("tails")], label=f"vtail{p_i}")

        for h2 in range(2):
            l2_last = None
            for bi, (src, sname) in enumerate(((aq_v, "aq"), (ak, "ak"))):
                for fc in range(NFC):
                    pi = (bi * NFC + fc) % 2

                    def fn_sq(sc, src=src, fc=fc, h2=h2, pi=pi):
                        return sc.activation(sq_v[pi][:],
                                             src[:, fc * WPQ + PADQ + h2 * 512:fc * WPQ + PADQ + (h2 + 1) * 512],
                                             AF.Square)
                    t_sq = mk("scalar", fn_sq, reads=[bf(sname, fc)], writes=[bf("sq", pi)],
                              fresh=("A8",) if (h2 == 0 and bi == 0 and fc == 0) else (),
                              label=f"l2sq{p_i}_{h2}_{bi}_{fc}")
                    first = (bi == 0 and fc == 0)
                    m = bi * 4 + fc // 2

                    def fn_m(te, m=m, pi=pi, first=first, last_=(bi == 1 and fc == NFC - 1)):
                        return te.matmul(ps_ft[0:8, 0:512], bigcol[:, 112 - m:120 - m][:, 0:8],
                                         sq_v[pi][:], start=first, stop=last_)
                    t_m = mk("tensor", fn_m, reads=[bf("sq", pi), bf("bigcol")],
                             writes=[bf("ps_ft")] if first else [],
                             extra=[t_sq, l2_last], label=f"l2mm{p_i}_{h2}_{bi}_{fc}")
                    if not first:
                        BUFS["ps_ft"].readers.append(t_m)
                    l2_last = t_m

            def fn_rs2(sc):
                sc.activation(rsq_v[0:8, 0:512], ps_ft[0:8, 0:512], AF.Ln,
                              bias=smallb[0:8, 5:6])
                return sc.activation(rsq_v[0:8, 0:512], rsq_v[0:8, 0:512], AF.Exp,
                                     scale=smallb[0:8, 3:4])
            t_l2rs = mk("scalar", fn_rs2, reads=[bf("smallb"), bf("ps_ft")], writes=[bf("rsq")],
                        fresh=("A7",) if h2 == 0 else (), extra=[l2_last],
                        label=f"l2rs{p_i}_{h2}")
            BUFS["ps_ft"].readers.append(t_l2rs)

            t_l2bf = mk("vector", lambda v: v.tensor_copy(rsb_v[0:8, :], rsq_v[0:8, 0:512]),
                        reads=[bf("rsq")], writes=[bf("rsb")], extra=[t_l2rs],
                        label=f"l2bf{p_i}_{h2}")

            for bi, (src, sname) in enumerate(((aq_v, "aq"), (ak, "ak"))):
                for fc in range(NFC):
                    hrow = bi * 4 + fc // 2

                    def fn_bc(te, hrow=hrow):
                        return te.matmul(ps_bc[:], sel8[:, hrow * 128:(hrow + 1) * 128],
                                         rsb_v[0:8, :], start=True, stop=True)
                    t_bc = mk("tensor", fn_bc, reads=[bf("rsb"), bf("sel8")],
                              writes=[bf("ps_bc")], extra=[t_l2bf],
                              label=f"l2bc{p_i}_{h2}_{bi}_{fc}")

                    def fn_ap2(v, src=src, fc=fc, h2=h2):
                        sl = src[:, fc * WPQ + PADQ + h2 * 512:fc * WPQ + PADQ + (h2 + 1) * 512]
                        return v.tensor_tensor(sl, sl, ps_bc[:], op=ALU.mult)
                    mk("vector", fn_ap2, reads=[bf("ps_bc"), bf(sname, fc)],
                       writes=[bf(sname, fc)], extra=[t_bc], label=f"l2ap{p_i}_{h2}_{bi}_{fc}")

        if p_i == 0:
            tap("t_beta", beta_tm[:], reads=[bf("beta_tm")])
            tap("t_aq", aq_v, reads=[bf("aq", i) for i in range(NFC)])
            tap("t_ak", ak[:], reads=[bf("ak", i) for i in range(NFC)])
            tap("t_av", av[:], reads=[bf("av", i) for i in range(NFC)])
        if p_i == 1:
            tap("t_av1", av[:], reads=[bf("av", i) for i in range(NFC)])

        if half == 0:
            mk("vector", lambda v: (v.memset(s_state[:], 0.0), v.memset(s_bf[:], 0.0))[-1],
               writes=[bf("s_state"), bf("s_bf")], label=f"s0_{p_i}")

        def qslq(buf, hfc, n):
            return buf[:, hfc * WPQ + PADQ + n * 128:hfc * WPQ + PADQ + (n + 1) * 128]

        def qslv(hfc, n):
            return av[:, hfc * WPV + PADV + n * 128:hfc * WPV + PADV + (n + 1) * 128]

        def pwt(hi, j):
            return powtf_v[:, hi * 896 + j * 128:hi * 896 + (j + 1) * 128]

        def pwu(hi):
            return powuf_v[:, hi * 128:(hi + 1) * 128]

        def xcf(hi):
            return xchf_v[:, hi * 256:(hi + 1) * 256]

        def xcb(hi):
            return xchb_v[:, hi * 256:(hi + 1) * 256]

        for n in range(NTC):
            for hg in range(2):
                heads = (2 * hg, 2 * hg + 1)
                ph1 = {}
                for hi, h in enumerate(heads):
                    f0, f1 = 2 * h, 2 * h + 1

                    def fn_g(te, f0=f0, f1=f1, n=n):
                        te.matmul(ps_ms[:, 0:128], qslq(ak, f0, n), qslq(ak, f0, n),
                                  start=True, stop=False)
                        return te.matmul(ps_ms[:, 0:128], qslq(ak, f1, n), qslq(ak, f1, n),
                                         start=False, stop=True)
                    t_g = mk("tensor", fn_g, reads=[bf("ak", f0), bf("ak", f1)],
                             writes=[bf("ps_ms")], label=f"G{p_i}_{n}_{h}")

                    def fn_a(v, hi=hi, h=h, n=n):
                        return v.scalar_tensor_tensor(abuf_v[:, hi * 128:(hi + 1) * 128],
                                                      ps_ms[:, 0:128],
                                                      beta_tm[:, n * H + h:n * H + h + 1],
                                                      lmask[:], op0=ALU.mult, op1=ALU.mult)
                    t_a = mk("vector", fn_a, reads=[bf("ps_ms"), bf("beta_tm"), bf("lmask")],
                             writes=[bf("abuf", hi)],
                             fresh=("A6",) if (n == 0 and hg == 0 and hi == 0) else (),
                             extra=[t_g, t_beta_fin], label=f"A{p_i}_{n}_{h}")

                    t_att = mk("tensor", lambda te, hi=hi: te.transpose(
                        msb[:, 512:640], abuf_v[:, hi * 128:(hi + 1) * 128], eye[:]),
                        reads=[bf("abuf", hi), bf("eye")], writes=[bf("ps_ms")],
                        label=f"At{p_i}_{n}_{h}")

                    def fn_p0(v, hi=hi):
                        v.tensor_copy(pwt(hi, 0), msb[:, 512:640])
                        return v.tensor_copy(pwu(hi), abuf_v[:, hi * 128:(hi + 1) * 128])
                    t_p0 = mk("vector", fn_p0, reads=[bf("ps_ms"), bf("abuf", hi)],
                              writes=[bf("pow_t", hi), bf("pow_u", hi)],
                              extra=[t_att], label=f"p0{p_i}_{n}_{h}")

                    prev_sq = t_p0
                    for j in range(6):
                        off = 2 ** (j + 1)

                        def fn_sq2(te, hi=hi, j=j, off=off):
                            te.matmul(ps_pw[:, 0:128 - off], pwt(hi, j),
                                      pwu(hi)[:, 0:128 - off], start=True, stop=True)
                            return te.matmul(ps_pw[:, 128 + off:256], pwu(hi),
                                             pwt(hi, j)[:, off:128], start=True, stop=True)
                        t_s2 = mk("tensor", fn_sq2, reads=[bf("pow_u", hi), bf("pow_t", hi)],
                                  writes=[bf("ps_pw")], extra=[prev_sq],
                                  label=f"sq{p_i}_{n}_{h}_{j}")

                        def fn_se(v, hi=hi, j=j, off=off):
                            v.tensor_copy(pwu(hi)[:, 0:128 - off], ps_pw[:, 0:128 - off])
                            v.memset(pwu(hi)[:, 128 - off:128], 0.0)
                            v.tensor_copy(pwt(hi, j + 1)[:, off:128], ps_pw[:, 128 + off:256])
                            return v.memset(pwt(hi, j + 1)[:, 0:off], 0.0)
                        prev_sq = mk("vector", fn_se, reads=[bf("ps_pw")],
                                     writes=[bf("pow_u", hi), bf("pow_t", hi)],
                                     extra=[t_s2], label=f"se{p_i}_{n}_{h}_{j}")

                    def fn_kv(te, f0=f0, f1=f1, n=n):
                        te.transpose(sub[:, 0:128], qslq(ak, f0, n), eye[:])
                        te.transpose(sub[:, 128:256], qslq(ak, f1, n), eye[:])
                        te.transpose(sub[:, 256:384], qslv(f0, n), eye[:])
                        return te.transpose(sub[:, 384:512], qslv(f1, n), eye[:])
                    t_kv = mk("tensor", fn_kv, reads=[bf("ak", f0), bf("ak", f1),
                                                      bf("av", f0), bf("av", f1), bf("eye")],
                              writes=[bf("ps_su")], label=f"kv{p_i}_{n}_{h}")

                    def fn_kve(v, hi=hi):
                        v.tensor_copy(ktm_v[:, hi * 256:(hi + 1) * 256], sub[:, 0:256])
                        return v.tensor_copy(vtm_v[:, hi * 256:(hi + 1) * 256], sub[:, 256:512])
                    t_kve = mk("vector", fn_kve, reads=[bf("ps_su")],
                               writes=[bf("ktm", hi), bf("vtm", hi)], extra=[t_kv],
                               label=f"kve{p_i}_{n}_{h}")

                    def fn_atn(te, f0=f0, f1=f1, n=n):
                        te.matmul(ps_ms[:, 256:384], qslq(ak, f0, n), qslq(aq_v, f0, n),
                                  start=True, stop=False)
                        return te.matmul(ps_ms[:, 256:384], qslq(ak, f1, n), qslq(aq_v, f1, n),
                                         start=False, stop=True)
                    t_atn = mk("tensor", fn_atn, reads=[bf("ak", f0), bf("ak", f1),
                                                        bf("aq", f0), bf("aq", f1)],
                               writes=[bf("ps_ms")], extra=[t_p0], label=f"atn{p_i}_{n}_{h}")

                    t_atne = mk("vector", lambda v, hi=hi: v.tensor_tensor(
                        attnb_v[:, hi * 128:(hi + 1) * 128], ps_ms[:, 256:384], umaski[:],
                        op=ALU.mult),
                        reads=[bf("ps_ms"), bf("umaski")], writes=[bf("attnb", hi)],
                        extra=[t_atn], label=f"atne{p_i}_{n}_{h}")
                    ph1[hi] = (prev_sq, t_kve, t_atne)

                ydone = {}
                for hi, h in enumerate(heads):
                    f0, f1 = 2 * h, 2 * h + 1

                    def fn_ks(te, h=h, f0=f0, f1=f1, n=n):
                        te.matmul(ps_ch[:, 0:256], qslq(ak, f0, n),
                                  s_bf[:, (2 * h) * 256:(2 * h + 1) * 256],
                                  start=True, stop=False)
                        return te.matmul(ps_ch[:, 0:256], qslq(ak, f1, n),
                                         s_bf[:, (2 * h + 1) * 256:(2 * h + 2) * 256],
                                         start=False, stop=True)
                    t_ks = mk("tensor", fn_ks, reads=[bf("ak", f0), bf("ak", f1), bf("s_bf")],
                              writes=[bf("ps_ch")], label=f"ks{p_i}_{n}_{h}")

                    def fn_y(v, hi=hi, h=h, n=n):
                        v.scalar_tensor_tensor(xcf(hi), ps_ch[:, 0:256],
                                               -1.0, vtm_v[:, hi * 256:(hi + 1) * 256],
                                               op0=ALU.mult, op1=ALU.add)
                        return v.tensor_scalar(xcf(hi), xcf(hi),
                                               beta_tm[:, n * H + h:n * H + h + 1], None,
                                               op0=ALU.mult)
                    ydone[hi] = mk("vector", fn_y, reads=[bf("ps_ch"), bf("vtm", hi),
                                                          bf("beta_tm")],
                                   writes=[bf("xch", hi)], extra=[t_ks, ph1[hi][1]],
                                   label=f"Y{p_i}_{n}_{h}")

                chain_prev = dict(ydone)
                for j in [6, 5, 4, 3, 2, 1, 0]:
                    for hi, h in enumerate(heads):
                        def fn_cm2(te, hi=hi, j=j):
                            return te.matmul(ps_ch[:, 256:512], pwt(hi, j), xcf(hi),
                                             start=True, stop=True)
                        t_cm2 = mk("tensor", fn_cm2, reads=[bf("pow_t", hi), bf("xch", hi)],
                                   writes=[bf("ps_ch")], extra=[chain_prev[hi], ph1[hi][0]],
                                   label=f"cm{p_i}_{n}_{h}_{j}")

                        def fn_cx(v, hi=hi, j=j):
                            r = v.tensor_tensor(xcf(hi), xcf(hi), ps_ch[:, 256:512],
                                                op=(ALU.add if j > 0 else ALU.subtract))
                            if j == 0:
                                r = v.tensor_copy(xcb(hi), xcf(hi))
                            return r
                        chain_prev[hi] = mk("vector", fn_cx,
                                            reads=[bf("ps_ch"), bf("xch", hi)],
                                            writes=[bf("xch", hi)]
                                            + ([bf("xchb", hi)] if j == 0 else []),
                                            extra=[t_cm2], label=f"cx{p_i}_{n}_{h}_{j}")

                for hi, h in enumerate(heads):
                    f0, f1 = 2 * h, 2 * h + 1

                    def fn_o(te, hi=hi, h=h, f0=f0, f1=f1, n=n):
                        te.matmul(ps_ch[:, 0:256], qslq(aq_v, f0, n),
                                  s_bf[:, (2 * h) * 256:(2 * h + 1) * 256],
                                  start=True, stop=False)
                        te.matmul(ps_ch[:, 0:256], qslq(aq_v, f1, n),
                                  s_bf[:, (2 * h + 1) * 256:(2 * h + 2) * 256],
                                  start=False, stop=False)
                        return te.matmul(ps_ch[:, 0:256], attnb_v[:, hi * 128:(hi + 1) * 128],
                                         xcb(hi), start=False, stop=True)
                    t_o = mk("tensor", fn_o, reads=[bf("aq", f0), bf("aq", f1), bf("s_bf"),
                                                    bf("attnb", hi), bf("xchb", hi)],
                             writes=[bf("ps_ch")], extra=[chain_prev[hi], ph1[hi][2]],
                             label=f"o{p_i}_{n}_{h}")

                    t_oe = mk("vector", lambda v, hi=hi: v.tensor_copy(
                        otm_v[:, hi * 256:(hi + 1) * 256], ps_ch[:, 0:256]),
                        reads=[bf("ps_ch")], writes=[bf("otm", hi)], extra=[t_o],
                        label=f"oe{p_i}_{n}_{h}")

                    def fn_ot(te, hi=hi):
                        te.transpose(bcb[:, 0:128], otm_v[:, hi * 256:hi * 256 + 128], eye[:])
                        return te.transpose(bcb[:, 128:256],
                                            otm_v[:, hi * 256 + 128:hi * 256 + 256], eye[:])
                    t_ot = mk("tensor", fn_ot, reads=[bf("otm", hi), bf("eye")],
                              writes=[bf("ps_bc")], extra=[t_oe],
                              label=f"ot{p_i}_{n}_{h}")

                    def fn_ote(v, n=n, f0=f0, f1=f1):
                        v.tensor_copy(del_fm[:, f0 * 1024 + n * 128:f0 * 1024 + (n + 1) * 128],
                                      bcb[:, 0:128])
                        return v.tensor_copy(
                            del_fm[:, f1 * 1024 + n * 128:f1 * 1024 + (n + 1) * 128],
                            bcb[:, 128:256])
                    mk("vector", fn_ote, reads=[bf("ps_bc")], writes=[bf("del_fm")],
                       extra=[t_ot], label=f"ote{p_i}_{n}_{h}")

                    def fn_su2(te, hi=hi):
                        te.matmul(ps_su[:, 0:256], ktm_v[:, hi * 256:hi * 256 + 128],
                                  xcb(hi), start=True, stop=True)
                        return te.matmul(ps_su[:, 256:512],
                                         ktm_v[:, hi * 256 + 128:hi * 256 + 256],
                                         xcb(hi), start=True, stop=True)
                    t_su2 = mk("tensor", fn_su2, reads=[bf("ktm", hi), bf("xchb", hi)],
                               writes=[bf("ps_su")], extra=[t_oe], label=f"su{p_i}_{n}_{h}")

                    def fn_sue(v, h=h):
                        s0 = s_state[:, (2 * h) * 256:(2 * h + 1) * 256]
                        s1 = s_state[:, (2 * h + 1) * 256:(2 * h + 2) * 256]
                        v.tensor_tensor(s0, s0, ps_su[:, 0:256], op=ALU.add)
                        v.tensor_tensor(s1, s1, ps_su[:, 256:512], op=ALU.add)
                        v.tensor_copy(s_bf[:, (2 * h) * 256:(2 * h + 1) * 256], s0)
                        return v.tensor_copy(s_bf[:, (2 * h + 1) * 256:(2 * h + 2) * 256], s1)
                    mk("vector", fn_sue, reads=[bf("ps_su"), bf("s_state")],
                       writes=[bf("s_state"), bf("s_bf")], extra=[t_su2],
                       label=f"sue{p_i}_{n}_{h}")

        if p_i == 0:
            tap("t_del", del_fm[:], reads=[bf("del_fm")])
            tap("t_sst", s_state[:], reads=[bf("s_state")])
        if p_i == 1:
            tap("t_del1", del_fm[:], reads=[bf("del_fm")])

        shuf_mask = [p - 4 if (p % 8) >= 4 else p for p in range(32)]
        outs4 = [("loc_h", loc_h, None), ("mid_h", mid_h, None), ("del_fm", del_fm, None),
                 ("av", av, None)]

        wo_deps = []
        for h2 in range(2):
            conv_block("lo", av, "av",
                       lambda fc, s_: auxb[:, fc * CWW + LOCO + s_:fc * CWW + LOCO + s_ + 1], 7,
                       loc_h, lambda fc: fc * 512, "loc_h", None, WPV, PADV,
                       h2span=(h2, h2), extra_dep=t_vpad if h2 == 0 else t_vtail)
            conv_block("mi", av, "av", cw_ap(CWO["mid"]), 25,
                       mid_h, lambda fc: fc * 512, "mid_h", None, WPV, PADV,
                       h2span=(h2, h2), extra_dep=t_vpad if h2 == 0 else t_vtail)

            if p_i == 0 and h2 == 0:
                tap("t_loc", loc_h[:], reads=[bf("loc_h")])
                tap("t_mid", mid_h[:], reads=[bf("mid_h")])
            if p_i == 1 and h2 == 0:
                tap("t_loc1", loc_h[:], reads=[bf("loc_h")])
                tap("t_mid1", mid_h[:], reads=[bf("mid_h")])

            def osl(oi, fc, h2=h2):
                if oi == 0:
                    return loc_h[:, fc * 512:(fc + 1) * 512]
                if oi == 1:
                    return mid_h[:, fc * 512:(fc + 1) * 512]
                if oi == 2:
                    return del_fm[:, fc * 1024 + h2 * 512:fc * 1024 + (h2 + 1) * 512]
                return av[:, fc * WPV + PADV + h2 * 512:fc * WPV + PADV + (h2 + 1) * 512]

            def obuf(oi, fc):
                nm = outs4[oi][0]
                return bf(nm, fc) if isinstance(BUFS[nm], list) else bf(nm)

            if p_i == 0 and h2 == 0 and "t_dcp" in d_dbg:
                t_cpd = mk("vector", lambda v, osl=osl: (v.tensor_copy(prod_v[0][:], osl(2, 0)),
                                                         v.tensor_copy(prod_v[1][:], osl(3, 0)))[-1],
                           reads=[obuf(2, 0), obuf(3, 0)],
                           writes=[bf("prod", 0), bf("prod", 1)],
                           fresh=("A8",), label="dbgcp")
                tap("t_dcp", prod_v[0][:], reads=[bf("prod", 0)], extra=[t_cpd])
                tap("t_vcp", prod_v[1][:], reads=[bf("prod", 1)], extra=[t_cpd])

            ft_last = None
            first_ft = True
            for fc in range(NFC):
                hh = fc // 2
                for oi in (2, 3, 0, 1):
                    m = (2 * oi) * 4 + hh

                    def fn_fm(te, oi=oi, fc=fc, m=m, first=first_ft, osl=osl):
                        return te.matmul(ps_ft[0:56, 0:512], bigcol[:, 112 - m:168 - m][:, 0:56],
                                         osl(oi, fc), start=first, stop=False)
                    t_f = mk("tensor", fn_fm, reads=[obuf(oi, fc), bf("bigcol")],
                             writes=[bf("ps_ft")] if first_ft else [],
                             extra=[ft_last], label=f"ftm{p_i}_{h2}_{fc}_{oi}")
                    if not first_ft:
                        BUFS["ps_ft"].readers.append(t_f)
                    ft_last = t_f
                    first_ft = False
                for oi in range(4):
                    pi = oi % 2

                    def fn_sq3(sc, oi=oi, fc=fc, pi=pi, osl=osl):
                        return sc.activation(sq_v[pi][:], osl(oi, fc), AF.Square)
                    t_sq = mk("scalar", fn_sq3, reads=[obuf(oi, fc)], writes=[bf("sq", pi)],
                              fresh=("A8",) if (fc == 0 and oi == 0) else (),
                              label=f"fts{p_i}_{h2}_{fc}_{oi}")
                    m = (2 * oi + 1) * 4 + hh

                    def fn_fv(te, m=m, pi=pi):
                        return te.matmul(ps_ft[0:56, 0:512], bigcol[:, 112 - m:168 - m][:, 0:56],
                                         sq_v[pi][:], start=False, stop=False)
                    t_f = mk("tensor", fn_fv, reads=[bf("sq", pi), bf("bigcol")],
                             extra=[t_sq, ft_last], label=f"ftv{p_i}_{h2}_{fc}_{oi}")
                    BUFS["ps_ft"].readers.append(t_f)
                    ft_last = t_f
                cidx = 0
                for oi in range(4):
                    for oj in range(oi + 1, 4):
                        pi = cidx % 2

                        def fn_pr(v, oi=oi, oj=oj, fc=fc, pi=pi, osl=osl):
                            return v.tensor_tensor(prod_v[pi][:], osl(oi, fc), osl(oj, fc),
                                                   op=ALU.mult)
                        t_pr = mk("vector", fn_pr, reads=[obuf(oi, fc), obuf(oj, fc)],
                                  writes=[bf("prod", pi)],
                                  fresh=("A8",) if (fc == 0 and cidx == 0) else (),
                                  label=f"ftp{p_i}_{h2}_{fc}_{cidx}")
                        m = (8 + cidx) * 4 + hh
                        lastmm = (fc == NFC - 1 and cidx == 5)

                        def fn_fc2(te, m=m, pi=pi, lastmm=lastmm):
                            return te.matmul(ps_ft[0:56, 0:512],
                                             bigcol[:, 112 - m:168 - m][:, 0:56],
                                             prod_v[pi][:], start=False, stop=lastmm)
                        t_f = mk("tensor", fn_fc2, reads=[bf("prod", pi), bf("bigcol")],
                                 extra=[t_pr, ft_last], label=f"ftc{p_i}_{h2}_{fc}_{cidx}")
                        BUFS["ps_ft"].readers.append(t_f)
                        ft_last = t_f
                        cidx += 1

            def fn_ftp1(v):
                v.tensor_scalar(rf_tail[0:56, :], ps_ft[0:56, 0:512], 1.0 / DH, None,
                                op0=ALU.mult)
                return v.stream_shuffle(sq_v[0][:], rf_tail[:, 0:512], shuf_mask)
            t_ftp1 = mk("vector", fn_ftp1, reads=[bf("ps_ft")],
                        writes=[bf("rf_tail"), bf("sq", 0)],
                        extra=[ft_last], label=f"ftp1_{p_i}_{h2}")

            def fn_ftp2(v):
                v.tensor_tensor(prod_v[0][0:56, :], sq_v[0][0:56, :], sq_v[0][0:56, :],
                                op=ALU.mult)
                v.tensor_scalar(prod_v[1][0:56, :], ps_ft[0:56, 0:512],
                                1.0 / (DH - 1), None, op0=ALU.mult)
                v.scalar_tensor_tensor(prod_v[0][0:56, :], prod_v[0][0:56, :],
                                       -float(DH) / (DH - 1), prod_v[1][0:56, :],
                                       op0=ALU.mult, op1=ALU.add)
                v.tensor_scalar(rf_tail[0:56, :], rf_tail[0:56, :],
                                rowmask[0:56, 0:1], None, op0=ALU.mult)
                return v.scalar_tensor_tensor(rf_tail[0:56, :], prod_v[0][0:56, :],
                                              rowmask[0:56, 1:2], rf_tail[0:56, :],
                                              op0=ALU.mult, op1=ALU.add)
            t_ftp = mk("vector", fn_ftp2, reads=[bf("ps_ft"), bf("sq", 0), bf("rowmask")],
                       writes=[bf("rf_tail"), bf("prod", 0), bf("prod", 1)],
                       extra=[t_ftp1], label=f"ftp2_{p_i}_{h2}")

            if p_i == 0 and h2 == 0:
                tap("t_rf", rf_tail[:], reads=[bf("rf_tail")], extra=[t_ftp])

            lg_tasks = [None] * 4
            for mcg in range(9):
                mcs = [2 * mcg] + ([2 * mcg + 1] if 2 * mcg + 1 < 17 else [])
                dbi = mcg % 2
                mw = min(256, 2160 - mcg * 256)
                t_lr = mk("sync", lambda q, mcg=mcg, dbi=dbi: [
                    q.dma_start(out=rw1_v[dbi], in_=s_w["rw1"][mcg])],
                    writes=[bf("rw1", dbi)],
                    fresh=("A5",) if mcg == 0 else (),
                    extra=[pc["rw1"]], n_dma=1, label=f"lr{p_i}_{h2}_{mcg}")
                for mi, mc in enumerate(mcs):
                    rows = 128 if mc < 16 else 112
                    hbi = mc % 2

                    def fn_rmm(te, mc=mc, mi=mi, rows=rows, dbi=dbi, hbi=hbi, h2=h2):
                        o = []
                        for kc in range(8):
                            o.append(te.matmul(
                                ps_mm[hbi][0:rows, :],
                                rw1_v[dbi][:, kc * 256 + mi * 128:kc * 256 + mi * 128 + rows],
                                xn_fm[:, kc * 1024 + h2 * 512:kc * 1024 + (h2 + 1) * 512],
                                start=(kc == 0), stop=False))
                        o.append(te.matmul(
                            ps_mm[hbi][0:rows, :],
                            rw1_v[dbi][0:56, 8 * 256 + mi * 128:8 * 256 + mi * 128 + rows],
                            rf_tail[0:56, :], start=False, stop=True))
                        return o[-1]
                    t_rm = mk("tensor", fn_rmm, reads=[bf("rw1", dbi), bf("xn_fm"),
                                                       bf("rf_tail")],
                              writes=[bf(f"ps_mm{hbi}")], extra=[t_lr, t_ftp],
                              label=f"rt{p_i}_{h2}_{mc}")

                    def fn_rev(sc, mc=mc, rows=rows, hbi=hbi):
                        return sc.activation(h1t[0:rows, hbi * 512:(hbi + 1) * 512],
                                             ps_mm[hbi][0:rows, :], AF.Silu,
                                             bias=rb1_r[0:rows, mc:mc + 1])
                    t_re = mk("scalar", fn_rev, reads=[bf(f"ps_mm{hbi}"), bf("auxf")],
                              writes=[bf("h1t")], extra=[t_rm], label=f"rte{p_i}_{h2}_{mc}")

                    def fn_lg(te, mc=mc, hbi=hbi):
                        o = []
                        rows2 = 128 if mc < 16 else 112
                        for tcl in range(4):
                            o.append(te.matmul(
                                ps_ft[:, tcl * 128:tcl * 128 + 16],
                                h1t[0:rows2, hbi * 512 + tcl * 128:hbi * 512 + (tcl + 1) * 128],
                                rw2_r[0:rows2, mc * 16:(mc + 1) * 16],
                                start=(mc == 0 and tcl == 0), stop=(mc == 16)))
                        return o[-1]
                    t_lg = mk("tensor", fn_lg, reads=[bf("h1t"), bf("rw2_r")],
                              writes=[bf("ps_ft")] if mc == 0 else [],
                              extra=[t_re] + ([lg_tasks[0]] if mc > 0 else []),
                              label=f"lg{p_i}_{h2}_{mc}")
                    if mc != 0:
                        BUFS["ps_ft"].readers.append(t_lg)
                    lg_tasks[0] = t_lg
            t_lg_last = lg_tasks[0]

            pfm_done = []
            for tcl in range(4):
                def fn_sm1(v, tcl=tcl):
                    v.tensor_tensor(p_tm[:], ps_ft[:, tcl * 128:tcl * 128 + 16],
                                    rb2_bc[:], op=ALU.add)
                    v.tensor_copy(smallb[:, 0:2], smallb[:, 0:2])
                    v.reduce_sum(smallb[:, 12:16].unsqueeze(2),
                                 p_tm[:].rearrange("p (g i) -> p g i", i=4), axis=AX)
                    v.tensor_copy(smallb[:, 0:2], smallb[:, 0:2])
                    v.tensor_scalar(smallb[:, 12:16], smallb[:, 12:16], 0.25, None,
                                    op0=ALU.mult)
                    v.tensor_copy(smallb[:, 0:2], smallb[:, 0:2])
                    mb = smallb[:, 12:16].unsqueeze(2).broadcast_to([128, 4, 4])
                    return v.tensor_tensor(p_tm[:].rearrange("p (g i) -> p g i", i=4),
                                           p_tm[:].rearrange("p (g i) -> p g i", i=4), mb,
                                           op=ALU.subtract)
                t_s1 = mk("vector", fn_sm1, reads=[bf("ps_ft"), bf("auxf")],
                          writes=[bf("p_tm"), bf("smallb")],
                          extra=[t_lg_last], label=f"sm1{p_i}_{h2}_{tcl}")
                t_s2 = mk("scalar", lambda sc: sc.activation(p_tm[:], p_tm[:], AF.Exp),
                          writes=[bf("p_tm")], extra=[t_s1], label=f"sm2{p_i}_{h2}_{tcl}")

                def fn_sm3(v):
                    v.reduce_sum(smallb[:, 12:16].unsqueeze(2),
                                 p_tm[:].rearrange("p (g i) -> p g i", i=4), axis=AX)
                    v.tensor_copy(smallb[:, 0:2], smallb[:, 0:2])
                    v.reciprocal(smallb[:, 12:16], smallb[:, 12:16])
                    v.tensor_copy(smallb[:, 0:2], smallb[:, 0:2])
                    rb = smallb[:, 12:16].unsqueeze(2).broadcast_to([128, 4, 4])
                    v.tensor_tensor(p_tm[:].rearrange("p (g i) -> p g i", i=4),
                                    p_tm[:].rearrange("p (g i) -> p g i", i=4), rb,
                                    op=ALU.mult)
                    v.tensor_copy(smallb[:, 0:2], smallb[:, 0:2])
                    return v.tensor_scalar(p_tmb[:], p_tm[:], 0.96, 0.01,
                                           op0=ALU.mult, op1=ALU.add)
                t_s3 = mk("vector", fn_sm3, writes=[bf("p_tm"), bf("p_tmb"), bf("smallb")],
                          extra=[t_s2], label=f"sm3{p_i}_{h2}_{tcl}")
                t_pt = mk("tensor", lambda te: te.transpose(bcb[0:16, 256:384], p_tmb[:],
                                                            eye[:]),
                          reads=[bf("p_tmb"), bf("eye")], writes=[bf("ps_bc")],
                          extra=[t_s3], label=f"pt{p_i}_{h2}_{tcl}")
                pfm_done.append(mk("vector",
                                   lambda v, tcl=tcl: v.tensor_copy(
                                       p_fm[0:16, tcl * 128:(tcl + 1) * 128],
                                       bcb[0:16, 256:384]),
                                   reads=[bf("ps_bc")], writes=[bf("p_fm")],
                                   extra=[t_pt], label=f"pte{p_i}_{h2}_{tcl}"))

            if p_i == 0 and h2 == 0:
                tap("t_pfm", p_fm[:], reads=[bf("p_fm")], extra=pfm_done)

            rms_last = None
            for hh in range(H):
                pb_done = []
                for oi in range(4):
                    idx = hh * 4 + oi

                    def fn_pb(te, idx=idx):
                        return te.matmul(ps_bc[:], sel16[:, idx * 128:(idx + 1) * 128],
                                         p_fm[0:16, 0:512], start=True, stop=True)
                    t_pb = mk("tensor", fn_pb, reads=[bf("p_fm"), bf("sel16")],
                              writes=[bf("ps_bc")], extra=pfm_done, label=f"pb{p_i}_{h2}_{idx}")
                    pb_done.append(mk("vector",
                                      lambda v, oi=oi: v.tensor_copy(
                                          pb_v[:, oi * 512:(oi + 1) * 512], ps_bc[:]),
                                      reads=[bf("ps_bc")], writes=[bf("pb", oi)],
                                      extra=[t_pb], label=f"pbe{p_i}_{h2}_{idx}"))
                for fc in (2 * hh, 2 * hh + 1):
                    def fn_mix(v, fc=fc, osl=osl):
                        msl = mixed_v[:, fc * 512:(fc + 1) * 512]
                        v.tensor_tensor(msl, osl(0, fc), pb_v[:, 0:512], op=ALU.mult)
                        last = None
                        for oi in range(1, 4):
                            v.tensor_tensor(prod_v[0][:], osl(oi, fc),
                                            pb_v[:, oi * 512:(oi + 1) * 512], op=ALU.mult)
                            last = v.tensor_tensor(msl, msl, prod_v[0][:], op=ALU.add)
                        return last
                    t_mx = mk("vector", fn_mix,
                              reads=[obuf(oi, fc) for oi in range(4)]
                              + [bf("pb", oi) for oi in range(4)],
                              writes=[bf("mixed", fc), bf("prod", 0)],
                              extra=pb_done, label=f"mix{p_i}_{h2}_{fc}")

                    t_m2 = mk("scalar", lambda sc, fc=fc: sc.activation(
                        sq_v[0][:], mixed_v[:, fc * 512:(fc + 1) * 512], AF.Square),
                        reads=[bf("mixed", fc)], writes=[bf("sq", 0)], extra=[t_mx],
                        label=f"m2{p_i}_{h2}_{fc}")
                    t_m2w = mk("vector", lambda v, fc=fc: v.tensor_scalar(
                        sq_v[1][:], sq_v[0][:], w2col[:, fc:fc + 1], None, op0=ALU.mult),
                        reads=[bf("sq", 0), bf("w2col")], writes=[bf("sq", 1)], extra=[t_m2],
                        label=f"m2w{p_i}_{h2}_{fc}")

                    def fn_rmm(te, hh=hh, first=(fc == 0), last_=(fc == NFC - 1)):
                        te.matmul(ps_ms[0:4, 0:512], bigcol[:, 112 - hh:120 - hh][:, 0:4],
                                  sq_v[0][:], start=first, stop=last_)
                        return te.matmul(ps_su[0:4, 0:512], bigcol[:, 112 - hh:120 - hh][:, 0:4],
                                         sq_v[1][:], start=first, stop=last_)
                    t_rm = mk("tensor", fn_rmm, reads=[bf("sq", 0), bf("sq", 1), bf("bigcol")],
                              writes=[bf("ps_ms"), bf("ps_su")] if fc == 0 else [],
                              extra=[t_m2, t_m2w, rms_last], label=f"rmm{p_i}_{h2}_{fc}")
                    if fc != 0:
                        BUFS["ps_ms"].readers.append(t_rm)
                        BUFS["ps_su"].readers.append(t_rm)
                    rms_last = t_rm

            def fn_rsa(v):
                v.tensor_scalar(rsA[0:4, 0:512], ps_ms[0:4, 0:512], 1.0 / DH, 1e-5,
                                op0=ALU.mult, op1=ALU.add)
                v.reciprocal(rsB[0:4, 0:512], rsA[0:4, 0:512])
                v.tensor_scalar(rsB[0:4, 0:512], rsB[0:4, 0:512], 1.0 / DH, None, op0=ALU.mult)
                v.tensor_tensor(rsB[0:4, 0:512], rsB[0:4, 0:512], ps_su[0:4, 0:512],
                                op=ALU.mult)
                return v.tensor_scalar(rsB[0:4, 0:512], rsB[0:4, 0:512], 1.0, 1e-5,
                                       op0=ALU.mult, op1=ALU.add)
            t_rsa = mk("vector", fn_rsa, reads=[bf("ps_ms"), bf("ps_su")],
                       writes=[bf("rsA"), bf("rsB")], fresh=("A7",), extra=[rms_last],
                       label=f"rsa{p_i}_{h2}")

            def fn_rsb2(sc):
                sc.activation(rsA[0:4, 0:512], rsA[0:4, 0:512], AF.Ln)
                return sc.activation(rsB[0:4, 0:512], rsB[0:4, 0:512], AF.Ln)
            t_rsb = mk("scalar", fn_rsb2, writes=[bf("rsA"), bf("rsB")], extra=[t_rsa],
                       label=f"rsb{p_i}_{h2}")
            t_rsc = mk("vector", lambda v: v.tensor_tensor(rsA[0:4, 0:512], rsA[0:4, 0:512],
                                                           rsB[0:4, 0:512], op=ALU.add),
                       writes=[bf("rsA")], extra=[t_rsb], label=f"rsc{p_i}_{h2}")
            t_rsd = mk("scalar", lambda sc: sc.activation(rsA[0:4, 0:512], rsA[0:4, 0:512],
                                                          AF.Exp, scale=smallb[0:4, 3:4]),
                       reads=[bf("smallb")], writes=[bf("rsA")], extra=[t_rsc],
                       label=f"rsd{p_i}_{h2}")
            t_rse = mk("vector", lambda v: v.tensor_copy(rs_bf2[0:4, :], rsA[0:4, 0:512]),
                       reads=[bf("rsA")], writes=[bf("rs_bf2")], extra=[t_rsd],
                       label=f"rse{p_i}_{h2}")

            for hh in range(H):
                bank, bank_n = (ps_bc, "ps_bc") if hh % 2 == 0 else (ps_pw, "ps_pw")

                def fn_rb(te, hh=hh, bank=bank):
                    return te.matmul(bank[:, 0:512], sel8[0:4, hh * 128:(hh + 1) * 128],
                                     rs_bf2[0:4, :], start=True, stop=True)
                t_rb = mk("tensor", fn_rb, reads=[bf("rs_bf2"), bf("sel8")],
                          writes=[bf(bank_n)], extra=[t_rse], label=f"rb{p_i}_{h2}_{hh}")
                for fc in (2 * hh, 2 * hh + 1):
                    def fn_on(v, fc=fc, bank=bank):
                        msl = mixed_v[:, fc * 512:(fc + 1) * 512]
                        return v.scalar_tensor_tensor(msl, msl, g_col[:, fc:fc + 1],
                                                      bank[:, 0:512],
                                                      op0=ALU.mult, op1=ALU.mult)
                    t_on = mk("vector", fn_on, reads=[bf(bank_n), bf("g_col"),
                                                      bf("mixed", fc)],
                              writes=[bf("mixed", fc)], extra=[t_rb],
                              label=f"on{p_i}_{h2}_{fc}")

            if p_i == 0 and h2 == 0:
                tap("t_rs", rs_bf2[:], reads=[bf("rs_bf2")], extra=[t_rse])
                tap("t_mix", mixed_v, reads=[bf("mixed", i) for i in range(NFC)])

            last_lwo = mk("sync", lambda q: [q.dma_start(out=wo_v, in_=s_w["wo"][:])],
                          writes=[bf("wo_s")], fresh=("A5",), extra=[pc["wo"]], n_dma=1,
                          label=f"lwo{p_i}_{h2}")
            for tcl in range(4):
                tc_g = h2 * 4 + tcl
                t_xl2 = mk("sync", lambda q, s=seq, t0=tok0, tc_g=tc_g: [
                    q.dma_start(out=x_t[:], in_=dx[s, t0 + tc_g * 128:t0 + (tc_g + 1) * 128, :])],
                    writes=[bf("x_t")], n_dma=1, label=f"xl2{p_i}_{tc_g}")
                acc_evs = []
                for n2 in range(2):
                    bk = n2

                    def fn_wo(te, tcl=tcl, n2=n2, bk=bk, h2=h2):
                        o = []
                        for kc in range(NFC):
                            o.append(te.matmul(
                                ps_mm[bk][:],
                                mixed_v[:, kc * 512 + tcl * 128:kc * 512 + (tcl + 1) * 128],
                                wo_v[:, kc * 1024 + n2 * 512:kc * 1024 + (n2 + 1) * 512],
                                start=(kc == 0), stop=(kc == NFC - 1)))
                        return o[-1]
                    t_wo2 = mk("tensor", fn_wo,
                               reads=[bf("mixed", kc) for kc in range(NFC)] + [bf("wo_s")],
                               writes=[bf(f"ps_mm{bk}")], extra=[last_lwo],
                               label=f"wo{p_i}_{h2}_{tcl}_{n2}")

                    def fn_acc(v, n2=n2, bk=bk):
                        return v.tensor_tensor(ln_scr[:, n2 * 512:(n2 + 1) * 512], x_t[:, n2 * 512:(n2 + 1) * 512],
                                               ps_mm[bk][:], op=ALU.add)
                    acc_evs.append(mk("vector", fn_acc, reads=[bf(f"ps_mm{bk}"), bf("x_t")],
                                      writes=[bf("ln_scr")],
                                      fresh=("A7",) if (tcl == 0 and n2 == 0) else (),
                                      extra=[t_wo2, t_xl2],
                                      label=f"acc{p_i}_{h2}_{tcl}_{n2}"))
                mk("gpsimd", lambda q, tc_g=tc_g: [
                    q.dma_start(out=dacc[tc_g * 128:(tc_g + 1) * 128, :], in_=ln_scr[:, 0:1024])],
                    reads=[bf("ln_scr")], writes=[bf("dacc", tc_g)], extra=acc_evs,
                    n_dma=1, label=f"accd{p_i}_{tc_g}")
                if p_i == 0 and h2 == 0 and tcl == 0:
                    tap("t_acc", ln_scr[:, 0:1024], reads=[bf("ln_scr")], extra=acc_evs)

        for h2 in range(2):
            ffn1_done = []
            t_lw1 = {}
            for blk in range(4):
                dbi = blk % 2
                t_lw1[blk] = mk("sync", lambda q, blk=blk, dbi=dbi: [
                    q.dma_start(out=w1t_v[dbi], in_=s_w["w1t"][blk])],
                    writes=[bf("w1t", dbi)],
                    fresh=("A4",) if (h2 == 0 and blk == 0) else (),
                    extra=[pc["w1t"]], n_dma=1, label=f"lw1{p_i}_{h2}_{blk}")
                for hi in range(8):
                    hc = blk * 8 + hi
                    bk = hc % 2

                    def fn_f1(te, hi=hi, dbi=dbi, bk=bk, h2=h2):
                        o = []
                        for kc in range(NFC):
                            o.append(te.matmul(
                                ps_mm[bk][:],
                                w1t_v[dbi][:, kc * 1024 + hi * 128:kc * 1024 + (hi + 1) * 128],
                                xn_fm[:, kc * 1024 + h2 * 512:kc * 1024 + (h2 + 1) * 512],
                                start=(kc == 0), stop=(kc == NFC - 1)))
                        return o[-1]
                    t_f1 = mk("tensor", fn_f1, reads=[bf("w1t", dbi), bf("xn_fm")],
                              writes=[bf(f"ps_mm{bk}")], extra=[t_lw1[blk]],
                              label=f"f1{p_i}_{h2}_{hc}")

                    def fn_f1e(sc, hc=hc, bk=bk):
                        return sc.activation(hq_v[:, hc * 512:(hc + 1) * 512], ps_mm[bk][:],
                                             AF.Relu, bias=b1_r[:, hc:hc + 1])
                    ffn1_done.append(mk("scalar", fn_f1e, reads=[bf(f"ps_mm{bk}"), bf("auxf")],
                                        writes=[bf("hq")],
                                        fresh=("A4",) if (h2 == 0 and hc == 0) else (),
                                        extra=[t_f1],
                                        label=f"f1e{p_i}_{h2}_{hc}"))

            if p_i == 0 and h2 == 0:
                tap("t_hq", hq_v, reads=[bf("hq")], extra=[ffn1_done[-1]])

            f2_mm = {}
            for c2 in range(2):
                t_lw2 = mk("sync", lambda q, c2=c2: [
                    q.dma_start(out=w2t_v, in_=s_w["w2t"][c2])],
                    writes=[bf("w2t")], fresh=("A5",) if (h2 == 0 and c2 == 0) else (),
                    extra=[pc["w2t"]], n_dma=1, label=f"lw2{p_i}_{h2}_{c2}")
                for tcl in range(4):
                    for n2 in range(2):
                        bidx = tcl * 2 + n2
                        bank = PSH[PSB[bidx]]

                        def fn_f2(te, c2=c2, tcl=tcl, n2=n2, bank=bank):
                            o = []
                            for hi in range(16):
                                hc = c2 * 16 + hi
                                o.append(te.matmul(
                                    bank[:],
                                    hq_v[:, hc * 512 + tcl * 128:hc * 512 + (tcl + 1) * 128],
                                    w2t_v[:, hi * 1024 + n2 * 512:hi * 1024 + (n2 + 1) * 512],
                                    start=(c2 == 0 and hi == 0), stop=False))
                            if c2 == 1:
                                o.append(te.matmul(bank[:], ones32[:],
                                                   bq_row[0:1, n2 * 512:(n2 + 1) * 512],
                                                   start=False, stop=True))
                            return o[-1]
                        t_f2 = mk("tensor", fn_f2,
                                  reads=[bf("hq"), bf("w2t"), bf("ones32"), bf("bq_row")],
                                  writes=[bf(PSB[bidx])] if c2 == 0 else [],
                                  extra=[t_lw2, ffn1_done[-1], f2_mm.get(bidx)],
                                  label=f"f2{p_i}_{h2}_{tcl}_{n2}_{c2}")
                        if c2 == 1:
                            BUFS[PSB[bidx]].readers.append(t_f2)
                        f2_mm[bidx] = t_f2

            for tcl in range(4):
                tc_g = h2 * 4 + tcl
                t_al = mk("sync", lambda q, tc_g=tc_g: [
                    q.dma_start(out=x_t[:], in_=dacc[tc_g * 128:(tc_g + 1) * 128, :])],
                    reads=[bf("dacc", tc_g)], writes=[bf("x_t")], n_dma=1,
                    label=f"al{p_i}_{tc_g}")
                f2evs = []
                for n2 in range(2):
                    bidx = tcl * 2 + n2
                    bank = PSH[PSB[bidx]]

                    def fn_q16(v, n2=n2, bank=bank):
                        return v.tensor_copy(c16_v[n2][:], bank[:])
                    t_q16 = mk("vector", fn_q16, reads=[bf(PSB[bidx])], writes=[bf("c16", n2)],
                               fresh=("A8",) if (h2 == 0 and tcl == 0 and n2 == 0) else (),
                               extra=[f2_mm[bidx]], label=f"q16{p_i}_{h2}_{tcl}_{n2}")

                    def fn_fin(v, n2=n2):
                        return v.tensor_tensor(x_t[:, n2 * 512:(n2 + 1) * 512],
                                               x_t[:, n2 * 512:(n2 + 1) * 512],
                                               c16_v[n2][:], op=ALU.add)
                    f2evs.append(mk("vector", fn_fin, reads=[bf("c16", n2), bf("x_t")],
                                    writes=[bf("x_t")], extra=[t_q16, t_al],
                                    label=f"fin{p_i}_{h2}_{tcl}_{n2}"))
                final_dma.append(mk("gpsimd", lambda q, s=seq, t0=tok0, tc_g=tc_g: [
                    q.dma_start(out=dout[s, t0 + tc_g * 128:t0 + (tc_g + 1) * 128, :],
                                in_=x_t[:, 0:1024])],
                    reads=[bf("x_t")], extra=f2evs, n_dma=1, label=f"od{p_i}_{tc_g}"))

    mk("sync", lambda q: q.nop(), extra=final_dma + tap_tasks, label="fin")

    BD.finalize()
    with nc.Block() as blk:
        blk.tensor(BD.emitter("tensor", sems))
        blk.vector(BD.emitter("vector", sems))
        blk.scalar(BD.emitter("scalar", sems))
        blk.gpsimd(BD.emitter("gpsimd", sems))
        blk.sync(BD.emitter("sync", sems))
    est.close()
    return nc


_NC_CACHE = {}


def _get_program():
    if "nc" not in _NC_CACHE:
        _NC_CACHE["nc"] = build_program()
    return _NC_CACHE["nc"]


def pack_aux(g):
    aux = np.zeros((128, AUXW), np.float32)
    for fc in range(8):
        b = fc * 44
        aux[:, b + 0:b + 4] = g["conv_q_w"][fc * 128:(fc + 1) * 128, :]
        aux[:, b + 4:b + 8] = g["conv_k_w"][fc * 128:(fc + 1) * 128, :]
        aux[:, b + 8:b + 12] = g["conv_v_w"][fc * 128:(fc + 1) * 128, :]
        aux[:, b + 12:b + 37] = g["mid_w"][fc * 128:(fc + 1) * 128, :]
        aux[:, b + 37:b + 44] = g["local_w"][fc * 128:(fc + 1) * 128, :]
    aux[:, 352:384] = g["b1"].reshape(32, 128).T
    aux[:, 384:400] = g["r_b1"][0:2048].reshape(16, 128).T
    aux[0:112, 400] = g["r_b1"][2048:2160]
    aux[:, 401:409] = g["mix_w"].reshape(-1).reshape(8, 128).T
    aux[:, 409:411] = g["onorm_w"].reshape(2, 128).T
    aux[:, 411:427] = np.broadcast_to(g["r_b2"], (128, 16))
    for fc in range(8):
        aux[:, 427 + fc * 4:427 + (fc + 1) * 4] = g["Wb"][fc * 128:(fc + 1) * 128, :]
    return aux


def kernel(**inputs):
    nc = _get_program()
    consts = host_consts()
    x = np.ascontiguousarray(np.asarray(inputs["L0_attn_hidden_states"], dtype=np.float32))
    wmap = {
        "Wq": "L0_attn_Wq", "Wk": "L0_attn_Wk", "Wv": "L0_attn_Wv", "Wb": "L0_attn_Wb",
        "conv_q_w": "L0_attn_conv_q_w", "conv_k_w": "L0_attn_conv_k_w",
        "conv_v_w": "L0_attn_conv_v_w", "local_w": "L0_attn_local_w",
        "mid_w": "L0_attn_mid_w", "r_w1": "L0_attn_r_w1", "r_b1": "L0_attn_r_b1",
        "r_w2": "L0_attn_r_w2", "r_b2": "L0_attn_r_b2", "mix_w": "L0_attn_mix_w",
        "onorm_w": "L0_attn_onorm_w", "Wo": "L0_attn_Wo",
        "w1": "L0_ffn_w1", "b1": "L0_ffn_b1", "w2": "L0_ffn_w2", "b2": "L0_ffn_b2",
    }
    g = {k: np.ascontiguousarray(np.asarray(inputs[v], dtype=np.float32))
         for k, v in wmap.items()}
    base = {k: g[k] for k in ("Wq", "Wk", "Wv", "r_w1", "r_w2", "Wo", "w1", "w2", "b2")}
    base["aux"] = pack_aux(g)
    base.update(consts)
    in_maps = []
    for c in range(NCORES):
        m = dict(base)
        m["x"] = np.ascontiguousarray(x[c * SPC:(c + 1) * SPC])
        in_maps.append(m)
    res = run_bass_kernel_spmd(nc, in_maps, core_ids=list(range(NCORES)))
    return np.concatenate([r["out"] for r in res.results], axis=0)


# version 06
# speedup vs baseline: 1.0754x; 1.0754x over previous
import numpy as np
import ml_dtypes
from contextlib import ExitStack

import concourse.bass as bass
import concourse.mybir as mybir
from concourse.bass_utils import run_bass_kernel_spmd

F32 = mybir.dt.float32
BF16 = mybir.dt.bfloat16
F16 = mybir.dt.float16
F8 = mybir.dt.float8e4
ALU = mybir.AluOpType
AF = mybir.ActivationFunctionType
AX = mybir.AxisListType.X

NCORES = 8
B, L, D = 32, 2048, 1024
SPC = B // NCORES
T = 1024
NPASS = SPC * (L // T)
NTC = T // 128
NFC = D // 128
H, DH = 4, 256
FFH = 4096
NHC = FFH // 128
PADV = 24
PADQ = 4
WPV = PADV + T
WPQ = PADQ + T

ENGINES = ("tensor", "vector", "scalar", "gpsimd", "sync")

AUXW = 464
AXF = 352


class Task:
    __slots__ = ("eng", "fn", "deps", "n_dma", "label", "milestone", "needed")

    def __init__(self, eng, fn, deps, n_dma, label):
        self.eng, self.fn, self.deps, self.n_dma, self.label = eng, fn, deps, n_dma, label
        self.milestone = None
        self.needed = False


class Bufr:
    __slots__ = ("w", "readers", "name")

    def __init__(self, name=""):
        self.w = None
        self.readers = []
        self.name = name


class Builder:
    def __init__(self, serialize=False):
        self.tasks = {e: [] for e in ENGINES}
        self.all = []
        self.arenas = {}
        self.serialize = serialize

    def mk(self, eng, fn, reads=(), writes=(), extra=(), fresh=(), n_dma=0, label=""):
        deps = [d for d in extra if d is not None]
        if self.serialize and self.all:
            deps.append(self.all[-1])
        for a in fresh:
            for b in self.arenas[a]:
                deps.extend(b.readers)
                if b.w is not None:
                    deps.append(b.w)
        for b in reads:
            if b.w is not None:
                deps.append(b.w)
        for b in writes:
            deps.extend(b.readers)
            if b.w is not None:
                deps.append(b.w)
        t = Task(eng, fn, deps, n_dma, label)
        self.tasks[eng].append(t)
        self.all.append(t)
        for b in reads:
            b.readers.append(t)
        for b in writes:
            b.w = t
            b.readers = []
        return t

    def finalize(self):
        for t in self.all:
            for d in t.deps:
                if d.eng != t.eng:
                    d.needed = True
        for e in ENGINES:
            cum = 0
            for t in self.tasks[e]:
                if t.needed:
                    cum += 16 * t.n_dma if t.n_dma else 1
                    t.milestone = cum

    def emitter(self, ename, sems):
        def run(engh):
            hw = dict.fromkeys(ENGINES, 0)
            for t in self.tasks[ename]:
                for d in t.deps:
                    if d.eng == ename:
                        continue
                    v = d.milestone
                    assert v is not None, f"{t.label}: dep {d.label} lacks milestone"
                    if v > hw[d.eng]:
                        engh.wait_ge(sems[d.eng], v)
                        hw[d.eng] = v
                insts = t.fn(engh)
                if t.needed:
                    if t.n_dma:
                        assert isinstance(insts, (list, tuple)) and len(insts) == t.n_dma, t.label
                        for i in insts:
                            i.then_inc(sems[ename], 16)
                    else:
                        last = insts[-1] if isinstance(insts, (list, tuple)) else insts
                        last.then_inc(sems[ename], 1)
        return run


def host_consts():
    b = ml_dtypes.bfloat16
    eye = np.eye(128, dtype=np.float32)
    lm = np.tril(np.ones((128, 128), np.float32), -1)
    um = np.triu(np.ones((128, 128), np.float32), 0)
    bc = np.zeros((128, 169), np.float32)
    bc[:, 112] = 1.0
    s8 = np.zeros((8, 8 * 128), np.float32)
    for h in range(8):
        s8[h, h * 128:(h + 1) * 128] = 1.0
    s16 = np.zeros((16, 16 * 128), np.float32)
    for h in range(16):
        s16[h, h * 128:(h + 1) * 128] = 1.0
    rm = np.zeros((128, 2), np.float32)
    for k in range(14):
        for h in range(4):
            rm[k * 4 + h, 1 if (k < 8 and k % 2 == 1) else 0] = 1.0
    return {"c_eye": eye.astype(b), "c_lmask": lm.astype(b),
            "c_umaski": um.astype(b), "c_bigcol": bc.astype(b),
            "c_sel8": s8.astype(b), "c_sel16": s16.astype(b),
            "c_rowmask": rm}


def build_program(debug_taps=(), serialize=False):
    nc = bass.Bass("TRN2", target_bir_lowering=False, detect_race_conditions=False)
    BD = Builder(serialize=serialize)
    mk = BD.mk

    dx = nc.dram_tensor("x", [SPC, L, D], F32, kind="ExternalInput").ap()
    dout = nc.dram_tensor("out", [SPC, L, D], F32, kind="ExternalOutput").ap()
    dacc = nc.dram_tensor("s_acc", [T, D], F32, kind="Internal").ap()
    dw = {}
    for nm, shp in [("Wq", [D, D]), ("Wk", [D, D]), ("Wv", [D, D]),
                    ("r_w1", [1080, 2160]), ("r_w2", [2160, 16]), ("Wo", [D, D]),
                    ("w1", [FFH, D]), ("w2", [D, FFH]), ("b2", [D]),
                    ("aux", [128, AUXW])]:
        dw[nm] = nc.dram_tensor(nm, shp, F32, kind="ExternalInput").ap()
    dcn = {nm: nc.dram_tensor(nm, shp, BF16, kind="ExternalInput").ap()
           for nm, shp in [("c_eye", [128, 128]), ("c_lmask", [128, 128]),
                           ("c_umaski", [128, 128]), ("c_bigcol", [128, 169]),
                           ("c_sel8", [8, 1024]), ("c_sel16", [16, 2048])]}
    dcn["c_rowmask"] = nc.dram_tensor("c_rowmask", [128, 2], F32, kind="ExternalInput").ap()
    s_w = {nm: nc.dram_tensor(f"s_{nm}", shp, dt_, kind="Internal").ap()
           for nm, shp, dt_ in [("wq", [128, 8192], BF16), ("wk", [128, 8192], BF16),
                                ("wv", [128, 8192], BF16), ("wo", [128, 8192], BF16),
                                ("rw1", [9, 128, 2304], BF16),
                                ("w1t", [4, 128, 8192], BF16),
                                ("w2t", [2, 128, 16384], F8)]}
    d_dbg = {nm: nc.dram_tensor(nm, list(shp), dt_, kind="ExternalOutput").ap()
             for nm, shp, dt_ in debug_taps}
    tap_tasks = []

    def tap(name, ap, reads=(), extra=()):
        if name not in d_dbg:
            return
        tap_tasks.append(mk("gpsimd",
                            lambda q, ap=ap, name=name: [q.dma_start(out=d_dbg[name][:], in_=ap)],
                            reads=list(reads), extra=[e for e in extra if e is not None],
                            n_dma=1, label=f"tap_{name}"))

    est = ExitStack()
    E = est.enter_context
    sems = {en: E(nc.semaphore(f"sm_{en}")) for en in ENGINES}

    BUFS = {}

    def sbuf(name, shape, dt_, nbuf=1):
        h = E(nc.sbuf_tensor(name, shape, dt_))
        BUFS[name] = Bufr(name) if nbuf == 1 else [Bufr(f"{name}{i}") for i in range(nbuf)]
        return h

    def vbuf(name, nbuf=1, arena=None):
        BUFS[name] = Bufr(name) if nbuf == 1 else [Bufr(f"{name}{i}") for i in range(nbuf)]
        if arena:
            bs = BUFS[name] if isinstance(BUFS[name], list) else [BUFS[name]]
            BD.arenas.setdefault(arena, []).extend(bs)

    def psum(name):
        h = E(nc.psum_tensor(name, [128, 512], F32))
        BUFS[name] = Bufr(name)
        return h

    def bf(name, i=None):
        b = BUFS[name]
        return b if i is None else b[i]

    x_t = sbuf("x_t", [128, 1024], F32)
    xn_fm = sbuf("xn_fm", [128, NFC * 1024], BF16)
    ak = sbuf("ak", [128, NFC * WPQ], BF16, nbuf=NFC)
    av = sbuf("av", [128, NFC * WPV], BF16, nbuf=NFC)
    tails = sbuf("tails", [128, 2 * NFC * PADQ + NFC * 3 + NFC * PADV], BF16)
    VTOF = 2 * NFC * PADQ + NFC * 3
    loc_h = sbuf("loc_h", [128, NFC * 512], BF16)
    mid_h = sbuf("mid_h", [128, NFC * 512], BF16)
    del_fm = sbuf("del_fm", [128, NFC * 1024], BF16)
    beta_tm = sbuf("beta_tm", [128, NTC * H], F32)
    stat_t = sbuf("stat_t", [128, 4 * NTC], F32)
    s_state = sbuf("s_state", [128, 2 * H * 256], F32)
    s_bf = sbuf("s_bf", [128, 2 * H * 256], BF16)
    rf_tail = sbuf("rf_tail", [128, 512], BF16)
    h1t = sbuf("h1t", [128, 2 * 512], BF16)
    p_tm = sbuf("p_tm", [128, 16], F32)
    p_tmb = sbuf("p_tmb", [128, 16], BF16)
    p_fm = sbuf("p_fm", [16, 512], BF16)
    rs_bf2 = sbuf("rs_bf2", [4, 512], BF16)
    eye = sbuf("eye", [128, 128], BF16)
    lmask = sbuf("lmask", [128, 128], BF16)
    umaski = sbuf("umaski", [128, 128], BF16)
    bigcol = sbuf("bigcol", [128, 169], BF16)
    sel8 = sbuf("sel8", [8, 1024], BF16)
    sel16 = sbuf("sel16", [16, 2048], BF16)
    rowmask = sbuf("rowmask", [128, 2], F32)
    wb_r = sbuf("wb_r", [128, NFC * H], BF16)
    auxb = sbuf("auxb", [128, AXF], F32)
    auxf = sbuf("auxf", [128, AUXW - AXF], F32)
    rw2_r = sbuf("rw2_r", [128, 17 * 16], BF16)
    g_col = sbuf("g_col", [128, NFC], F32)
    w2col = sbuf("w2col", [128, NFC], F32)
    b1_r = auxf[:, 0:32]
    rb1_r = auxf[:, 32:49]
    mix8 = auxf[:, 49:57]
    onorm2 = auxf[:, 57:59]
    rb2_bc = auxf[:, 59:75]
    wbf32 = auxf[:, 75:107]
    bq_row = sbuf("bq_row", [1, 1024], F32)
    ones32 = sbuf("ones32", [1, 128], F32)
    smallb = sbuf("smallb", [128, 16], F32)

    u4 = sbuf("u4_raw", [128, 24608], BF16)
    u5 = sbuf("u5_raw", [128, 8192], BF16)
    u6 = sbuf("u6_raw", [128, 9216], BF16)
    u7 = sbuf("u7_raw", [128, 2048], BF16)
    u8 = sbuf("u8_raw", [128, 2048], BF16)
    for un in ("u4_raw", "u5_raw", "u6_raw", "u7_raw", "u8_raw"):
        del BUFS[un]

    wq_v = u4[:, 0:8192]
    wk_v = u4[:, 8192:16384]
    wv_v = u4[:, 0:8192]
    aq_v = u4[:, 16384:16384 + NFC * WPQ]
    w1t_v = [u4[:, 0:8192], u4[:, 8192:16384]]
    hq_v = u4[:, 16384:24576].bitcast(F8)
    pc_in = u4[:, 0:8192].bitcast(F32)
    pc_bf = u4[:, 8192:12288]
    pc_f8 = u4[:, 12288:14336].bitcast(F8)
    pc_t = u4[:, 14336:18432]
    pc_t8 = u4[:, 18432:20480].bitcast(F8)
    vbuf("wq_s", arena="A4"); vbuf("wk_s", arena="A4"); vbuf("wv_s", arena="A4")
    vbuf("aq", nbuf=NFC, arena="A4")
    vbuf("w1t", nbuf=2, arena="A4"); vbuf("hq", arena="A4")
    vbuf("pc_in", arena="A4"); vbuf("pc_bf", arena="A4"); vbuf("pc_f8", arena="A4")
    vbuf("pc_t", arena="A4"); vbuf("pc_t8", arena="A4")

    rw1_v = [u5[:, 0:2304], u5[:, 2304:4608]]
    wo_v = u5[:, 0:8192]
    w2t_v = u5[:, 0:8192].bitcast(F8)
    vbuf("rw1", nbuf=2, arena="A5"); vbuf("wo_s", arena="A5"); vbuf("w2t", arena="A5")

    powtf_v = u6[:, 0:1024].bitcast(F32)
    powuf_v = u6[:, 1024:2048].bitcast(F32)
    xchf_v = u6[:, 2048:4096].bitcast(F32)
    xchb_v = u6[:, 4096:5120]
    abuf_v = u6[:, 5120:5632]
    attnb_v = u6[:, 5632:6144]
    ktm_v = u6[:, 6144:7168]
    vtm_v = u6[:, 7168:8192]
    otm_v = u6[:, 8192:9216]
    pb_v = u6[:, 0:2048]
    mixed_v = u6[:, 2048:6144]
    vbuf("pow_t", nbuf=4, arena="A6"); vbuf("pow_u", nbuf=4, arena="A6")
    vbuf("abuf", nbuf=4, arena="A6")
    vbuf("attnb", nbuf=4, arena="A6")
    vbuf("ktm", nbuf=4, arena="A6"); vbuf("vtm", nbuf=4, arena="A6")
    vbuf("xch", nbuf=4, arena="A6"); vbuf("xchb", nbuf=4, arena="A6")
    vbuf("otm", nbuf=4, arena="A6")
    vbuf("pb", nbuf=4, arena="A6"); vbuf("mixed", nbuf=NFC, arena="A6")

    ln_scr = u7[:, 0:2048].bitcast(F32)
    rsq_v = u7[:, 0:1024].bitcast(F32)
    rsb_v = u7[:, 1024:1536]
    rsA = u7[:, 0:1024].bitcast(F32)
    rsB = u7[:, 1024:2048].bitcast(F32)
    vbuf("ln_scr", arena="A7"); vbuf("rsq", arena="A7"); vbuf("rsb", arena="A7")
    vbuf("rsA", arena="A7"); vbuf("rsB", arena="A7")

    xn_t_v = u8[:, 0:2048]
    prod_v = [u8[:, 0:512], u8[:, 512:1024]]
    sq_v = [u8[:, 1024:1536], u8[:, 1536:2048]]
    c16_v = [u8[:, 0:512].bitcast(F16), u8[:, 512:1024].bitcast(F16)]
    bqf16_v = u8[:, 0:1024].bitcast(F16)
    cacc = [u8[:, 0:1024].bitcast(F32), u8[:, 1024:2048].bitcast(F32)]
    vbuf("xn_t", nbuf=2, arena="A8")
    vbuf("prod", nbuf=2, arena="A8"); vbuf("sq", nbuf=2, arena="A8")
    vbuf("c16", nbuf=2, arena="A8"); vbuf("bqf16", arena="A8")
    vbuf("cacc", nbuf=2, arena="A8")
    vbuf("dacc", nbuf=NTC)

    def xnt(i):
        return xn_t_v[:, i * 1024:(i + 1) * 1024]

    ps_mm = [psum("ps_mm0"), psum("ps_mm1")]
    ps_pw = psum("ps_pw")
    ps_ms = psum("ps_ms")
    ps_ch = psum("ps_ch")
    ps_su = psum("ps_su")
    ps_ft = psum("ps_ft")
    ps_bc = psum("ps_bc")
    PSB = ["ps_mm0", "ps_mm1", "ps_pw", "ps_ms", "ps_ch", "ps_su", "ps_ft", "ps_bc"]
    PSH = {"ps_mm0": ps_mm[0], "ps_mm1": ps_mm[1], "ps_pw": ps_pw, "ps_ms": ps_ms,
           "ps_ch": ps_ch, "ps_su": ps_su, "ps_ft": ps_ft, "ps_bc": ps_bc}
    mmb0 = ps_mm[0][:].bitcast(BF16)
    msb = ps_ms[:].bitcast(BF16)
    sub = ps_su[:].bitcast(BF16)
    bcb = ps_bc[:].bitcast(BF16)

    CWO = {"q": 0, "k": 4, "v": 8, "mid": 12}
    LOCO = 37
    CWW = 44

    mk("sync", lambda q: [
        q.dma_start(out=eye[:], in_=dcn["c_eye"][:]),
        q.dma_start(out=lmask[:], in_=dcn["c_lmask"][:]),
        q.dma_start(out=umaski[:], in_=dcn["c_umaski"][:]),
        q.dma_start(out=bigcol[:], in_=dcn["c_bigcol"][:]),
        q.dma_start(out=sel8[:], in_=dcn["c_sel8"][:]),
        q.dma_start(out=sel16[:], in_=dcn["c_sel16"][:]),
        q.dma_start(out=rowmask[:], in_=dcn["c_rowmask"][:]),
    ], writes=[bf("eye"), bf("lmask"), bf("umaski"), bf("bigcol"), bf("sel8"), bf("sel16"),
               bf("rowmask")], n_dma=7, label="c")

    t_auxc = mk("sync", lambda q: [
        q.dma_start(out=auxb[:], in_=dw["aux"][:, 0:AXF]),
        q.dma_start(out=auxf[:], in_=dw["aux"][:, AXF:AUXW]),
        q.dma_start(out=bq_row[:], in_=dw["b2"].unsqueeze(0)),
    ], writes=[bf("auxb"), bf("auxf"), bf("bq_row")], n_dma=3, label="auxc")

    def fn_prep(v):
        on8 = onorm2.unsqueeze(1).broadcast_to([128, 4, 2])
        v.tensor_tensor(g_col[:].rearrange("p (a c) -> p a c", c=2),
                        mix8.rearrange("p (a c) -> p a c", c=2),
                        on8, op=ALU.mult)
        v.tensor_tensor(w2col[:], mix8, mix8, op=ALU.mult)
        v.memset(smallb[:, 2:3], 1e-5)
        v.memset(smallb[:, 3:4], -0.5)
        v.memset(smallb[:, 4:5], 0.5)
        v.memset(smallb[:, 5:6], 1e-12)
        v.memset(ones32[:], 1.0)
        v.memset(rf_tail[:], 0.0)
        v.tensor_copy(wb_r[:], wbf32)
        v.tensor_copy(bqf16_v[0:1, :], bq_row[:])
        return v.tensor_copy(bq_row[:], bqf16_v[0:1, :])
    t_prep = mk("vector", fn_prep, reads=[bf("auxf"), bf("bq_row")],
                writes=[bf("g_col"), bf("w2col"), bf("bq_row"), bf("smallb"),
                        bf("ones32"), bf("wb_r"), bf("rf_tail"), bf("bqf16")],
                extra=[t_auxc], label="prep")

    t_rw2l = mk("sync", lambda q: [
        q.dma_start(out=ln_scr[0:(128 if k < 16 else 112), k * 16:(k + 1) * 16],
                    in_=dw["r_w2"][k * 128:k * 128 + (128 if k < 16 else 112), :])
        for k in range(17)],
        writes=[bf("ln_scr")], n_dma=17, label="rw2l")

    def fn_rw2(v):
        v.memset(rw2_r[:], 0.0)
        return v.tensor_copy(rw2_r[:, 0:17 * 16], ln_scr[:, 0:17 * 16])
    mk("vector", fn_rw2, reads=[bf("ln_scr")], writes=[bf("rw2_r")],
       extra=[t_rw2l, t_prep], label="rw2")

    pc = {}

    def pc_direct(name, dram, nk, ncols, store_fn, n_store):
        last = None
        for kc in range(nk):
            rows = min(128, dram.shape[0] - kc * 128)
            mk("sync", lambda q, r=rows, c=ncols, k=kc, d=dram: [
                q.dma_start(out=pc_in[0:r, 0:c], in_=d[k * 128:k * 128 + r, :])],
               writes=[bf("pc_in")], fresh=("A4",) if (name == "wq" and kc == 0) else (),
               n_dma=1, label=f"pcl_{name}{kc}")
            mk("vector", lambda v, r=rows, c=ncols: v.tensor_copy(pc_bf[0:r, 0:c], pc_in[0:r, 0:c]),
               reads=[bf("pc_in")], writes=[bf("pc_bf")], label=f"pcc_{name}{kc}")
            last = mk("gpsimd", lambda q, k=kc, r=rows: store_fn(q, k, r),
                      reads=[bf("pc_bf")], n_dma=n_store, label=f"pcs_{name}{kc}")
        return last

    def st_w(s):
        return lambda q, k, r: [q.dma_start(out=s[0:r, k * 1024:(k + 1) * 1024],
                                            in_=pc_bf[0:r, 0:1024])]

    def st_rw1(q, k, r):
        return [q.dma_start(
            out=s_w["rw1"][mcg, 0:r, k * 256:k * 256 + min(256, 2160 - mcg * 256)],
            in_=pc_bf[0:r, mcg * 256:mcg * 256 + min(256, 2160 - mcg * 256)])
            for mcg in range(9)]

    pc["wq"] = pc_direct("wq", dw["Wq"], 8, 1024, st_w(s_w["wq"]), 1)
    pc["wk"] = pc_direct("wk", dw["Wk"], 8, 1024, st_w(s_w["wk"]), 1)
    pc["wv"] = pc_direct("wv", dw["Wv"], 8, 1024, st_w(s_w["wv"]), 1)
    pc["wo"] = pc_direct("wo", dw["Wo"], 8, 1024, st_w(s_w["wo"]), 1)
    pc["rw1"] = pc_direct("rw1", dw["r_w1"], 9, 2160, st_rw1, 9)

    for hc in range(NHC):
        mk("sync", lambda q, k=hc: [
            q.dma_start(out=pc_in[:, 0:1024], in_=dw["w1"][k * 128:(k + 1) * 128, :])],
           writes=[bf("pc_in")], n_dma=1, label=f"pcl_w1{hc}")
        mk("vector", lambda v: v.tensor_copy(pc_bf[:, 0:1024], pc_in[:, 0:1024]),
           reads=[bf("pc_in")], writes=[bf("pc_bf")], label=f"pcc_w1{hc}")

        def fn_tr(te):
            o = []
            for fc in range(8):
                o.append(te.transpose(mmb0[:, fc * 128:(fc + 1) * 128],
                                      pc_bf[:, fc * 128:(fc + 1) * 128], eye[:]))
            return o[-1]
        mk("tensor", fn_tr, reads=[bf("pc_bf"), bf("eye")],
           writes=[bf("ps_mm0")], label=f"pct_w1{hc}")

        def fn_ev(v):
            return v.tensor_copy(pc_t[:, 0:1024], mmb0[:, 0:1024])
        mk("vector", fn_ev, reads=[bf("ps_mm0")], writes=[bf("pc_t")],
           label=f"pce_w1{hc}")
        pc["w1t"] = mk("gpsimd", lambda q, k=hc: [
            q.dma_start(out=s_w["w1t"][k // 8, :,
                                       kc * 1024 + (k % 8) * 128:kc * 1024 + (k % 8) * 128 + 128],
                        in_=pc_t[:, kc * 128:(kc + 1) * 128]) for kc in range(8)],
            reads=[bf("pc_t")], n_dma=8, label=f"pcs_w1{hc}")

    for kc in range(8):
        for hf in range(2):
            mk("sync", lambda q, k=kc, h=hf: [
                q.dma_start(out=pc_in[:, 0:2048],
                            in_=dw["w2"][k * 128:(k + 1) * 128, h * 2048:(h + 1) * 2048])],
               writes=[bf("pc_in")], n_dma=1, label=f"pcl_w2{kc}_{hf}")

            def fn_q(v):
                v.tensor_copy(pc_f8[:, 0:2048], pc_in[:, 0:2048])
                return v.tensor_copy(pc_bf[:, 0:2048], pc_f8[:, 0:2048])
            mk("vector", fn_q, reads=[bf("pc_in")], writes=[bf("pc_bf"), bf("pc_f8")],
               label=f"pcq_w2{kc}_{hf}")
            for g in range(2):
                def fn_tr(te, g=g):
                    o = []
                    for j in range(8):
                        o.append(te.transpose(mmb0[:, j * 128:(j + 1) * 128],
                                              pc_bf[:, (g * 8 + j) * 128:(g * 8 + j + 1) * 128],
                                              eye[:]))
                    return o[-1]
                mk("tensor", fn_tr, reads=[bf("pc_bf"), bf("eye")],
                   writes=[bf("ps_mm0")], label=f"pct_w2{kc}_{hf}_{g}")

                def fn_ev(v, g=g):
                    return v.tensor_copy(pc_t8[:, g * 1024:(g + 1) * 1024], mmb0[:, 0:1024])
                mk("vector", fn_ev, reads=[bf("ps_mm0")], writes=[bf("pc_t8")],
                   label=f"pce_w2{kc}_{hf}_{g}")
                pc["w2t"] = mk("gpsimd", lambda q, k=kc, h=hf, g=g: [
                    q.dma_start(out=s_w["w2t"][h, :,
                                               (g * 8 + j) * 1024 + k * 128:(g * 8 + j) * 1024 + (k + 1) * 128],
                                in_=pc_t8[:, (g * 8 + j) * 128:(g * 8 + j + 1) * 128])
                    for j in range(8)],
                    reads=[bf("pc_t8")], n_dma=8, label=f"pcs_w2{kc}_{hf}_{g}")

    final_dma = []
    for p_i in range(NPASS):
        seq, half = p_i // 2, p_i % 2
        tok0 = half * T

        t_lwq = mk("sync", lambda q: [q.dma_start(out=wq_v, in_=s_w["wq"][:])],
                   writes=[bf("wq_s")], fresh=("A4",), extra=[pc["wq"]], n_dma=1,
                   label=f"lwq{p_i}")
        t_lwk = mk("sync", lambda q: [q.dma_start(out=wk_v, in_=s_w["wk"][:])],
                   writes=[bf("wk_s")], extra=[pc["wk"]], n_dma=1, label=f"lwk{p_i}")

        t_xn_done = None
        prev_ln = None
        for tc in range(NTC):
            pi = tc % 2
            t_xl = mk("sync", lambda q, s=seq, t0=tok0, tc=tc: [
                q.dma_start(out=x_t[:], in_=dx[s, t0 + tc * 128:t0 + (tc + 1) * 128, :])],
                writes=[bf("x_t")], n_dma=1, label=f"xl{p_i}_{tc}")
            if p_i == 0 and tc == 0:
                tap("t_xt", x_t[:], reads=[bf("x_t")])

            def fn_st(v, tc=tc):
                c0 = stat_t[:, tc * 4:tc * 4 + 1]
                c1 = stat_t[:, tc * 4 + 1:tc * 4 + 2]
                c2 = stat_t[:, tc * 4 + 2:tc * 4 + 3]
                c3 = stat_t[:, tc * 4 + 3:tc * 4 + 4]
                v.reduce_sum(c0, x_t[:], axis=AX)
                v.tensor_tensor(ln_scr[:, 0:1024], x_t[:], x_t[:], op=ALU.mult)
                v.tensor_scalar(c0, c0, 1.0 / 1024, None, op0=ALU.mult)
                v.reduce_sum(c1, ln_scr[:, 0:1024], axis=AX)
                v.tensor_tensor(c2, c0, c0, op=ALU.mult)
                v.tensor_copy(c3, c0)
                return v.scalar_tensor_tensor(c1, c1, 1.0 / 1024, c2,
                                              op0=ALU.mult, op1=ALU.subtract)
            t_st = mk("vector", fn_st, reads=[bf("x_t")], writes=[bf("stat_t"), bf("ln_scr")],
                      fresh=("A7",) if tc == 0 else (), extra=[t_prep, prev_ln],
                      label=f"lnst{p_i}_{tc}")
            if p_i == 0 and tc == 0:
                tap("t_lnscr", ln_scr[:, 0:1024], reads=[bf("ln_scr")], extra=[t_st])

            def fn_rs(sc, tc=tc):
                sc.activation(stat_t[:, tc * 4 + 2:tc * 4 + 3],
                              stat_t[:, tc * 4 + 1:tc * 4 + 2], AF.Ln,
                              bias=smallb[:, 2:3])
                return sc.activation(stat_t[:, tc * 4 + 3:tc * 4 + 4],
                                     stat_t[:, tc * 4 + 2:tc * 4 + 3], AF.Exp,
                                     scale=smallb[:, 3:4])
            t_rs = mk("scalar", fn_rs, reads=[bf("smallb")], writes=[bf("stat_t")],
                      extra=[t_st], label=f"lnrs{p_i}_{tc}")

            def fn_xn(v, tc=tc, pi=pi):
                return v.tensor_scalar(xnt(pi), x_t[:],
                                       stat_t[:, tc * 4:tc * 4 + 1],
                                       stat_t[:, tc * 4 + 3:tc * 4 + 4],
                                       op0=ALU.subtract, op1=ALU.mult)
            t_xn = mk("vector", fn_xn, reads=[bf("x_t"), bf("stat_t")],
                      writes=[bf("xn_t", pi)],
                      fresh=("A8",) if (tc == 0) else (), extra=[t_rs],
                      label=f"xn{p_i}_{tc}")
            prev_ln = t_xn

            def fn_tr(te, pi=pi):
                o = []
                for fc in range(NFC):
                    o.append(te.transpose(mmb0[:, fc * 128:(fc + 1) * 128],
                                          xnt(pi)[:, fc * 128:(fc + 1) * 128], eye[:]))
                return o[-1]
            t_tr = mk("tensor", fn_tr, reads=[bf("xn_t", pi), bf("eye")],
                      writes=[bf("ps_mm0")], label=f"xnT{p_i}_{tc}")

            def fn_ev(v, tc=tc):
                return v.tensor_copy(
                    xn_fm[:].rearrange("p (f c) -> p f c", c=1024)[:, :, tc * 128:(tc + 1) * 128],
                    mmb0[:, 0:1024].rearrange("p (f c) -> p f c", c=128))
            t_xn_done = mk("vector", fn_ev, reads=[bf("ps_mm0")],
                           writes=[bf("xn_fm")], extra=[t_tr], label=f"xnE{p_i}_{tc}")

        if p_i == 0:
            tap("t_stat", stat_t[:], reads=[bf("stat_t")])
            tap("t_xnfm", xn_fm[:], reads=[bf("xn_fm")])

        def fn_pad(v, half=half):
            last = None
            for bi, (dst, wp, pd) in enumerate(((aq_v, WPQ, PADQ), (ak, WPQ, PADQ),
                                                (av, WPV, PADV))):
                for fc in range(NFC):
                    if half == 0:
                        last = v.memset(dst[:, fc * wp:fc * wp + pd], 0.0)
                    elif bi < 2:
                        tof = bi * NFC * PADQ
                        last = v.tensor_copy(dst[:, fc * wp:fc * wp + pd],
                                             tails[:, tof + fc * pd:tof + (fc + 1) * pd])
                    else:
                        tof = 2 * NFC * PADQ
                        last = v.tensor_copy(dst[:, fc * wp + pd - 3:fc * wp + pd],
                                             tails[:, tof + fc * 3:tof + (fc + 1) * 3])
            return last
        t_pad = mk("vector", fn_pad, reads=[bf("tails")],
                   writes=[bf("aq", i) for i in range(NFC)] + [bf("ak", i) for i in range(NFC)]
                   + [bf("av", i) for i in range(NFC)], label=f"pad{p_i}")

        def proj(pname, wview, wname, dst, dname, wp, pd, dep):
            for mc in range(NFC):
                for nt in range(2):
                    bk = (mc * 2 + nt) % 2

                    def fn_mm(te, w=wview, mc=mc, nt=nt, bk=bk):
                        o = []
                        for kc in range(NFC):
                            o.append(te.matmul(
                                ps_mm[bk][:],
                                w[:, kc * 1024 + mc * 128:kc * 1024 + (mc + 1) * 128],
                                xn_fm[:, kc * 1024 + nt * 512:kc * 1024 + (nt + 1) * 512],
                                start=(kc == 0), stop=(kc == NFC - 1)))
                        return o[-1]
                    t_mm = mk("tensor", fn_mm, reads=[bf(wname), bf("xn_fm")],
                              writes=[bf(f"ps_mm{bk}")], extra=[dep, t_xn_done],
                              label=f"{pname}mm{p_i}_{mc}_{nt}")

                    def fn_ev(v, d=dst, mc=mc, nt=nt, bk=bk, wp=wp, pd=pd):
                        return v.tensor_copy(
                            d[:, mc * wp + pd + nt * 512:mc * wp + pd + (nt + 1) * 512],
                            ps_mm[bk][:])
                    mk("vector", fn_ev, reads=[bf(f"ps_mm{bk}")], writes=[bf(dname, mc)],
                       extra=[t_pad], label=f"{pname}ev{p_i}_{mc}_{nt}")

        proj("q", wq_v, "wq_s", aq_v, "aq", WPQ, PADQ, t_lwq)
        t_lwv = mk("sync", lambda q: [q.dma_start(out=wv_v, in_=s_w["wv"][:])],
                   writes=[bf("wv_s")], fresh=("A4",), extra=[pc["wv"]],
                   label=f"lwv{p_i}", n_dma=1)
        proj("k", wk_v, "wk_s", ak, "ak", WPQ, PADQ, t_lwk)
        proj("v", wv_v, "wv_s", av, "av", WPV, PADV, t_lwv)

        t_beta_fin = None
        for tc in range(NTC):
            def fn_bmm(te, tc=tc):
                o = []
                for kc in range(NFC):
                    o.append(te.matmul(ps_ft[:, 0:H],
                                       xn_fm[:, kc * 1024 + tc * 128:kc * 1024 + (tc + 1) * 128],
                                       wb_r[:, kc * H:(kc + 1) * H],
                                       start=(kc == 0), stop=(kc == NFC - 1)))
                return o[-1]
            t_bmm = mk("tensor", fn_bmm, reads=[bf("xn_fm"), bf("wb_r")], writes=[bf("ps_ft")],
                       extra=[t_xn_done], label=f"bmm{p_i}_{tc}")
            t_ba = mk("scalar", lambda sc: sc.activation(smallb[:, 8:12], ps_ft[:, 0:H],
                                                         AF.Tanh, scale=smallb[:, 4:5]),
                      reads=[bf("ps_ft")], writes=[bf("smallb")], extra=[t_bmm],
                      label=f"bact{p_i}_{tc}")
            t_beta_fin = mk("vector",
                            lambda v, tc=tc: v.tensor_scalar(beta_tm[:, tc * H:(tc + 1) * H],
                                                             smallb[:, 8:12], 0.5, 0.5,
                                                             op0=ALU.mult, op1=ALU.add),
                            reads=[bf("smallb")], writes=[bf("beta_tm")], extra=[t_ba],
                            label=f"bfin{p_i}_{tc}")

        def fn_tails(v):
            last = None
            for bi, (dst, wp, pd) in enumerate(((aq_v, WPQ, PADQ), (ak, WPQ, PADQ),
                                                (av, WPV, PADV))):
                for fc in range(NFC):
                    if bi < 2:
                        tof = bi * NFC * PADQ
                        last = v.tensor_copy(tails[:, tof + fc * pd:tof + (fc + 1) * pd],
                                             dst[:, fc * wp + 1024:fc * wp + pd + 1024])
                    else:
                        tof = 2 * NFC * PADQ
                        last = v.tensor_copy(tails[:, tof + fc * 3:tof + (fc + 1) * 3],
                                             dst[:, fc * wp + pd + 1021:fc * wp + pd + 1024])
            return last
        t_tails = mk("vector", fn_tails,
                     reads=[bf("aq", i) for i in range(NFC)] + [bf("ak", i) for i in range(NFC)]
                     + [bf("av", i) for i in range(NFC)],
                     writes=[bf("tails")], label=f"tails{p_i}")

        def conv_block(pname, src, sname, taps_ap, ktaps, dst, dst_off, dstname, act,
                       wp, pd, h2span=(0, 1), extra_dep=None):
            for fc in range(NFC):
                acc_tasks = []
                for ih, nt in enumerate(range(h2span[0], h2span[1] + 1)):
                    def fn_acc(v, fc=fc, nt=nt, ih=ih, taps_ap=taps_ap, ktaps=ktaps,
                               src=src, wp=wp, pd=pd):
                        o = None
                        for s_ in range(ktaps):
                            lag = ktaps - 1 - s_
                            base = fc * wp + pd + nt * 512 - lag
                            if s_ == 0:
                                o = v.tensor_scalar(cacc[ih][:], src[:, base:base + 512],
                                                    taps_ap(fc, s_), None, op0=ALU.mult)
                            else:
                                o = v.scalar_tensor_tensor(cacc[ih][:],
                                                           src[:, base:base + 512],
                                                           taps_ap(fc, s_), cacc[ih][:],
                                                           op0=ALU.mult, op1=ALU.add)
                        return o
                    acc_tasks.append(mk(
                        "vector", fn_acc, reads=[bf(sname, fc), bf("auxb")],
                        writes=[bf("cacc", ih)],
                        fresh=("A8",) if (pname in ("cq", "lo") and fc == 0 and ih == 0) else (),
                        extra=[t_tails, extra_dep], label=f"cva_{pname}{p_i}_{fc}_{ih}"))

                dwr = [bf(dstname, fc) if isinstance(BUFS[dstname], list) else bf(dstname)]
                nsp = h2span[1] - h2span[0] + 1

                def fn_evs(sc, fc=fc, nsp=nsp, act=act, dst=dst, dst_off=dst_off):
                    o = []
                    for ih in range(nsp):
                        o.append(sc.activation(
                            dst[:, dst_off(fc) + ih * 512:dst_off(fc) + (ih + 1) * 512],
                            cacc[ih][:], act if act is not None else AF.Copy))
                    return o[-1]
                mk("scalar", fn_evs,
                   reads=[bf("cacc", ih) for ih in range(nsp)], writes=dwr,
                   extra=acc_tasks, label=f"cve_{pname}{p_i}_{fc}")

        def cw_ap(off):
            return lambda fc, s_: auxb[:, fc * CWW + off + s_:fc * CWW + off + s_ + 1]

        conv_block("cq", aq_v, "aq", cw_ap(CWO["q"]), 4, aq_v,
                   lambda fc: fc * WPQ + PADQ, "aq", AF.Silu, WPQ, PADQ)
        conv_block("ck", ak, "ak", cw_ap(CWO["k"]), 4, ak,
                   lambda fc: fc * WPQ + PADQ, "ak", AF.Silu, WPQ, PADQ)
        conv_block("cv", av, "av", cw_ap(CWO["v"]), 4, av,
                   lambda fc: fc * WPV + PADV, "av", AF.Silu, WPV, PADV)

        def fn_vpad(v, half=half):
            last = None
            tof = VTOF
            for fc in range(NFC):
                reg = av[:, fc * WPV:fc * WPV + PADV]
                last = (v.memset(reg, 0.0) if half == 0 else
                        v.tensor_copy(reg, tails[:, tof + fc * PADV:tof + (fc + 1) * PADV]))
            return last
        t_vpad = mk("vector", fn_vpad, reads=[bf("tails")],
                    writes=[bf("av", i) for i in range(NFC)], label=f"vpad{p_i}")

        def fn_vtail(v):
            last = None
            tof = VTOF
            for fc in range(NFC):
                last = v.tensor_copy(tails[:, tof + fc * PADV:tof + (fc + 1) * PADV],
                                     av[:, fc * WPV + 1024:fc * WPV + PADV + 1024])
            return last
        t_vtail = mk("vector", fn_vtail, reads=[bf("av", i) for i in range(NFC)],
                     writes=[bf("tails")], label=f"vtail{p_i}")

        for h2 in range(2):
            l2_last = None
            for bi, (src, sname) in enumerate(((aq_v, "aq"), (ak, "ak"))):
                for fc in range(NFC):
                    pi = (bi * NFC + fc) % 2

                    def fn_sq(sc, src=src, fc=fc, h2=h2, pi=pi):
                        return sc.activation(sq_v[pi][:],
                                             src[:, fc * WPQ + PADQ + h2 * 512:fc * WPQ + PADQ + (h2 + 1) * 512],
                                             AF.Square)
                    t_sq = mk("scalar", fn_sq, reads=[bf(sname, fc)], writes=[bf("sq", pi)],
                              fresh=("A8",) if (h2 == 0 and bi == 0 and fc == 0) else (),
                              label=f"l2sq{p_i}_{h2}_{bi}_{fc}")
                    first = (bi == 0 and fc == 0)
                    m = bi * 4 + fc // 2

                    def fn_m(te, m=m, pi=pi, first=first, last_=(bi == 1 and fc == NFC - 1)):
                        return te.matmul(ps_ft[0:8, 0:512], bigcol[:, 112 - m:120 - m][:, 0:8],
                                         sq_v[pi][:], start=first, stop=last_)
                    t_m = mk("tensor", fn_m, reads=[bf("sq", pi), bf("bigcol")],
                             writes=[bf("ps_ft")] if first else [],
                             extra=[t_sq, l2_last], label=f"l2mm{p_i}_{h2}_{bi}_{fc}")
                    if not first:
                        BUFS["ps_ft"].readers.append(t_m)
                    l2_last = t_m

            def fn_rs2(sc):
                sc.activation(rsq_v[0:8, 0:512], ps_ft[0:8, 0:512], AF.Ln,
                              bias=smallb[0:8, 5:6])
                return sc.activation(rsq_v[0:8, 0:512], rsq_v[0:8, 0:512], AF.Exp,
                                     scale=smallb[0:8, 3:4])
            t_l2rs = mk("scalar", fn_rs2, reads=[bf("smallb"), bf("ps_ft")], writes=[bf("rsq")],
                        fresh=("A7",) if h2 == 0 else (), extra=[l2_last],
                        label=f"l2rs{p_i}_{h2}")
            BUFS["ps_ft"].readers.append(t_l2rs)

            t_l2bf = mk("vector", lambda v: v.tensor_copy(rsb_v[0:8, :], rsq_v[0:8, 0:512]),
                        reads=[bf("rsq")], writes=[bf("rsb")], extra=[t_l2rs],
                        label=f"l2bf{p_i}_{h2}")

            for bi, (src, sname) in enumerate(((aq_v, "aq"), (ak, "ak"))):
                for fc in range(NFC):
                    hrow = bi * 4 + fc // 2

                    def fn_bc(te, hrow=hrow):
                        return te.matmul(ps_bc[:], sel8[:, hrow * 128:(hrow + 1) * 128],
                                         rsb_v[0:8, :], start=True, stop=True)
                    t_bc = mk("tensor", fn_bc, reads=[bf("rsb"), bf("sel8")],
                              writes=[bf("ps_bc")], extra=[t_l2bf],
                              label=f"l2bc{p_i}_{h2}_{bi}_{fc}")

                    def fn_ap2(v, src=src, fc=fc, h2=h2):
                        sl = src[:, fc * WPQ + PADQ + h2 * 512:fc * WPQ + PADQ + (h2 + 1) * 512]
                        return v.tensor_tensor(sl, sl, ps_bc[:], op=ALU.mult)
                    mk("vector", fn_ap2, reads=[bf("ps_bc"), bf(sname, fc)],
                       writes=[bf(sname, fc)], extra=[t_bc], label=f"l2ap{p_i}_{h2}_{bi}_{fc}")

        if p_i == 0:
            tap("t_beta", beta_tm[:], reads=[bf("beta_tm")])
            tap("t_aq", aq_v, reads=[bf("aq", i) for i in range(NFC)])
            tap("t_ak", ak[:], reads=[bf("ak", i) for i in range(NFC)])
            tap("t_av", av[:], reads=[bf("av", i) for i in range(NFC)])
        if p_i == 1:
            tap("t_av1", av[:], reads=[bf("av", i) for i in range(NFC)])

        if half == 0:
            mk("vector", lambda v: (v.memset(s_state[:], 0.0), v.memset(s_bf[:], 0.0))[-1],
               writes=[bf("s_state"), bf("s_bf")], label=f"s0_{p_i}")

        def qslq(buf, hfc, n):
            return buf[:, hfc * WPQ + PADQ + n * 128:hfc * WPQ + PADQ + (n + 1) * 128]

        def qslv(hfc, n):
            return av[:, hfc * WPV + PADV + n * 128:hfc * WPV + PADV + (n + 1) * 128]

        def pwt(hi):
            return powtf_v[:, hi * 128:(hi + 1) * 128]

        def pwu(hi):
            return powuf_v[:, hi * 128:(hi + 1) * 128]

        def xcf(hi):
            return xchf_v[:, hi * 256:(hi + 1) * 256]

        def xcb(hi):
            return xchb_v[:, hi * 256:(hi + 1) * 256]

        def delta_phase1(n, hg, sl):
            ph1 = {}
            for k2, h in enumerate((2 * hg, 2 * hg + 1)):
                hi = sl * 2 + k2
                f0, f1 = 2 * h, 2 * h + 1

                def fn_g(te, f0=f0, f1=f1, n=n):
                    te.matmul(ps_ms[:, 0:128], qslq(ak, f0, n), qslq(ak, f0, n),
                              start=True, stop=False)
                    return te.matmul(ps_ms[:, 0:128], qslq(ak, f1, n), qslq(ak, f1, n),
                                     start=False, stop=True)
                t_g = mk("tensor", fn_g, reads=[bf("ak", f0), bf("ak", f1)],
                         writes=[bf("ps_ms")], label=f"G{p_i}_{n}_{h}")

                def fn_a(v, hi=hi, h=h, n=n):
                    return v.scalar_tensor_tensor(abuf_v[:, hi * 128:(hi + 1) * 128],
                                                  ps_ms[:, 0:128],
                                                  beta_tm[:, n * H + h:n * H + h + 1],
                                                  lmask[:], op0=ALU.mult, op1=ALU.mult)
                t_a = mk("vector", fn_a, reads=[bf("ps_ms"), bf("beta_tm"), bf("lmask")],
                         writes=[bf("abuf", hi)],
                         fresh=("A6",) if (n == 0 and hg == 0 and k2 == 0) else (),
                         extra=[t_g, t_beta_fin], label=f"A{p_i}_{n}_{h}")

                t_att = mk("tensor", lambda te, hi=hi: te.transpose(
                    msb[:, 512:640], abuf_v[:, hi * 128:(hi + 1) * 128], eye[:]),
                    reads=[bf("abuf", hi), bf("eye")], writes=[bf("ps_ms")],
                    label=f"At{p_i}_{n}_{h}")

                def fn_p0(v, hi=hi):
                    v.tensor_copy(pwt(hi), msb[:, 512:640])
                    return v.tensor_copy(pwu(hi), abuf_v[:, hi * 128:(hi + 1) * 128])
                t_p0 = mk("vector", fn_p0, reads=[bf("ps_ms"), bf("abuf", hi)],
                          writes=[bf("pow_t", hi), bf("pow_u", hi)],
                          extra=[t_att], label=f"p0{p_i}_{n}_{h}")

                def fn_kv(te, f0=f0, f1=f1, n=n):
                    te.transpose(sub[:, 0:128], qslq(ak, f0, n), eye[:])
                    te.transpose(sub[:, 128:256], qslq(ak, f1, n), eye[:])
                    te.transpose(sub[:, 256:384], qslv(f0, n), eye[:])
                    return te.transpose(sub[:, 384:512], qslv(f1, n), eye[:])
                t_kv = mk("tensor", fn_kv, reads=[bf("ak", f0), bf("ak", f1),
                                                  bf("av", f0), bf("av", f1), bf("eye")],
                          writes=[bf("ps_su")], label=f"kv{p_i}_{n}_{h}")

                def fn_kve(v, hi=hi):
                    v.tensor_copy(ktm_v[:, hi * 256:(hi + 1) * 256], sub[:, 0:256])
                    return v.tensor_copy(vtm_v[:, hi * 256:(hi + 1) * 256], sub[:, 256:512])
                t_kve = mk("vector", fn_kve, reads=[bf("ps_su")],
                           writes=[bf("ktm", hi), bf("vtm", hi)], extra=[t_kv],
                           label=f"kve{p_i}_{n}_{h}")

                def fn_atn(te, f0=f0, f1=f1, n=n):
                    te.matmul(ps_ms[:, 256:384], qslq(ak, f0, n), qslq(aq_v, f0, n),
                              start=True, stop=False)
                    return te.matmul(ps_ms[:, 256:384], qslq(ak, f1, n), qslq(aq_v, f1, n),
                                     start=False, stop=True)
                t_atn = mk("tensor", fn_atn, reads=[bf("ak", f0), bf("ak", f1),
                                                    bf("aq", f0), bf("aq", f1)],
                           writes=[bf("ps_ms")], extra=[t_p0], label=f"atn{p_i}_{n}_{h}")

                t_atne = mk("vector", lambda v, hi=hi: v.tensor_tensor(
                    attnb_v[:, hi * 128:(hi + 1) * 128], ps_ms[:, 256:384], umaski[:],
                    op=ALU.mult),
                    reads=[bf("ps_ms"), bf("umaski")], writes=[bf("attnb", hi)],
                    extra=[t_atn], label=f"atne{p_i}_{n}_{h}")
                ph1[k2] = (t_p0, t_kve, t_atne)
            return ph1

        def delta_phase2(n, hg, sl, ph1):
            heads = (2 * hg, 2 * hg + 1)
            chain_prev = {}
            for k2, h in enumerate(heads):
                hi = sl * 2 + k2
                f0, f1 = 2 * h, 2 * h + 1

                def fn_ks(te, h=h, f0=f0, f1=f1, n=n):
                    te.matmul(ps_ch[:, 0:256], qslq(ak, f0, n),
                              s_bf[:, (2 * h) * 256:(2 * h + 1) * 256],
                              start=True, stop=False)
                    return te.matmul(ps_ch[:, 0:256], qslq(ak, f1, n),
                                     s_bf[:, (2 * h + 1) * 256:(2 * h + 2) * 256],
                                     start=False, stop=True)
                t_ks = mk("tensor", fn_ks, reads=[bf("ak", f0), bf("ak", f1), bf("s_bf")],
                          writes=[bf("ps_ch")], label=f"ks{p_i}_{n}_{h}")

                def fn_y(v, hi=hi, h=h, n=n):
                    v.scalar_tensor_tensor(xcf(hi), ps_ch[:, 0:256],
                                           -1.0, vtm_v[:, hi * 256:(hi + 1) * 256],
                                           op0=ALU.mult, op1=ALU.add)
                    return v.tensor_scalar(xcf(hi), xcf(hi),
                                           beta_tm[:, n * H + h:n * H + h + 1], None,
                                           op0=ALU.mult)
                chain_prev[k2] = mk("vector", fn_y,
                                    reads=[bf("ps_ch"), bf("vtm", hi), bf("beta_tm")],
                                    writes=[bf("xch", hi)], extra=[t_ks, ph1[k2][1]],
                                    label=f"Y{p_i}_{n}_{h}")

            pow_prev = {k2: ph1[k2][0] for k2 in range(2)}
            for step in range(7):
                for k2, h in enumerate(heads):
                    hi = sl * 2 + k2

                    def fn_cm2(te, hi=hi):
                        return te.matmul(ps_ch[:, 256:512], pwt(hi), xcf(hi),
                                         start=True, stop=True)
                    t_cm2 = mk("tensor", fn_cm2, reads=[bf("pow_t", hi), bf("xch", hi)],
                               writes=[bf("ps_ch")],
                               extra=[chain_prev[k2], pow_prev[k2]],
                               label=f"cm{p_i}_{n}_{h}_{step}")

                    def fn_cx(v, hi=hi, step=step):
                        r = v.tensor_tensor(xcf(hi), xcf(hi), ps_ch[:, 256:512],
                                            op=(ALU.subtract if step == 0 else ALU.add))
                        if step == 6:
                            r = v.tensor_copy(xcb(hi), xcf(hi))
                        return r
                    chain_prev[k2] = mk("vector", fn_cx,
                                        reads=[bf("ps_ch"), bf("xch", hi)],
                                        writes=[bf("xch", hi)]
                                        + ([bf("xchb", hi)] if step == 6 else []),
                                        extra=[t_cm2], label=f"cx{p_i}_{n}_{h}_{step}")
                if step == 6:
                    break
                for k2, h in enumerate(heads):
                    hi = sl * 2 + k2
                    off = 2 ** (step + 1)

                    def fn_sq2(te, hi=hi, off=off):
                        te.matmul(ps_pw[:, 0:128 - off], pwt(hi),
                                  pwu(hi)[:, 0:128 - off], start=True, stop=True)
                        return te.matmul(ps_pw[:, 128 + off:256], pwu(hi),
                                         pwt(hi)[:, off:128], start=True, stop=True)
                    t_s2 = mk("tensor", fn_sq2, reads=[bf("pow_u", hi), bf("pow_t", hi)],
                              writes=[bf("ps_pw")],
                              extra=[pow_prev[k2], chain_prev[k2]],
                              label=f"sq{p_i}_{n}_{h}_{step}")

                    def fn_se(v, hi=hi, off=off):
                        v.tensor_copy(pwu(hi)[:, 0:128 - off], ps_pw[:, 0:128 - off])
                        v.memset(pwu(hi)[:, 128 - off:128], 0.0)
                        v.tensor_copy(pwt(hi)[:, off:128], ps_pw[:, 128 + off:256])
                        return v.memset(pwt(hi)[:, 0:off], 0.0)
                    pow_prev[k2] = mk("vector", fn_se, reads=[bf("ps_pw")],
                                      writes=[bf("pow_u", hi), bf("pow_t", hi)],
                                      extra=[t_s2], label=f"se{p_i}_{n}_{h}_{step}")

            for k2, h in enumerate(heads):
                hi = sl * 2 + k2
                f0, f1 = 2 * h, 2 * h + 1

                def fn_o(te, hi=hi, h=h, f0=f0, f1=f1, n=n):
                    te.matmul(ps_ch[:, 0:256], qslq(aq_v, f0, n),
                              s_bf[:, (2 * h) * 256:(2 * h + 1) * 256],
                              start=True, stop=False)
                    te.matmul(ps_ch[:, 0:256], qslq(aq_v, f1, n),
                              s_bf[:, (2 * h + 1) * 256:(2 * h + 2) * 256],
                              start=False, stop=False)
                    return te.matmul(ps_ch[:, 0:256], attnb_v[:, hi * 128:(hi + 1) * 128],
                                     xcb(hi), start=False, stop=True)
                t_o = mk("tensor", fn_o, reads=[bf("aq", f0), bf("aq", f1), bf("s_bf"),
                                                bf("attnb", hi), bf("xchb", hi)],
                         writes=[bf("ps_ch")], extra=[chain_prev[k2], ph1[k2][2]],
                         label=f"o{p_i}_{n}_{h}")

                t_oe = mk("vector", lambda v, hi=hi: v.tensor_copy(
                    otm_v[:, hi * 256:(hi + 1) * 256], ps_ch[:, 0:256]),
                    reads=[bf("ps_ch")], writes=[bf("otm", hi)], extra=[t_o],
                    label=f"oe{p_i}_{n}_{h}")

                def fn_ot(te, hi=hi):
                    te.transpose(bcb[:, 0:128], otm_v[:, hi * 256:hi * 256 + 128], eye[:])
                    return te.transpose(bcb[:, 128:256],
                                        otm_v[:, hi * 256 + 128:hi * 256 + 256], eye[:])
                t_ot = mk("tensor", fn_ot, reads=[bf("otm", hi), bf("eye")],
                          writes=[bf("ps_bc")], extra=[t_oe],
                          label=f"ot{p_i}_{n}_{h}")

                def fn_ote(v, n=n, f0=f0, f1=f1):
                    v.tensor_copy(del_fm[:, f0 * 1024 + n * 128:f0 * 1024 + (n + 1) * 128],
                                  bcb[:, 0:128])
                    return v.tensor_copy(
                        del_fm[:, f1 * 1024 + n * 128:f1 * 1024 + (n + 1) * 128],
                        bcb[:, 128:256])
                mk("vector", fn_ote, reads=[bf("ps_bc")], writes=[bf("del_fm")],
                   extra=[t_ot], label=f"ote{p_i}_{n}_{h}")

                def fn_su2(te, hi=hi):
                    te.matmul(ps_pw[:, 0:256],
                              ktm_v[:, hi * 256:hi * 256 + 128],
                              xcb(hi), start=True, stop=True)
                    return te.matmul(ps_pw[:, 256:512],
                                     ktm_v[:, hi * 256 + 128:hi * 256 + 256],
                                     xcb(hi), start=True, stop=True)
                t_su2 = mk("tensor", fn_su2, reads=[bf("ktm", hi), bf("xchb", hi)],
                           writes=[bf("ps_pw")], extra=[t_oe], label=f"su{p_i}_{n}_{h}")

                def fn_sue(v, h=h):
                    s0 = s_state[:, (2 * h) * 256:(2 * h + 1) * 256]
                    s1 = s_state[:, (2 * h + 1) * 256:(2 * h + 2) * 256]
                    v.tensor_tensor(s0, s0, ps_pw[:, 0:256], op=ALU.add)
                    v.tensor_tensor(s1, s1, ps_pw[:, 256:512], op=ALU.add)
                    v.tensor_copy(s_bf[:, (2 * h) * 256:(2 * h + 1) * 256], s0)
                    return v.tensor_copy(s_bf[:, (2 * h + 1) * 256:(2 * h + 2) * 256], s1)
                mk("vector", fn_sue, reads=[bf("ps_pw"), bf("s_state")],
                   writes=[bf("s_state"), bf("s_bf")], extra=[t_su2],
                   label=f"sue{p_i}_{n}_{h}")

        pend = None
        for gi, (n, hg) in enumerate([(n, hg) for n in range(NTC) for hg in range(2)]):
            st = delta_phase1(n, hg, gi % 2)
            if pend is not None:
                delta_phase2(*pend)
            pend = (n, hg, gi % 2, st)
        delta_phase2(*pend)


        if p_i == 0:
            tap("t_del", del_fm[:], reads=[bf("del_fm")])
            tap("t_sst", s_state[:], reads=[bf("s_state")])
        if p_i == 1:
            tap("t_del1", del_fm[:], reads=[bf("del_fm")])

        shuf_mask = [p - 4 if (p % 8) >= 4 else p for p in range(32)]
        outs4 = [("loc_h", loc_h, None), ("mid_h", mid_h, None), ("del_fm", del_fm, None),
                 ("av", av, None)]

        wo_deps = []
        for h2 in range(2):
            conv_block("lo", av, "av",
                       lambda fc, s_: auxb[:, fc * CWW + LOCO + s_:fc * CWW + LOCO + s_ + 1], 7,
                       loc_h, lambda fc: fc * 512, "loc_h", None, WPV, PADV,
                       h2span=(h2, h2), extra_dep=t_vpad if h2 == 0 else t_vtail)
            conv_block("mi", av, "av", cw_ap(CWO["mid"]), 25,
                       mid_h, lambda fc: fc * 512, "mid_h", None, WPV, PADV,
                       h2span=(h2, h2), extra_dep=t_vpad if h2 == 0 else t_vtail)

            if p_i == 0 and h2 == 0:
                tap("t_loc", loc_h[:], reads=[bf("loc_h")])
                tap("t_mid", mid_h[:], reads=[bf("mid_h")])
            if p_i == 1 and h2 == 0:
                tap("t_loc1", loc_h[:], reads=[bf("loc_h")])
                tap("t_mid1", mid_h[:], reads=[bf("mid_h")])

            def osl(oi, fc, h2=h2):
                if oi == 0:
                    return loc_h[:, fc * 512:(fc + 1) * 512]
                if oi == 1:
                    return mid_h[:, fc * 512:(fc + 1) * 512]
                if oi == 2:
                    return del_fm[:, fc * 1024 + h2 * 512:fc * 1024 + (h2 + 1) * 512]
                return av[:, fc * WPV + PADV + h2 * 512:fc * WPV + PADV + (h2 + 1) * 512]

            def obuf(oi, fc):
                nm = outs4[oi][0]
                return bf(nm, fc) if isinstance(BUFS[nm], list) else bf(nm)

            if p_i == 0 and h2 == 0 and "t_dcp" in d_dbg:
                t_cpd = mk("vector", lambda v, osl=osl: (v.tensor_copy(prod_v[0][:], osl(2, 0)),
                                                         v.tensor_copy(prod_v[1][:], osl(3, 0)))[-1],
                           reads=[obuf(2, 0), obuf(3, 0)],
                           writes=[bf("prod", 0), bf("prod", 1)],
                           fresh=("A8",), label="dbgcp")
                tap("t_dcp", prod_v[0][:], reads=[bf("prod", 0)], extra=[t_cpd])
                tap("t_vcp", prod_v[1][:], reads=[bf("prod", 1)], extra=[t_cpd])

            ft_last = None
            first_ft = True
            for fc in range(NFC):
                hh = fc // 2
                for oi in (2, 3, 0, 1):
                    m = (2 * oi) * 4 + hh

                    def fn_fm(te, oi=oi, fc=fc, m=m, first=first_ft, osl=osl):
                        return te.matmul(ps_ft[0:56, 0:512], bigcol[:, 112 - m:168 - m][:, 0:56],
                                         osl(oi, fc), start=first, stop=False)
                    t_f = mk("tensor", fn_fm, reads=[obuf(oi, fc), bf("bigcol")],
                             writes=[bf("ps_ft")] if first_ft else [],
                             extra=[ft_last], label=f"ftm{p_i}_{h2}_{fc}_{oi}")
                    if not first_ft:
                        BUFS["ps_ft"].readers.append(t_f)
                    ft_last = t_f
                    first_ft = False
                for oi in range(4):
                    pi = oi % 2

                    def fn_sq3(sc, oi=oi, fc=fc, pi=pi, osl=osl):
                        return sc.activation(sq_v[pi][:], osl(oi, fc), AF.Square)
                    t_sq = mk("scalar", fn_sq3, reads=[obuf(oi, fc)], writes=[bf("sq", pi)],
                              fresh=("A8",) if (fc == 0 and oi == 0) else (),
                              label=f"fts{p_i}_{h2}_{fc}_{oi}")
                    m = (2 * oi + 1) * 4 + hh

                    def fn_fv(te, m=m, pi=pi):
                        return te.matmul(ps_ft[0:56, 0:512], bigcol[:, 112 - m:168 - m][:, 0:56],
                                         sq_v[pi][:], start=False, stop=False)
                    t_f = mk("tensor", fn_fv, reads=[bf("sq", pi), bf("bigcol")],
                             extra=[t_sq, ft_last], label=f"ftv{p_i}_{h2}_{fc}_{oi}")
                    BUFS["ps_ft"].readers.append(t_f)
                    ft_last = t_f
                cidx = 0
                for oi in range(4):
                    for oj in range(oi + 1, 4):
                        pi = cidx % 2

                        def fn_pr(v, oi=oi, oj=oj, fc=fc, pi=pi, osl=osl):
                            return v.tensor_tensor(prod_v[pi][:], osl(oi, fc), osl(oj, fc),
                                                   op=ALU.mult)
                        t_pr = mk("vector", fn_pr, reads=[obuf(oi, fc), obuf(oj, fc)],
                                  writes=[bf("prod", pi)],
                                  fresh=("A8",) if (fc == 0 and cidx == 0) else (),
                                  label=f"ftp{p_i}_{h2}_{fc}_{cidx}")
                        m = (8 + cidx) * 4 + hh
                        lastmm = (fc == NFC - 1 and cidx == 5)

                        def fn_fc2(te, m=m, pi=pi, lastmm=lastmm):
                            return te.matmul(ps_ft[0:56, 0:512],
                                             bigcol[:, 112 - m:168 - m][:, 0:56],
                                             prod_v[pi][:], start=False, stop=lastmm)
                        t_f = mk("tensor", fn_fc2, reads=[bf("prod", pi), bf("bigcol")],
                                 extra=[t_pr, ft_last], label=f"ftc{p_i}_{h2}_{fc}_{cidx}")
                        BUFS["ps_ft"].readers.append(t_f)
                        ft_last = t_f
                        cidx += 1

            def fn_ftp1(v):
                v.tensor_scalar(rf_tail[0:56, :], ps_ft[0:56, 0:512], 1.0 / DH, None,
                                op0=ALU.mult)
                return v.stream_shuffle(sq_v[0][:], rf_tail[:, 0:512], shuf_mask)
            t_ftp1 = mk("vector", fn_ftp1, reads=[bf("ps_ft")],
                        writes=[bf("rf_tail"), bf("sq", 0)],
                        extra=[ft_last], label=f"ftp1_{p_i}_{h2}")

            def fn_ftp2(v):
                v.tensor_tensor(prod_v[0][0:56, :], sq_v[0][0:56, :], sq_v[0][0:56, :],
                                op=ALU.mult)
                v.tensor_scalar(prod_v[1][0:56, :], ps_ft[0:56, 0:512],
                                1.0 / (DH - 1), None, op0=ALU.mult)
                v.scalar_tensor_tensor(prod_v[0][0:56, :], prod_v[0][0:56, :],
                                       -float(DH) / (DH - 1), prod_v[1][0:56, :],
                                       op0=ALU.mult, op1=ALU.add)
                v.tensor_scalar(rf_tail[0:56, :], rf_tail[0:56, :],
                                rowmask[0:56, 0:1], None, op0=ALU.mult)
                return v.scalar_tensor_tensor(rf_tail[0:56, :], prod_v[0][0:56, :],
                                              rowmask[0:56, 1:2], rf_tail[0:56, :],
                                              op0=ALU.mult, op1=ALU.add)
            t_ftp = mk("vector", fn_ftp2, reads=[bf("ps_ft"), bf("sq", 0), bf("rowmask")],
                       writes=[bf("rf_tail"), bf("prod", 0), bf("prod", 1)],
                       extra=[t_ftp1], label=f"ftp2_{p_i}_{h2}")

            if p_i == 0 and h2 == 0:
                tap("t_rf", rf_tail[:], reads=[bf("rf_tail")], extra=[t_ftp])

            lg_tasks = [None] * 4
            for mcg in range(9):
                mcs = [2 * mcg] + ([2 * mcg + 1] if 2 * mcg + 1 < 17 else [])
                dbi = mcg % 2
                mw = min(256, 2160 - mcg * 256)
                t_lr = mk("sync", lambda q, mcg=mcg, dbi=dbi: [
                    q.dma_start(out=rw1_v[dbi], in_=s_w["rw1"][mcg])],
                    writes=[bf("rw1", dbi)],
                    fresh=("A5",) if mcg == 0 else (),
                    extra=[pc["rw1"]], n_dma=1, label=f"lr{p_i}_{h2}_{mcg}")
                for mi, mc in enumerate(mcs):
                    rows = 128 if mc < 16 else 112
                    hbi = mc % 2

                    def fn_rmm(te, mc=mc, mi=mi, rows=rows, dbi=dbi, hbi=hbi, h2=h2):
                        o = []
                        for kc in range(8):
                            o.append(te.matmul(
                                ps_mm[hbi][0:rows, :],
                                rw1_v[dbi][:, kc * 256 + mi * 128:kc * 256 + mi * 128 + rows],
                                xn_fm[:, kc * 1024 + h2 * 512:kc * 1024 + (h2 + 1) * 512],
                                start=(kc == 0), stop=False))
                        o.append(te.matmul(
                            ps_mm[hbi][0:rows, :],
                            rw1_v[dbi][0:56, 8 * 256 + mi * 128:8 * 256 + mi * 128 + rows],
                            rf_tail[0:56, :], start=False, stop=True))
                        return o[-1]
                    t_rm = mk("tensor", fn_rmm, reads=[bf("rw1", dbi), bf("xn_fm"),
                                                       bf("rf_tail")],
                              writes=[bf(f"ps_mm{hbi}")], extra=[t_lr, t_ftp],
                              label=f"rt{p_i}_{h2}_{mc}")

                    def fn_rev(sc, mc=mc, rows=rows, hbi=hbi):
                        return sc.activation(h1t[0:rows, hbi * 512:(hbi + 1) * 512],
                                             ps_mm[hbi][0:rows, :], AF.Silu,
                                             bias=rb1_r[0:rows, mc:mc + 1])
                    t_re = mk("scalar", fn_rev, reads=[bf(f"ps_mm{hbi}"), bf("auxf")],
                              writes=[bf("h1t")], extra=[t_rm], label=f"rte{p_i}_{h2}_{mc}")

                    def fn_lg(te, mc=mc, hbi=hbi):
                        o = []
                        rows2 = 128 if mc < 16 else 112
                        for tcl in range(4):
                            o.append(te.matmul(
                                ps_ft[:, tcl * 128:tcl * 128 + 16],
                                h1t[0:rows2, hbi * 512 + tcl * 128:hbi * 512 + (tcl + 1) * 128],
                                rw2_r[0:rows2, mc * 16:(mc + 1) * 16],
                                start=(mc == 0 and tcl == 0), stop=(mc == 16)))
                        return o[-1]
                    t_lg = mk("tensor", fn_lg, reads=[bf("h1t"), bf("rw2_r")],
                              writes=[bf("ps_ft")] if mc == 0 else [],
                              extra=[t_re] + ([lg_tasks[0]] if mc > 0 else []),
                              label=f"lg{p_i}_{h2}_{mc}")
                    if mc != 0:
                        BUFS["ps_ft"].readers.append(t_lg)
                    lg_tasks[0] = t_lg
            t_lg_last = lg_tasks[0]

            pfm_done = []
            for tcl in range(4):
                def fn_sm1(v, tcl=tcl):
                    v.tensor_tensor(p_tm[:], ps_ft[:, tcl * 128:tcl * 128 + 16],
                                    rb2_bc[:], op=ALU.add)
                    v.tensor_copy(smallb[:, 0:2], smallb[:, 0:2])
                    v.reduce_sum(smallb[:, 12:16].unsqueeze(2),
                                 p_tm[:].rearrange("p (g i) -> p g i", i=4), axis=AX)
                    v.tensor_copy(smallb[:, 0:2], smallb[:, 0:2])
                    v.tensor_scalar(smallb[:, 12:16], smallb[:, 12:16], 0.25, None,
                                    op0=ALU.mult)
                    v.tensor_copy(smallb[:, 0:2], smallb[:, 0:2])
                    mb = smallb[:, 12:16].unsqueeze(2).broadcast_to([128, 4, 4])
                    return v.tensor_tensor(p_tm[:].rearrange("p (g i) -> p g i", i=4),
                                           p_tm[:].rearrange("p (g i) -> p g i", i=4), mb,
                                           op=ALU.subtract)
                t_s1 = mk("vector", fn_sm1, reads=[bf("ps_ft"), bf("auxf")],
                          writes=[bf("p_tm"), bf("smallb")],
                          extra=[t_lg_last], label=f"sm1{p_i}_{h2}_{tcl}")
                t_s2 = mk("scalar", lambda sc: sc.activation(p_tm[:], p_tm[:], AF.Exp),
                          writes=[bf("p_tm")], extra=[t_s1], label=f"sm2{p_i}_{h2}_{tcl}")

                def fn_sm3(v):
                    v.reduce_sum(smallb[:, 12:16].unsqueeze(2),
                                 p_tm[:].rearrange("p (g i) -> p g i", i=4), axis=AX)
                    v.tensor_copy(smallb[:, 0:2], smallb[:, 0:2])
                    v.reciprocal(smallb[:, 12:16], smallb[:, 12:16])
                    v.tensor_copy(smallb[:, 0:2], smallb[:, 0:2])
                    rb = smallb[:, 12:16].unsqueeze(2).broadcast_to([128, 4, 4])
                    v.tensor_tensor(p_tm[:].rearrange("p (g i) -> p g i", i=4),
                                    p_tm[:].rearrange("p (g i) -> p g i", i=4), rb,
                                    op=ALU.mult)
                    v.tensor_copy(smallb[:, 0:2], smallb[:, 0:2])
                    return v.tensor_scalar(p_tmb[:], p_tm[:], 0.96, 0.01,
                                           op0=ALU.mult, op1=ALU.add)
                t_s3 = mk("vector", fn_sm3, writes=[bf("p_tm"), bf("p_tmb"), bf("smallb")],
                          extra=[t_s2], label=f"sm3{p_i}_{h2}_{tcl}")
                t_pt = mk("tensor", lambda te: te.transpose(bcb[0:16, 256:384], p_tmb[:],
                                                            eye[:]),
                          reads=[bf("p_tmb"), bf("eye")], writes=[bf("ps_bc")],
                          extra=[t_s3], label=f"pt{p_i}_{h2}_{tcl}")
                pfm_done.append(mk("vector",
                                   lambda v, tcl=tcl: v.tensor_copy(
                                       p_fm[0:16, tcl * 128:(tcl + 1) * 128],
                                       bcb[0:16, 256:384]),
                                   reads=[bf("ps_bc")], writes=[bf("p_fm")],
                                   extra=[t_pt], label=f"pte{p_i}_{h2}_{tcl}"))

            if p_i == 0 and h2 == 0:
                tap("t_pfm", p_fm[:], reads=[bf("p_fm")], extra=pfm_done)

            rms_last = None
            for hh in range(H):
                pb_done = []
                for oi in range(4):
                    idx = hh * 4 + oi

                    def fn_pb(te, idx=idx):
                        return te.matmul(ps_bc[:], sel16[:, idx * 128:(idx + 1) * 128],
                                         p_fm[0:16, 0:512], start=True, stop=True)
                    t_pb = mk("tensor", fn_pb, reads=[bf("p_fm"), bf("sel16")],
                              writes=[bf("ps_bc")], extra=pfm_done, label=f"pb{p_i}_{h2}_{idx}")
                    pb_done.append(mk("vector",
                                      lambda v, oi=oi: v.tensor_copy(
                                          pb_v[:, oi * 512:(oi + 1) * 512], ps_bc[:]),
                                      reads=[bf("ps_bc")], writes=[bf("pb", oi)],
                                      extra=[t_pb], label=f"pbe{p_i}_{h2}_{idx}"))
                for fc in (2 * hh, 2 * hh + 1):
                    def fn_mix(v, fc=fc, osl=osl):
                        msl = mixed_v[:, fc * 512:(fc + 1) * 512]
                        v.tensor_tensor(msl, osl(0, fc), pb_v[:, 0:512], op=ALU.mult)
                        last = None
                        for oi in range(1, 4):
                            v.tensor_tensor(prod_v[0][:], osl(oi, fc),
                                            pb_v[:, oi * 512:(oi + 1) * 512], op=ALU.mult)
                            last = v.tensor_tensor(msl, msl, prod_v[0][:], op=ALU.add)
                        return last
                    t_mx = mk("vector", fn_mix,
                              reads=[obuf(oi, fc) for oi in range(4)]
                              + [bf("pb", oi) for oi in range(4)],
                              writes=[bf("mixed", fc), bf("prod", 0)],
                              extra=pb_done, label=f"mix{p_i}_{h2}_{fc}")

                    t_m2 = mk("scalar", lambda sc, fc=fc: sc.activation(
                        sq_v[0][:], mixed_v[:, fc * 512:(fc + 1) * 512], AF.Square),
                        reads=[bf("mixed", fc)], writes=[bf("sq", 0)], extra=[t_mx],
                        label=f"m2{p_i}_{h2}_{fc}")
                    t_m2w = mk("vector", lambda v, fc=fc: v.tensor_scalar(
                        sq_v[1][:], sq_v[0][:], w2col[:, fc:fc + 1], None, op0=ALU.mult),
                        reads=[bf("sq", 0), bf("w2col")], writes=[bf("sq", 1)], extra=[t_m2],
                        label=f"m2w{p_i}_{h2}_{fc}")

                    def fn_rmm(te, hh=hh, first=(fc == 0), last_=(fc == NFC - 1)):
                        te.matmul(ps_ms[0:4, 0:512], bigcol[:, 112 - hh:120 - hh][:, 0:4],
                                  sq_v[0][:], start=first, stop=last_)
                        return te.matmul(ps_su[0:4, 0:512], bigcol[:, 112 - hh:120 - hh][:, 0:4],
                                         sq_v[1][:], start=first, stop=last_)
                    t_rm = mk("tensor", fn_rmm, reads=[bf("sq", 0), bf("sq", 1), bf("bigcol")],
                              writes=[bf("ps_ms"), bf("ps_su")] if fc == 0 else [],
                              extra=[t_m2, t_m2w, rms_last], label=f"rmm{p_i}_{h2}_{fc}")
                    if fc != 0:
                        BUFS["ps_ms"].readers.append(t_rm)
                        BUFS["ps_su"].readers.append(t_rm)
                    rms_last = t_rm

            def fn_rsa(v):
                v.tensor_scalar(rsA[0:4, 0:512], ps_ms[0:4, 0:512], 1.0 / DH, 1e-5,
                                op0=ALU.mult, op1=ALU.add)
                v.reciprocal(rsB[0:4, 0:512], rsA[0:4, 0:512])
                v.tensor_scalar(rsB[0:4, 0:512], rsB[0:4, 0:512], 1.0 / DH, None, op0=ALU.mult)
                v.tensor_tensor(rsB[0:4, 0:512], rsB[0:4, 0:512], ps_su[0:4, 0:512],
                                op=ALU.mult)
                return v.tensor_scalar(rsB[0:4, 0:512], rsB[0:4, 0:512], 1.0, 1e-5,
                                       op0=ALU.mult, op1=ALU.add)
            t_rsa = mk("vector", fn_rsa, reads=[bf("ps_ms"), bf("ps_su")],
                       writes=[bf("rsA"), bf("rsB")], fresh=("A7",), extra=[rms_last],
                       label=f"rsa{p_i}_{h2}")

            def fn_rsb2(sc):
                sc.activation(rsA[0:4, 0:512], rsA[0:4, 0:512], AF.Ln)
                return sc.activation(rsB[0:4, 0:512], rsB[0:4, 0:512], AF.Ln)
            t_rsb = mk("scalar", fn_rsb2, writes=[bf("rsA"), bf("rsB")], extra=[t_rsa],
                       label=f"rsb{p_i}_{h2}")
            t_rsc = mk("vector", lambda v: v.tensor_tensor(rsA[0:4, 0:512], rsA[0:4, 0:512],
                                                           rsB[0:4, 0:512], op=ALU.add),
                       writes=[bf("rsA")], extra=[t_rsb], label=f"rsc{p_i}_{h2}")
            t_rsd = mk("scalar", lambda sc: sc.activation(rsA[0:4, 0:512], rsA[0:4, 0:512],
                                                          AF.Exp, scale=smallb[0:4, 3:4]),
                       reads=[bf("smallb")], writes=[bf("rsA")], extra=[t_rsc],
                       label=f"rsd{p_i}_{h2}")
            t_rse = mk("vector", lambda v: v.tensor_copy(rs_bf2[0:4, :], rsA[0:4, 0:512]),
                       reads=[bf("rsA")], writes=[bf("rs_bf2")], extra=[t_rsd],
                       label=f"rse{p_i}_{h2}")

            for hh in range(H):
                bank, bank_n = (ps_bc, "ps_bc") if hh % 2 == 0 else (ps_pw, "ps_pw")

                def fn_rb(te, hh=hh, bank=bank):
                    return te.matmul(bank[:, 0:512], sel8[0:4, hh * 128:(hh + 1) * 128],
                                     rs_bf2[0:4, :], start=True, stop=True)
                t_rb = mk("tensor", fn_rb, reads=[bf("rs_bf2"), bf("sel8")],
                          writes=[bf(bank_n)], extra=[t_rse], label=f"rb{p_i}_{h2}_{hh}")
                for fc in (2 * hh, 2 * hh + 1):
                    def fn_on(v, fc=fc, bank=bank):
                        msl = mixed_v[:, fc * 512:(fc + 1) * 512]
                        return v.scalar_tensor_tensor(msl, msl, g_col[:, fc:fc + 1],
                                                      bank[:, 0:512],
                                                      op0=ALU.mult, op1=ALU.mult)
                    t_on = mk("vector", fn_on, reads=[bf(bank_n), bf("g_col"),
                                                      bf("mixed", fc)],
                              writes=[bf("mixed", fc)], extra=[t_rb],
                              label=f"on{p_i}_{h2}_{fc}")

            if p_i == 0 and h2 == 0:
                tap("t_rs", rs_bf2[:], reads=[bf("rs_bf2")], extra=[t_rse])
                tap("t_mix", mixed_v, reads=[bf("mixed", i) for i in range(NFC)])

            last_lwo = mk("sync", lambda q: [q.dma_start(out=wo_v, in_=s_w["wo"][:])],
                          writes=[bf("wo_s")], fresh=("A5",), extra=[pc["wo"]], n_dma=1,
                          label=f"lwo{p_i}_{h2}")
            for tcl in range(4):
                tc_g = h2 * 4 + tcl
                t_xl2 = mk("sync", lambda q, s=seq, t0=tok0, tc_g=tc_g: [
                    q.dma_start(out=x_t[:], in_=dx[s, t0 + tc_g * 128:t0 + (tc_g + 1) * 128, :])],
                    writes=[bf("x_t")], n_dma=1, label=f"xl2{p_i}_{tc_g}")
                acc_evs = []
                for n2 in range(2):
                    bk = n2

                    def fn_wo(te, tcl=tcl, n2=n2, bk=bk, h2=h2):
                        o = []
                        for kc in range(NFC):
                            o.append(te.matmul(
                                ps_mm[bk][:],
                                mixed_v[:, kc * 512 + tcl * 128:kc * 512 + (tcl + 1) * 128],
                                wo_v[:, kc * 1024 + n2 * 512:kc * 1024 + (n2 + 1) * 512],
                                start=(kc == 0), stop=(kc == NFC - 1)))
                        return o[-1]
                    t_wo2 = mk("tensor", fn_wo,
                               reads=[bf("mixed", kc) for kc in range(NFC)] + [bf("wo_s")],
                               writes=[bf(f"ps_mm{bk}")], extra=[last_lwo],
                               label=f"wo{p_i}_{h2}_{tcl}_{n2}")

                    def fn_acc(v, n2=n2, bk=bk):
                        return v.tensor_tensor(ln_scr[:, n2 * 512:(n2 + 1) * 512], x_t[:, n2 * 512:(n2 + 1) * 512],
                                               ps_mm[bk][:], op=ALU.add)
                    acc_evs.append(mk("vector", fn_acc, reads=[bf(f"ps_mm{bk}"), bf("x_t")],
                                      writes=[bf("ln_scr")],
                                      fresh=("A7",) if (tcl == 0 and n2 == 0) else (),
                                      extra=[t_wo2, t_xl2],
                                      label=f"acc{p_i}_{h2}_{tcl}_{n2}"))
                mk("gpsimd", lambda q, tc_g=tc_g: [
                    q.dma_start(out=dacc[tc_g * 128:(tc_g + 1) * 128, :], in_=ln_scr[:, 0:1024])],
                    reads=[bf("ln_scr")], writes=[bf("dacc", tc_g)], extra=acc_evs,
                    n_dma=1, label=f"accd{p_i}_{tc_g}")
                if p_i == 0 and h2 == 0 and tcl == 0:
                    tap("t_acc", ln_scr[:, 0:1024], reads=[bf("ln_scr")], extra=acc_evs)

        for h2 in range(2):
            ffn1_done = []
            t_lw1 = {}
            for blk in range(4):
                dbi = blk % 2
                t_lw1[blk] = mk("sync", lambda q, blk=blk, dbi=dbi: [
                    q.dma_start(out=w1t_v[dbi], in_=s_w["w1t"][blk])],
                    writes=[bf("w1t", dbi)],
                    fresh=("A4",) if (h2 == 0 and blk == 0) else (),
                    extra=[pc["w1t"]], n_dma=1, label=f"lw1{p_i}_{h2}_{blk}")
                for hi in range(8):
                    hc = blk * 8 + hi
                    bk = hc % 2

                    def fn_f1(te, hi=hi, dbi=dbi, bk=bk, h2=h2):
                        o = []
                        for kc in range(NFC):
                            o.append(te.matmul(
                                ps_mm[bk][:],
                                w1t_v[dbi][:, kc * 1024 + hi * 128:kc * 1024 + (hi + 1) * 128],
                                xn_fm[:, kc * 1024 + h2 * 512:kc * 1024 + (h2 + 1) * 512],
                                start=(kc == 0), stop=(kc == NFC - 1)))
                        return o[-1]
                    t_f1 = mk("tensor", fn_f1, reads=[bf("w1t", dbi), bf("xn_fm")],
                              writes=[bf(f"ps_mm{bk}")], extra=[t_lw1[blk]],
                              label=f"f1{p_i}_{h2}_{hc}")

                    def fn_f1e(sc, hc=hc, bk=bk):
                        return sc.activation(hq_v[:, hc * 512:(hc + 1) * 512], ps_mm[bk][:],
                                             AF.Relu, bias=b1_r[:, hc:hc + 1])
                    ffn1_done.append(mk("scalar", fn_f1e, reads=[bf(f"ps_mm{bk}"), bf("auxf")],
                                        writes=[bf("hq")],
                                        fresh=("A4",) if (h2 == 0 and hc == 0) else (),
                                        extra=[t_f1],
                                        label=f"f1e{p_i}_{h2}_{hc}"))

            if p_i == 0 and h2 == 0:
                tap("t_hq", hq_v, reads=[bf("hq")], extra=[ffn1_done[-1]])

            f2_mm = {}
            for c2 in range(2):
                t_lw2 = mk("sync", lambda q, c2=c2: [
                    q.dma_start(out=w2t_v, in_=s_w["w2t"][c2])],
                    writes=[bf("w2t")], fresh=("A5",) if (h2 == 0 and c2 == 0) else (),
                    extra=[pc["w2t"]], n_dma=1, label=f"lw2{p_i}_{h2}_{c2}")
                for tcl in range(4):
                    for n2 in range(2):
                        bidx = tcl * 2 + n2
                        bank = PSH[PSB[bidx]]

                        def fn_f2(te, c2=c2, tcl=tcl, n2=n2, bank=bank):
                            o = []
                            for hi in range(16):
                                hc = c2 * 16 + hi
                                o.append(te.matmul(
                                    bank[:],
                                    hq_v[:, hc * 512 + tcl * 128:hc * 512 + (tcl + 1) * 128],
                                    w2t_v[:, hi * 1024 + n2 * 512:hi * 1024 + (n2 + 1) * 512],
                                    start=(c2 == 0 and hi == 0), stop=False))
                            if c2 == 1:
                                o.append(te.matmul(bank[:], ones32[:],
                                                   bq_row[0:1, n2 * 512:(n2 + 1) * 512],
                                                   start=False, stop=True))
                            return o[-1]
                        t_f2 = mk("tensor", fn_f2,
                                  reads=[bf("hq"), bf("w2t"), bf("ones32"), bf("bq_row")],
                                  writes=[bf(PSB[bidx])] if c2 == 0 else [],
                                  extra=[t_lw2, ffn1_done[-1], f2_mm.get(bidx)],
                                  label=f"f2{p_i}_{h2}_{tcl}_{n2}_{c2}")
                        if c2 == 1:
                            BUFS[PSB[bidx]].readers.append(t_f2)
                        f2_mm[bidx] = t_f2

            for tcl in range(4):
                tc_g = h2 * 4 + tcl
                t_al = mk("sync", lambda q, tc_g=tc_g: [
                    q.dma_start(out=x_t[:], in_=dacc[tc_g * 128:(tc_g + 1) * 128, :])],
                    reads=[bf("dacc", tc_g)], writes=[bf("x_t")], n_dma=1,
                    label=f"al{p_i}_{tc_g}")
                f2evs = []
                for n2 in range(2):
                    bidx = tcl * 2 + n2
                    bank = PSH[PSB[bidx]]

                    def fn_q16(v, n2=n2, bank=bank):
                        return v.tensor_copy(c16_v[n2][:], bank[:])
                    t_q16 = mk("vector", fn_q16, reads=[bf(PSB[bidx])], writes=[bf("c16", n2)],
                               fresh=("A8",) if (h2 == 0 and tcl == 0 and n2 == 0) else (),
                               extra=[f2_mm[bidx]], label=f"q16{p_i}_{h2}_{tcl}_{n2}")

                    def fn_fin(v, n2=n2):
                        return v.tensor_tensor(x_t[:, n2 * 512:(n2 + 1) * 512],
                                               x_t[:, n2 * 512:(n2 + 1) * 512],
                                               c16_v[n2][:], op=ALU.add)
                    f2evs.append(mk("vector", fn_fin, reads=[bf("c16", n2), bf("x_t")],
                                    writes=[bf("x_t")], extra=[t_q16, t_al],
                                    label=f"fin{p_i}_{h2}_{tcl}_{n2}"))
                final_dma.append(mk("gpsimd", lambda q, s=seq, t0=tok0, tc_g=tc_g: [
                    q.dma_start(out=dout[s, t0 + tc_g * 128:t0 + (tc_g + 1) * 128, :],
                                in_=x_t[:, 0:1024])],
                    reads=[bf("x_t")], extra=f2evs, n_dma=1, label=f"od{p_i}_{tc_g}"))

    mk("sync", lambda q: q.nop(), extra=final_dma + tap_tasks, label="fin")

    BD.finalize()
    with nc.Block() as blk:
        blk.tensor(BD.emitter("tensor", sems))
        blk.vector(BD.emitter("vector", sems))
        blk.scalar(BD.emitter("scalar", sems))
        blk.gpsimd(BD.emitter("gpsimd", sems))
        blk.sync(BD.emitter("sync", sems))
    est.close()
    return nc


_NC_CACHE = {}


def _get_program():
    if "nc" not in _NC_CACHE:
        _NC_CACHE["nc"] = build_program()
    return _NC_CACHE["nc"]


def pack_aux(g):
    aux = np.zeros((128, AUXW), np.float32)
    for fc in range(8):
        b = fc * 44
        aux[:, b + 0:b + 4] = g["conv_q_w"][fc * 128:(fc + 1) * 128, :]
        aux[:, b + 4:b + 8] = g["conv_k_w"][fc * 128:(fc + 1) * 128, :]
        aux[:, b + 8:b + 12] = g["conv_v_w"][fc * 128:(fc + 1) * 128, :]
        aux[:, b + 12:b + 37] = g["mid_w"][fc * 128:(fc + 1) * 128, :]
        aux[:, b + 37:b + 44] = g["local_w"][fc * 128:(fc + 1) * 128, :]
    aux[:, 352:384] = g["b1"].reshape(32, 128).T
    aux[:, 384:400] = g["r_b1"][0:2048].reshape(16, 128).T
    aux[0:112, 400] = g["r_b1"][2048:2160]
    aux[:, 401:409] = g["mix_w"].reshape(-1).reshape(8, 128).T
    aux[:, 409:411] = g["onorm_w"].reshape(2, 128).T
    aux[:, 411:427] = np.broadcast_to(g["r_b2"], (128, 16))
    for fc in range(8):
        aux[:, 427 + fc * 4:427 + (fc + 1) * 4] = g["Wb"][fc * 128:(fc + 1) * 128, :]
    return aux


def kernel(**inputs):
    nc = _get_program()
    consts = host_consts()
    x = np.ascontiguousarray(np.asarray(inputs["L0_attn_hidden_states"], dtype=np.float32))
    wmap = {
        "Wq": "L0_attn_Wq", "Wk": "L0_attn_Wk", "Wv": "L0_attn_Wv", "Wb": "L0_attn_Wb",
        "conv_q_w": "L0_attn_conv_q_w", "conv_k_w": "L0_attn_conv_k_w",
        "conv_v_w": "L0_attn_conv_v_w", "local_w": "L0_attn_local_w",
        "mid_w": "L0_attn_mid_w", "r_w1": "L0_attn_r_w1", "r_b1": "L0_attn_r_b1",
        "r_w2": "L0_attn_r_w2", "r_b2": "L0_attn_r_b2", "mix_w": "L0_attn_mix_w",
        "onorm_w": "L0_attn_onorm_w", "Wo": "L0_attn_Wo",
        "w1": "L0_ffn_w1", "b1": "L0_ffn_b1", "w2": "L0_ffn_w2", "b2": "L0_ffn_b2",
    }
    g = {k: np.ascontiguousarray(np.asarray(inputs[v], dtype=np.float32))
         for k, v in wmap.items()}
    base = {k: g[k] for k in ("Wq", "Wk", "Wv", "r_w1", "r_w2", "Wo", "w1", "w2", "b2")}
    base["aux"] = pack_aux(g)
    base.update(consts)
    in_maps = []
    for c in range(NCORES):
        m = dict(base)
        m["x"] = np.ascontiguousarray(x[c * SPC:(c + 1) * SPC])
        in_maps.append(m)
    res = run_bass_kernel_spmd(nc, in_maps, core_ids=list(range(NCORES)))
    return np.concatenate([r["out"] for r in res.results], axis=0)


# version 07 (final)
# speedup vs baseline: 1.0776x; 1.0020x over previous
import numpy as np
import ml_dtypes
from contextlib import ExitStack

import concourse.bass as bass
import concourse.mybir as mybir
from concourse.bass_utils import run_bass_kernel_spmd

F32 = mybir.dt.float32
BF16 = mybir.dt.bfloat16
F16 = mybir.dt.float16
F8 = mybir.dt.float8e4
ALU = mybir.AluOpType
AF = mybir.ActivationFunctionType
AX = mybir.AxisListType.X

NCORES = 8
B, L, D = 32, 2048, 1024
SPC = B // NCORES
T = 1024
NPASS = SPC * (L // T)
NTC = T // 128
NFC = D // 128
H, DH = 4, 256
FFH = 4096
NHC = FFH // 128
PADV = 24
PADQ = 4
WPV = PADV + T
WPQ = PADQ + T

ENGINES = ("tensor", "vector", "scalar", "gpsimd", "sync")

AUXW = 464
AXF = 352


class Task:
    __slots__ = ("eng", "fn", "deps", "n_dma", "label", "milestone", "needed")

    def __init__(self, eng, fn, deps, n_dma, label):
        self.eng, self.fn, self.deps, self.n_dma, self.label = eng, fn, deps, n_dma, label
        self.milestone = None
        self.needed = False


class Bufr:
    __slots__ = ("w", "readers", "name")

    def __init__(self, name=""):
        self.w = None
        self.readers = []
        self.name = name


class Builder:
    def __init__(self, serialize=False):
        self.tasks = {e: [] for e in ENGINES}
        self.all = []
        self.arenas = {}
        self.serialize = serialize

    def mk(self, eng, fn, reads=(), writes=(), extra=(), fresh=(), n_dma=0, label=""):
        deps = [d for d in extra if d is not None]
        if self.serialize and self.all:
            deps.append(self.all[-1])
        for a in fresh:
            for b in self.arenas[a]:
                deps.extend(b.readers)
                if b.w is not None:
                    deps.append(b.w)
        for b in reads:
            if b.w is not None:
                deps.append(b.w)
        for b in writes:
            deps.extend(b.readers)
            if b.w is not None:
                deps.append(b.w)
        t = Task(eng, fn, deps, n_dma, label)
        self.tasks[eng].append(t)
        self.all.append(t)
        for b in reads:
            b.readers.append(t)
        for b in writes:
            b.w = t
            b.readers = []
        return t

    def finalize(self):
        for t in self.all:
            for d in t.deps:
                if d.eng != t.eng:
                    d.needed = True
        for e in ENGINES:
            cum = 0
            for t in self.tasks[e]:
                if t.needed:
                    cum += 16 * t.n_dma if t.n_dma else 1
                    t.milestone = cum

    def emitter(self, ename, sems):
        def run(engh):
            hw = dict.fromkeys(ENGINES, 0)
            for t in self.tasks[ename]:
                for d in t.deps:
                    if d.eng == ename:
                        continue
                    v = d.milestone
                    assert v is not None, f"{t.label}: dep {d.label} lacks milestone"
                    if v > hw[d.eng]:
                        engh.wait_ge(sems[d.eng], v)
                        hw[d.eng] = v
                insts = t.fn(engh)
                if t.needed:
                    if t.n_dma:
                        assert isinstance(insts, (list, tuple)) and len(insts) == t.n_dma, t.label
                        for i in insts:
                            i.then_inc(sems[ename], 16)
                    else:
                        last = insts[-1] if isinstance(insts, (list, tuple)) else insts
                        last.then_inc(sems[ename], 1)
        return run


def host_consts():
    b = ml_dtypes.bfloat16
    eye = np.eye(128, dtype=np.float32)
    lm = np.tril(np.ones((128, 128), np.float32), -1)
    um = np.triu(np.ones((128, 128), np.float32), 0)
    bc = np.zeros((128, 169), np.float32)
    bc[:, 112] = 1.0
    s8 = np.zeros((8, 8 * 128), np.float32)
    for h in range(8):
        s8[h, h * 128:(h + 1) * 128] = 1.0
    s16 = np.zeros((16, 16 * 128), np.float32)
    for h in range(16):
        s16[h, h * 128:(h + 1) * 128] = 1.0
    rm = np.zeros((128, 2), np.float32)
    for k in range(14):
        for h in range(4):
            rm[k * 4 + h, 1 if (k < 8 and k % 2 == 1) else 0] = 1.0
    return {"c_eye": eye.astype(b), "c_lmask": lm.astype(b),
            "c_umaski": um.astype(b), "c_bigcol": bc.astype(b),
            "c_sel8": s8.astype(b), "c_sel16": s16.astype(b),
            "c_rowmask": rm}


def build_program(debug_taps=(), serialize=False):
    nc = bass.Bass("TRN2", target_bir_lowering=False, detect_race_conditions=False)
    BD = Builder(serialize=serialize)
    mk = BD.mk

    dx = nc.dram_tensor("x", [SPC, L, D], F32, kind="ExternalInput").ap()
    dout = nc.dram_tensor("out", [SPC, L, D], F32, kind="ExternalOutput").ap()
    dacc = nc.dram_tensor("s_acc", [T, D], F32, kind="Internal").ap()
    dw = {}
    for nm, shp in [("Wq", [D, D]), ("Wk", [D, D]), ("Wv", [D, D]),
                    ("r_w1", [1080, 2160]), ("r_w2", [2160, 16]), ("Wo", [D, D]),
                    ("w1", [FFH, D]), ("w2", [D, FFH]), ("b2", [D]),
                    ("aux", [128, AUXW])]:
        dw[nm] = nc.dram_tensor(nm, shp, F32, kind="ExternalInput").ap()
    dcn = {nm: nc.dram_tensor(nm, shp, BF16, kind="ExternalInput").ap()
           for nm, shp in [("c_eye", [128, 128]), ("c_lmask", [128, 128]),
                           ("c_umaski", [128, 128]), ("c_bigcol", [128, 169]),
                           ("c_sel8", [8, 1024]), ("c_sel16", [16, 2048])]}
    dcn["c_rowmask"] = nc.dram_tensor("c_rowmask", [128, 2], F32, kind="ExternalInput").ap()
    s_w = {nm: nc.dram_tensor(f"s_{nm}", shp, dt_, kind="Internal").ap()
           for nm, shp, dt_ in [("wq", [128, 8192], BF16), ("wk", [128, 8192], BF16),
                                ("wv", [128, 8192], BF16), ("wo", [128, 8192], BF16),
                                ("rw1", [9, 128, 2304], BF16),
                                ("w1t", [4, 128, 8192], BF16),
                                ("w2t", [2, 128, 16384], F8)]}
    d_dbg = {nm: nc.dram_tensor(nm, list(shp), dt_, kind="ExternalOutput").ap()
             for nm, shp, dt_ in debug_taps}
    tap_tasks = []

    def tap(name, ap, reads=(), extra=()):
        if name not in d_dbg:
            return
        tap_tasks.append(mk("gpsimd",
                            lambda q, ap=ap, name=name: [q.dma_start(out=d_dbg[name][:], in_=ap)],
                            reads=list(reads), extra=[e for e in extra if e is not None],
                            n_dma=1, label=f"tap_{name}"))

    est = ExitStack()
    E = est.enter_context
    sems = {en: E(nc.semaphore(f"sm_{en}")) for en in ENGINES}

    BUFS = {}

    def sbuf(name, shape, dt_, nbuf=1):
        h = E(nc.sbuf_tensor(name, shape, dt_))
        BUFS[name] = Bufr(name) if nbuf == 1 else [Bufr(f"{name}{i}") for i in range(nbuf)]
        return h

    def vbuf(name, nbuf=1, arena=None):
        BUFS[name] = Bufr(name) if nbuf == 1 else [Bufr(f"{name}{i}") for i in range(nbuf)]
        if arena:
            bs = BUFS[name] if isinstance(BUFS[name], list) else [BUFS[name]]
            BD.arenas.setdefault(arena, []).extend(bs)

    def psum(name):
        h = E(nc.psum_tensor(name, [128, 512], F32))
        BUFS[name] = Bufr(name)
        return h

    def bf(name, i=None):
        b = BUFS[name]
        return b if i is None else b[i]

    x_t = sbuf("x_t", [128, 1024], F32)
    xn_fm = sbuf("xn_fm", [128, NFC * 1024], BF16)
    ak = sbuf("ak", [128, NFC * WPQ], BF16, nbuf=NFC)
    av = sbuf("av", [128, NFC * WPV], BF16, nbuf=NFC)
    tails = sbuf("tails", [128, 2 * NFC * PADQ + NFC * 3 + NFC * PADV], BF16)
    VTOF = 2 * NFC * PADQ + NFC * 3
    loc_h = sbuf("loc_h", [128, NFC * 512], BF16)
    mid_h = sbuf("mid_h", [128, NFC * 512], BF16)
    del_fm = sbuf("del_fm", [128, NFC * 1024], BF16, nbuf=2)
    beta_tm = sbuf("beta_tm", [128, NTC * H], F32)
    stat_t = sbuf("stat_t", [128, 4 * NTC], F32)
    s_state = sbuf("s_state", [128, 2 * H * 256], F32)
    s_bf = sbuf("s_bf", [128, 2 * H * 256], BF16)
    rf_tail = sbuf("rf_tail", [128, 512], BF16)
    h1t = sbuf("h1t", [128, 2 * 512], BF16)
    p_tm = sbuf("p_tm", [128, 16], F32)
    p_tmb = sbuf("p_tmb", [128, 16], BF16)
    p_fm = sbuf("p_fm", [16, 512], BF16)
    rs_bf2 = sbuf("rs_bf2", [4, 512], BF16)
    eye = sbuf("eye", [128, 128], BF16)
    lmask = sbuf("lmask", [128, 128], BF16)
    umaski = sbuf("umaski", [128, 128], BF16)
    bigcol = sbuf("bigcol", [128, 169], BF16)
    sel8 = sbuf("sel8", [8, 1024], BF16)
    sel16 = sbuf("sel16", [16, 2048], BF16)
    rowmask = sbuf("rowmask", [128, 2], F32)
    wb_r = sbuf("wb_r", [128, NFC * H], BF16)
    auxb = sbuf("auxb", [128, AXF], F32)
    auxf = sbuf("auxf", [128, AUXW - AXF], F32)
    rw2_r = sbuf("rw2_r", [128, 17 * 16], BF16)
    g_col = sbuf("g_col", [128, NFC], F32)
    w2col = sbuf("w2col", [128, NFC], F32)
    b1_r = auxf[:, 0:32]
    rb1_r = auxf[:, 32:49]
    mix8 = auxf[:, 49:57]
    onorm2 = auxf[:, 57:59]
    rb2_bc = auxf[:, 59:75]
    wbf32 = auxf[:, 75:107]
    bq_row = sbuf("bq_row", [1, 1024], F32)
    ones32 = sbuf("ones32", [1, 128], F32)
    smallb = sbuf("smallb", [128, 16], F32)

    u4 = sbuf("u4_raw", [128, 24608], BF16)
    u5 = sbuf("u5_raw", [128, 8192], BF16)
    u6 = sbuf("u6_raw", [128, 9216], BF16)
    u7 = sbuf("u7_raw", [128, 2048], BF16)
    u8 = sbuf("u8_raw", [128, 2048], BF16)
    for un in ("u4_raw", "u5_raw", "u6_raw", "u7_raw", "u8_raw"):
        del BUFS[un]

    wq_v = u4[:, 0:8192]
    wk_v = u4[:, 8192:16384]
    wv_v = u4[:, 0:8192]
    aq_v = u4[:, 16384:16384 + NFC * WPQ]
    w1t_v = [u4[:, 0:8192], u4[:, 8192:16384]]
    hq_v = u4[:, 16384:24576].bitcast(F8)
    pc_in = u4[:, 0:8192].bitcast(F32)
    pc_bf = u4[:, 8192:12288]
    pc_f8 = u4[:, 12288:14336].bitcast(F8)
    pc_t = u4[:, 14336:18432]
    pc_t8 = u4[:, 18432:20480].bitcast(F8)
    vbuf("wq_s", arena="A4"); vbuf("wk_s", arena="A4"); vbuf("wv_s", arena="A4")
    vbuf("aq", nbuf=NFC, arena="A4")
    vbuf("w1t", nbuf=2, arena="A4"); vbuf("hq", arena="A4")
    vbuf("pc_in", arena="A4"); vbuf("pc_bf", arena="A4"); vbuf("pc_f8", arena="A4")
    vbuf("pc_t", arena="A4"); vbuf("pc_t8", arena="A4")

    rw1_v = [u5[:, 0:2304], u5[:, 2304:4608]]
    wo_v = u5[:, 0:8192]
    w2t_v = u5[:, 0:8192].bitcast(F8)
    vbuf("rw1", nbuf=2, arena="A5"); vbuf("wo_s", arena="A5"); vbuf("w2t", arena="A5")

    powtf_v = u6[:, 0:1024].bitcast(F32)
    powuf_v = u6[:, 1024:2048].bitcast(F32)
    xchf_v = u6[:, 2048:4096].bitcast(F32)
    xchb_v = u6[:, 4096:5120]
    abuf_v = u6[:, 5120:5632]
    attnb_v = u6[:, 5632:6144]
    ktm_v = u6[:, 6144:7168]
    vtm_v = u6[:, 7168:8192]
    otm_v = u6[:, 8192:9216]
    pb_v = u6[:, 0:2048]
    mixed_v = u6[:, 2048:6144]
    vbuf("pow_t", nbuf=4, arena="A6"); vbuf("pow_u", nbuf=4, arena="A6")
    vbuf("abuf", nbuf=4, arena="A6")
    vbuf("attnb", nbuf=4, arena="A6")
    vbuf("ktm", nbuf=4, arena="A6"); vbuf("vtm", nbuf=4, arena="A6")
    vbuf("xch", nbuf=4, arena="A6"); vbuf("xchb", nbuf=4, arena="A6")
    vbuf("otm", nbuf=4, arena="A6")
    vbuf("pb", nbuf=4, arena="A6"); vbuf("mixed", nbuf=NFC, arena="A6")

    ln_scr = u7[:, 0:2048].bitcast(F32)
    rsq_v = u7[:, 0:1024].bitcast(F32)
    rsb_v = u7[:, 1024:1536]
    rsA = u7[:, 0:1024].bitcast(F32)
    rsB = u7[:, 1024:2048].bitcast(F32)
    vbuf("ln_scr", arena="A7"); vbuf("rsq", arena="A7"); vbuf("rsb", arena="A7")
    vbuf("rsA", arena="A7"); vbuf("rsB", arena="A7")

    xn_t_v = u8[:, 0:2048]
    prod_v = [u8[:, 0:512], u8[:, 512:1024]]
    sq_v = [u8[:, 1024:1536], u8[:, 1536:2048]]
    c16_v = [u8[:, 0:512].bitcast(F16), u8[:, 512:1024].bitcast(F16)]
    bqf16_v = u8[:, 0:1024].bitcast(F16)
    cacc = [u8[:, 0:1024].bitcast(F32), u8[:, 1024:2048].bitcast(F32)]
    vbuf("xn_t", nbuf=2, arena="A8")
    vbuf("prod", nbuf=2, arena="A8"); vbuf("sq", nbuf=2, arena="A8")
    vbuf("c16", nbuf=2, arena="A8"); vbuf("bqf16", arena="A8")
    vbuf("cacc", nbuf=2, arena="A8")
    vbuf("dacc", nbuf=NTC)

    def xnt(i):
        return xn_t_v[:, i * 1024:(i + 1) * 1024]

    ps_mm = [psum("ps_mm0"), psum("ps_mm1")]
    ps_pw = psum("ps_pw")
    ps_ms = psum("ps_ms")
    ps_ch = psum("ps_ch")
    ps_su = psum("ps_su")
    ps_ft = psum("ps_ft")
    ps_bc = psum("ps_bc")
    PSB = ["ps_mm0", "ps_mm1", "ps_pw", "ps_ms", "ps_ch", "ps_su", "ps_ft", "ps_bc"]
    PSH = {"ps_mm0": ps_mm[0], "ps_mm1": ps_mm[1], "ps_pw": ps_pw, "ps_ms": ps_ms,
           "ps_ch": ps_ch, "ps_su": ps_su, "ps_ft": ps_ft, "ps_bc": ps_bc}
    mmb0 = ps_mm[0][:].bitcast(BF16)
    msb = ps_ms[:].bitcast(BF16)
    sub = ps_su[:].bitcast(BF16)
    bcb = ps_bc[:].bitcast(BF16)

    CWO = {"q": 0, "k": 4, "v": 8, "mid": 12}
    LOCO = 37
    CWW = 44

    mk("sync", lambda q: [
        q.dma_start(out=eye[:], in_=dcn["c_eye"][:]),
        q.dma_start(out=lmask[:], in_=dcn["c_lmask"][:]),
        q.dma_start(out=umaski[:], in_=dcn["c_umaski"][:]),
        q.dma_start(out=bigcol[:], in_=dcn["c_bigcol"][:]),
        q.dma_start(out=sel8[:], in_=dcn["c_sel8"][:]),
        q.dma_start(out=sel16[:], in_=dcn["c_sel16"][:]),
        q.dma_start(out=rowmask[:], in_=dcn["c_rowmask"][:]),
    ], writes=[bf("eye"), bf("lmask"), bf("umaski"), bf("bigcol"), bf("sel8"), bf("sel16"),
               bf("rowmask")], n_dma=7, label="c")

    t_auxc = mk("sync", lambda q: [
        q.dma_start(out=auxb[:], in_=dw["aux"][:, 0:AXF]),
        q.dma_start(out=auxf[:], in_=dw["aux"][:, AXF:AUXW]),
        q.dma_start(out=bq_row[:], in_=dw["b2"].unsqueeze(0)),
    ], writes=[bf("auxb"), bf("auxf"), bf("bq_row")], n_dma=3, label="auxc")

    def fn_prep(v):
        on8 = onorm2.unsqueeze(1).broadcast_to([128, 4, 2])
        v.tensor_tensor(g_col[:].rearrange("p (a c) -> p a c", c=2),
                        mix8.rearrange("p (a c) -> p a c", c=2),
                        on8, op=ALU.mult)
        v.tensor_tensor(w2col[:], mix8, mix8, op=ALU.mult)
        v.memset(smallb[:, 2:3], 1e-5)
        v.memset(smallb[:, 3:4], -0.5)
        v.memset(smallb[:, 4:5], 0.5)
        v.memset(smallb[:, 5:6], 1e-12)
        v.memset(ones32[:], 1.0)
        v.memset(rf_tail[:], 0.0)
        v.tensor_copy(wb_r[:], wbf32)
        v.tensor_copy(bqf16_v[0:1, :], bq_row[:])
        return v.tensor_copy(bq_row[:], bqf16_v[0:1, :])
    t_prep = mk("vector", fn_prep, reads=[bf("auxf"), bf("bq_row")],
                writes=[bf("g_col"), bf("w2col"), bf("bq_row"), bf("smallb"),
                        bf("ones32"), bf("wb_r"), bf("rf_tail"), bf("bqf16")],
                extra=[t_auxc], label="prep")

    t_rw2l = mk("sync", lambda q: [
        q.dma_start(out=ln_scr[0:(128 if k < 16 else 112), k * 16:(k + 1) * 16],
                    in_=dw["r_w2"][k * 128:k * 128 + (128 if k < 16 else 112), :])
        for k in range(17)],
        writes=[bf("ln_scr")], n_dma=17, label="rw2l")

    def fn_rw2(v):
        v.memset(rw2_r[:], 0.0)
        return v.tensor_copy(rw2_r[:, 0:17 * 16], ln_scr[:, 0:17 * 16])
    mk("vector", fn_rw2, reads=[bf("ln_scr")], writes=[bf("rw2_r")],
       extra=[t_rw2l, t_prep], label="rw2")

    pc = {}

    def pc_direct(name, dram, nk, ncols, store_fn, n_store):
        last = None
        for kc in range(nk):
            rows = min(128, dram.shape[0] - kc * 128)
            mk("sync", lambda q, r=rows, c=ncols, k=kc, d=dram: [
                q.dma_start(out=pc_in[0:r, 0:c], in_=d[k * 128:k * 128 + r, :])],
               writes=[bf("pc_in")], fresh=("A4",) if (name == "wq" and kc == 0) else (),
               n_dma=1, label=f"pcl_{name}{kc}")
            mk("vector", lambda v, r=rows, c=ncols: v.tensor_copy(pc_bf[0:r, 0:c], pc_in[0:r, 0:c]),
               reads=[bf("pc_in")], writes=[bf("pc_bf")], label=f"pcc_{name}{kc}")
            last = mk("gpsimd", lambda q, k=kc, r=rows: store_fn(q, k, r),
                      reads=[bf("pc_bf")], n_dma=n_store, label=f"pcs_{name}{kc}")
        return last

    def st_w(s):
        return lambda q, k, r: [q.dma_start(out=s[0:r, k * 1024:(k + 1) * 1024],
                                            in_=pc_bf[0:r, 0:1024])]

    def st_rw1(q, k, r):
        return [q.dma_start(
            out=s_w["rw1"][mcg, 0:r, k * 256:k * 256 + min(256, 2160 - mcg * 256)],
            in_=pc_bf[0:r, mcg * 256:mcg * 256 + min(256, 2160 - mcg * 256)])
            for mcg in range(9)]

    pc["wq"] = pc_direct("wq", dw["Wq"], 8, 1024, st_w(s_w["wq"]), 1)
    pc["wk"] = pc_direct("wk", dw["Wk"], 8, 1024, st_w(s_w["wk"]), 1)
    pc["wv"] = pc_direct("wv", dw["Wv"], 8, 1024, st_w(s_w["wv"]), 1)
    pc["wo"] = pc_direct("wo", dw["Wo"], 8, 1024, st_w(s_w["wo"]), 1)
    pc["rw1"] = pc_direct("rw1", dw["r_w1"], 9, 2160, st_rw1, 9)

    for hc in range(NHC):
        mk("sync", lambda q, k=hc: [
            q.dma_start(out=pc_in[:, 0:1024], in_=dw["w1"][k * 128:(k + 1) * 128, :])],
           writes=[bf("pc_in")], n_dma=1, label=f"pcl_w1{hc}")
        mk("vector", lambda v: v.tensor_copy(pc_bf[:, 0:1024], pc_in[:, 0:1024]),
           reads=[bf("pc_in")], writes=[bf("pc_bf")], label=f"pcc_w1{hc}")

        def fn_tr(te):
            o = []
            for fc in range(8):
                o.append(te.transpose(mmb0[:, fc * 128:(fc + 1) * 128],
                                      pc_bf[:, fc * 128:(fc + 1) * 128], eye[:]))
            return o[-1]
        mk("tensor", fn_tr, reads=[bf("pc_bf"), bf("eye")],
           writes=[bf("ps_mm0")], label=f"pct_w1{hc}")

        def fn_ev(v):
            return v.tensor_copy(pc_t[:, 0:1024], mmb0[:, 0:1024])
        mk("vector", fn_ev, reads=[bf("ps_mm0")], writes=[bf("pc_t")],
           label=f"pce_w1{hc}")
        pc["w1t"] = mk("gpsimd", lambda q, k=hc: [
            q.dma_start(out=s_w["w1t"][k // 8, :,
                                       kc * 1024 + (k % 8) * 128:kc * 1024 + (k % 8) * 128 + 128],
                        in_=pc_t[:, kc * 128:(kc + 1) * 128]) for kc in range(8)],
            reads=[bf("pc_t")], n_dma=8, label=f"pcs_w1{hc}")

    for kc in range(8):
        for hf in range(2):
            mk("sync", lambda q, k=kc, h=hf: [
                q.dma_start(out=pc_in[:, 0:2048],
                            in_=dw["w2"][k * 128:(k + 1) * 128, h * 2048:(h + 1) * 2048])],
               writes=[bf("pc_in")], n_dma=1, label=f"pcl_w2{kc}_{hf}")

            def fn_q(v):
                v.tensor_copy(pc_f8[:, 0:2048], pc_in[:, 0:2048])
                return v.tensor_copy(pc_bf[:, 0:2048], pc_f8[:, 0:2048])
            mk("vector", fn_q, reads=[bf("pc_in")], writes=[bf("pc_bf"), bf("pc_f8")],
               label=f"pcq_w2{kc}_{hf}")
            for g in range(2):
                def fn_tr(te, g=g):
                    o = []
                    for j in range(8):
                        o.append(te.transpose(mmb0[:, j * 128:(j + 1) * 128],
                                              pc_bf[:, (g * 8 + j) * 128:(g * 8 + j + 1) * 128],
                                              eye[:]))
                    return o[-1]
                mk("tensor", fn_tr, reads=[bf("pc_bf"), bf("eye")],
                   writes=[bf("ps_mm0")], label=f"pct_w2{kc}_{hf}_{g}")

                def fn_ev(v, g=g):
                    return v.tensor_copy(pc_t8[:, g * 1024:(g + 1) * 1024], mmb0[:, 0:1024])
                mk("vector", fn_ev, reads=[bf("ps_mm0")], writes=[bf("pc_t8")],
                   label=f"pce_w2{kc}_{hf}_{g}")
                pc["w2t"] = mk("gpsimd", lambda q, k=kc, h=hf, g=g: [
                    q.dma_start(out=s_w["w2t"][h, :,
                                               (g * 8 + j) * 1024 + k * 128:(g * 8 + j) * 1024 + (k + 1) * 128],
                                in_=pc_t8[:, (g * 8 + j) * 128:(g * 8 + j + 1) * 128])
                    for j in range(8)],
                    reads=[bf("pc_t8")], n_dma=8, label=f"pcs_w2{kc}_{hf}_{g}")

    final_dma = []
    for p_i in range(NPASS):
        seq, half = p_i // 2, p_i % 2
        tok0 = half * T

        t_lwq = mk("sync", lambda q: [q.dma_start(out=wq_v, in_=s_w["wq"][:])],
                   writes=[bf("wq_s")], fresh=("A4",), extra=[pc["wq"]], n_dma=1,
                   label=f"lwq{p_i}")
        t_lwk = mk("sync", lambda q: [q.dma_start(out=wk_v, in_=s_w["wk"][:])],
                   writes=[bf("wk_s")], extra=[pc["wk"]], n_dma=1, label=f"lwk{p_i}")

        t_xn_done = None
        prev_ln = None
        for tc in range(NTC):
            pi = tc % 2
            t_xl = mk("sync", lambda q, s=seq, t0=tok0, tc=tc: [
                q.dma_start(out=x_t[:], in_=dx[s, t0 + tc * 128:t0 + (tc + 1) * 128, :])],
                writes=[bf("x_t")], n_dma=1, label=f"xl{p_i}_{tc}")
            if p_i == 0 and tc == 0:
                tap("t_xt", x_t[:], reads=[bf("x_t")])

            def fn_st(v, tc=tc):
                c0 = stat_t[:, tc * 4:tc * 4 + 1]
                c1 = stat_t[:, tc * 4 + 1:tc * 4 + 2]
                c2 = stat_t[:, tc * 4 + 2:tc * 4 + 3]
                c3 = stat_t[:, tc * 4 + 3:tc * 4 + 4]
                v.reduce_sum(c0, x_t[:], axis=AX)
                v.tensor_tensor(ln_scr[:, 0:1024], x_t[:], x_t[:], op=ALU.mult)
                v.tensor_scalar(c0, c0, 1.0 / 1024, None, op0=ALU.mult)
                v.reduce_sum(c1, ln_scr[:, 0:1024], axis=AX)
                v.tensor_tensor(c2, c0, c0, op=ALU.mult)
                v.tensor_copy(c3, c0)
                return v.scalar_tensor_tensor(c1, c1, 1.0 / 1024, c2,
                                              op0=ALU.mult, op1=ALU.subtract)
            t_st = mk("vector", fn_st, reads=[bf("x_t")], writes=[bf("stat_t"), bf("ln_scr")],
                      fresh=("A7",) if tc == 0 else (), extra=[t_prep, prev_ln],
                      label=f"lnst{p_i}_{tc}")
            if p_i == 0 and tc == 0:
                tap("t_lnscr", ln_scr[:, 0:1024], reads=[bf("ln_scr")], extra=[t_st])

            def fn_rs(sc, tc=tc):
                sc.activation(stat_t[:, tc * 4 + 2:tc * 4 + 3],
                              stat_t[:, tc * 4 + 1:tc * 4 + 2], AF.Ln,
                              bias=smallb[:, 2:3])
                return sc.activation(stat_t[:, tc * 4 + 3:tc * 4 + 4],
                                     stat_t[:, tc * 4 + 2:tc * 4 + 3], AF.Exp,
                                     scale=smallb[:, 3:4])
            t_rs = mk("scalar", fn_rs, reads=[bf("smallb")], writes=[bf("stat_t")],
                      extra=[t_st], label=f"lnrs{p_i}_{tc}")

            def fn_xn(v, tc=tc, pi=pi):
                return v.tensor_scalar(xnt(pi), x_t[:],
                                       stat_t[:, tc * 4:tc * 4 + 1],
                                       stat_t[:, tc * 4 + 3:tc * 4 + 4],
                                       op0=ALU.subtract, op1=ALU.mult)
            t_xn = mk("vector", fn_xn, reads=[bf("x_t"), bf("stat_t")],
                      writes=[bf("xn_t", pi)],
                      fresh=("A8",) if (tc == 0) else (), extra=[t_rs],
                      label=f"xn{p_i}_{tc}")
            prev_ln = t_xn

            def fn_tr(te, pi=pi):
                o = []
                for fc in range(NFC):
                    o.append(te.transpose(mmb0[:, fc * 128:(fc + 1) * 128],
                                          xnt(pi)[:, fc * 128:(fc + 1) * 128], eye[:]))
                return o[-1]
            t_tr = mk("tensor", fn_tr, reads=[bf("xn_t", pi), bf("eye")],
                      writes=[bf("ps_mm0")], label=f"xnT{p_i}_{tc}")

            def fn_ev(v, tc=tc):
                return v.tensor_copy(
                    xn_fm[:].rearrange("p (f c) -> p f c", c=1024)[:, :, tc * 128:(tc + 1) * 128],
                    mmb0[:, 0:1024].rearrange("p (f c) -> p f c", c=128))
            t_xn_done = mk("vector", fn_ev, reads=[bf("ps_mm0")],
                           writes=[bf("xn_fm")], extra=[t_tr], label=f"xnE{p_i}_{tc}")

        if p_i == 0:
            tap("t_stat", stat_t[:], reads=[bf("stat_t")])
            tap("t_xnfm", xn_fm[:], reads=[bf("xn_fm")])

        def fn_pad(v, half=half):
            last = None
            for bi, (dst, wp, pd) in enumerate(((aq_v, WPQ, PADQ), (ak, WPQ, PADQ),
                                                (av, WPV, PADV))):
                for fc in range(NFC):
                    if half == 0:
                        last = v.memset(dst[:, fc * wp:fc * wp + pd], 0.0)
                    elif bi < 2:
                        tof = bi * NFC * PADQ
                        last = v.tensor_copy(dst[:, fc * wp:fc * wp + pd],
                                             tails[:, tof + fc * pd:tof + (fc + 1) * pd])
                    else:
                        tof = 2 * NFC * PADQ
                        last = v.tensor_copy(dst[:, fc * wp + pd - 3:fc * wp + pd],
                                             tails[:, tof + fc * 3:tof + (fc + 1) * 3])
            return last
        t_pad = mk("vector", fn_pad, reads=[bf("tails")],
                   writes=[bf("aq", i) for i in range(NFC)] + [bf("ak", i) for i in range(NFC)]
                   + [bf("av", i) for i in range(NFC)], label=f"pad{p_i}")

        def proj(pname, wview, wname, dst, dname, wp, pd, dep):
            for mc in range(NFC):
                for nt in range(2):
                    bk = (mc * 2 + nt) % 2

                    def fn_mm(te, w=wview, mc=mc, nt=nt, bk=bk):
                        o = []
                        for kc in range(NFC):
                            o.append(te.matmul(
                                ps_mm[bk][:],
                                w[:, kc * 1024 + mc * 128:kc * 1024 + (mc + 1) * 128],
                                xn_fm[:, kc * 1024 + nt * 512:kc * 1024 + (nt + 1) * 512],
                                start=(kc == 0), stop=(kc == NFC - 1)))
                        return o[-1]
                    t_mm = mk("tensor", fn_mm, reads=[bf(wname), bf("xn_fm")],
                              writes=[bf(f"ps_mm{bk}")], extra=[dep, t_xn_done],
                              label=f"{pname}mm{p_i}_{mc}_{nt}")

                    def fn_ev(v, d=dst, mc=mc, nt=nt, bk=bk, wp=wp, pd=pd):
                        return v.tensor_copy(
                            d[:, mc * wp + pd + nt * 512:mc * wp + pd + (nt + 1) * 512],
                            ps_mm[bk][:])
                    mk("vector", fn_ev, reads=[bf(f"ps_mm{bk}")], writes=[bf(dname, mc)],
                       extra=[t_pad], label=f"{pname}ev{p_i}_{mc}_{nt}")

        proj("q", wq_v, "wq_s", aq_v, "aq", WPQ, PADQ, t_lwq)
        t_lwv = mk("sync", lambda q: [q.dma_start(out=wv_v, in_=s_w["wv"][:])],
                   writes=[bf("wv_s")], fresh=("A4",), extra=[pc["wv"]],
                   label=f"lwv{p_i}", n_dma=1)
        proj("k", wk_v, "wk_s", ak, "ak", WPQ, PADQ, t_lwk)
        proj("v", wv_v, "wv_s", av, "av", WPV, PADV, t_lwv)

        t_beta_fin = None
        for tc in range(NTC):
            def fn_bmm(te, tc=tc):
                o = []
                for kc in range(NFC):
                    o.append(te.matmul(ps_ft[:, 0:H],
                                       xn_fm[:, kc * 1024 + tc * 128:kc * 1024 + (tc + 1) * 128],
                                       wb_r[:, kc * H:(kc + 1) * H],
                                       start=(kc == 0), stop=(kc == NFC - 1)))
                return o[-1]
            t_bmm = mk("tensor", fn_bmm, reads=[bf("xn_fm"), bf("wb_r")], writes=[bf("ps_ft")],
                       extra=[t_xn_done], label=f"bmm{p_i}_{tc}")
            t_ba = mk("scalar", lambda sc: sc.activation(smallb[:, 8:12], ps_ft[:, 0:H],
                                                         AF.Tanh, scale=smallb[:, 4:5]),
                      reads=[bf("ps_ft")], writes=[bf("smallb")], extra=[t_bmm],
                      label=f"bact{p_i}_{tc}")
            t_beta_fin = mk("vector",
                            lambda v, tc=tc: v.tensor_scalar(beta_tm[:, tc * H:(tc + 1) * H],
                                                             smallb[:, 8:12], 0.5, 0.5,
                                                             op0=ALU.mult, op1=ALU.add),
                            reads=[bf("smallb")], writes=[bf("beta_tm")], extra=[t_ba],
                            label=f"bfin{p_i}_{tc}")

        def fn_tails(v):
            last = None
            for bi, (dst, wp, pd) in enumerate(((aq_v, WPQ, PADQ), (ak, WPQ, PADQ),
                                                (av, WPV, PADV))):
                for fc in range(NFC):
                    if bi < 2:
                        tof = bi * NFC * PADQ
                        last = v.tensor_copy(tails[:, tof + fc * pd:tof + (fc + 1) * pd],
                                             dst[:, fc * wp + 1024:fc * wp + pd + 1024])
                    else:
                        tof = 2 * NFC * PADQ
                        last = v.tensor_copy(tails[:, tof + fc * 3:tof + (fc + 1) * 3],
                                             dst[:, fc * wp + pd + 1021:fc * wp + pd + 1024])
            return last
        t_tails = mk("vector", fn_tails,
                     reads=[bf("aq", i) for i in range(NFC)] + [bf("ak", i) for i in range(NFC)]
                     + [bf("av", i) for i in range(NFC)],
                     writes=[bf("tails")], label=f"tails{p_i}")

        def conv_block(pname, src, sname, taps_ap, ktaps, dst, dst_off, dstname, act,
                       wp, pd, h2span=(0, 1), extra_dep=None):
            for fc in range(NFC):
                acc_tasks = []
                for ih, nt in enumerate(range(h2span[0], h2span[1] + 1)):
                    def fn_acc(v, fc=fc, nt=nt, ih=ih, taps_ap=taps_ap, ktaps=ktaps,
                               src=src, wp=wp, pd=pd):
                        o = None
                        for s_ in range(ktaps):
                            lag = ktaps - 1 - s_
                            base = fc * wp + pd + nt * 512 - lag
                            if s_ == 0:
                                o = v.tensor_scalar(cacc[ih][:], src[:, base:base + 512],
                                                    taps_ap(fc, s_), None, op0=ALU.mult)
                            else:
                                o = v.scalar_tensor_tensor(cacc[ih][:],
                                                           src[:, base:base + 512],
                                                           taps_ap(fc, s_), cacc[ih][:],
                                                           op0=ALU.mult, op1=ALU.add)
                        return o
                    acc_tasks.append(mk(
                        "vector", fn_acc, reads=[bf(sname, fc), bf("auxb")],
                        writes=[bf("cacc", ih)],
                        fresh=("A8",) if (pname in ("cq", "lo") and fc == 0 and ih == 0) else (),
                        extra=[t_tails, extra_dep], label=f"cva_{pname}{p_i}_{fc}_{ih}"))

                dwr = [bf(dstname, fc) if isinstance(BUFS[dstname], list) else bf(dstname)]
                nsp = h2span[1] - h2span[0] + 1

                def fn_evs(sc, fc=fc, nsp=nsp, act=act, dst=dst, dst_off=dst_off):
                    o = []
                    for ih in range(nsp):
                        o.append(sc.activation(
                            dst[:, dst_off(fc) + ih * 512:dst_off(fc) + (ih + 1) * 512],
                            cacc[ih][:], act if act is not None else AF.Copy))
                    return o[-1]
                mk("scalar", fn_evs,
                   reads=[bf("cacc", ih) for ih in range(nsp)], writes=dwr,
                   extra=acc_tasks, label=f"cve_{pname}{p_i}_{fc}")

        def cw_ap(off):
            return lambda fc, s_: auxb[:, fc * CWW + off + s_:fc * CWW + off + s_ + 1]

        conv_block("cq", aq_v, "aq", cw_ap(CWO["q"]), 4, aq_v,
                   lambda fc: fc * WPQ + PADQ, "aq", AF.Silu, WPQ, PADQ)
        conv_block("ck", ak, "ak", cw_ap(CWO["k"]), 4, ak,
                   lambda fc: fc * WPQ + PADQ, "ak", AF.Silu, WPQ, PADQ)
        conv_block("cv", av, "av", cw_ap(CWO["v"]), 4, av,
                   lambda fc: fc * WPV + PADV, "av", AF.Silu, WPV, PADV)

        def fn_vpad(v, half=half):
            last = None
            tof = VTOF
            for fc in range(NFC):
                reg = av[:, fc * WPV:fc * WPV + PADV]
                last = (v.memset(reg, 0.0) if half == 0 else
                        v.tensor_copy(reg, tails[:, tof + fc * PADV:tof + (fc + 1) * PADV]))
            return last
        t_vpad = mk("vector", fn_vpad, reads=[bf("tails")],
                    writes=[bf("av", i) for i in range(NFC)], label=f"vpad{p_i}")

        def fn_vtail(v):
            last = None
            tof = VTOF
            for fc in range(NFC):
                last = v.tensor_copy(tails[:, tof + fc * PADV:tof + (fc + 1) * PADV],
                                     av[:, fc * WPV + 1024:fc * WPV + PADV + 1024])
            return last
        t_vtail = mk("vector", fn_vtail, reads=[bf("av", i) for i in range(NFC)],
                     writes=[bf("tails")], label=f"vtail{p_i}")

        for h2 in range(2):
            l2_last = None
            for bi, (src, sname) in enumerate(((aq_v, "aq"), (ak, "ak"))):
                for fc in range(NFC):
                    pi = (bi * NFC + fc) % 2

                    def fn_sq(sc, src=src, fc=fc, h2=h2, pi=pi):
                        return sc.activation(sq_v[pi][:],
                                             src[:, fc * WPQ + PADQ + h2 * 512:fc * WPQ + PADQ + (h2 + 1) * 512],
                                             AF.Square)
                    t_sq = mk("scalar", fn_sq, reads=[bf(sname, fc)], writes=[bf("sq", pi)],
                              fresh=("A8",) if (h2 == 0 and bi == 0 and fc == 0) else (),
                              label=f"l2sq{p_i}_{h2}_{bi}_{fc}")
                    first = (bi == 0 and fc == 0)
                    m = bi * 4 + fc // 2

                    def fn_m(te, m=m, pi=pi, first=first, last_=(bi == 1 and fc == NFC - 1)):
                        return te.matmul(ps_ft[0:8, 0:512], bigcol[:, 112 - m:120 - m][:, 0:8],
                                         sq_v[pi][:], start=first, stop=last_)
                    t_m = mk("tensor", fn_m, reads=[bf("sq", pi), bf("bigcol")],
                             writes=[bf("ps_ft")] if first else [],
                             extra=[t_sq, l2_last], label=f"l2mm{p_i}_{h2}_{bi}_{fc}")
                    if not first:
                        BUFS["ps_ft"].readers.append(t_m)
                    l2_last = t_m

            def fn_rs2(sc):
                sc.activation(rsq_v[0:8, 0:512], ps_ft[0:8, 0:512], AF.Ln,
                              bias=smallb[0:8, 5:6])
                return sc.activation(rsq_v[0:8, 0:512], rsq_v[0:8, 0:512], AF.Exp,
                                     scale=smallb[0:8, 3:4])
            t_l2rs = mk("scalar", fn_rs2, reads=[bf("smallb"), bf("ps_ft")], writes=[bf("rsq")],
                        fresh=("A7",) if h2 == 0 else (), extra=[l2_last],
                        label=f"l2rs{p_i}_{h2}")
            BUFS["ps_ft"].readers.append(t_l2rs)

            t_l2bf = mk("vector", lambda v: v.tensor_copy(rsb_v[0:8, :], rsq_v[0:8, 0:512]),
                        reads=[bf("rsq")], writes=[bf("rsb")], extra=[t_l2rs],
                        label=f"l2bf{p_i}_{h2}")

            for bi, (src, sname) in enumerate(((aq_v, "aq"), (ak, "ak"))):
                for fc in range(NFC):
                    hrow = bi * 4 + fc // 2

                    def fn_bc(te, hrow=hrow):
                        return te.matmul(ps_bc[:], sel8[:, hrow * 128:(hrow + 1) * 128],
                                         rsb_v[0:8, :], start=True, stop=True)
                    t_bc = mk("tensor", fn_bc, reads=[bf("rsb"), bf("sel8")],
                              writes=[bf("ps_bc")], extra=[t_l2bf],
                              label=f"l2bc{p_i}_{h2}_{bi}_{fc}")

                    def fn_ap2(v, src=src, fc=fc, h2=h2):
                        sl = src[:, fc * WPQ + PADQ + h2 * 512:fc * WPQ + PADQ + (h2 + 1) * 512]
                        return v.tensor_tensor(sl, sl, ps_bc[:], op=ALU.mult)
                    mk("vector", fn_ap2, reads=[bf("ps_bc"), bf(sname, fc)],
                       writes=[bf(sname, fc)], extra=[t_bc], label=f"l2ap{p_i}_{h2}_{bi}_{fc}")

        if p_i == 0:
            tap("t_beta", beta_tm[:], reads=[bf("beta_tm")])
            tap("t_aq", aq_v, reads=[bf("aq", i) for i in range(NFC)])
            tap("t_ak", ak[:], reads=[bf("ak", i) for i in range(NFC)])
            tap("t_av", av[:], reads=[bf("av", i) for i in range(NFC)])
        if p_i == 1:
            tap("t_av1", av[:], reads=[bf("av", i) for i in range(NFC)])

        if half == 0:
            mk("vector", lambda v: (v.memset(s_state[:], 0.0), v.memset(s_bf[:], 0.0))[-1],
               writes=[bf("s_state"), bf("s_bf")], label=f"s0_{p_i}")

        def qslq(buf, hfc, n):
            return buf[:, hfc * WPQ + PADQ + n * 128:hfc * WPQ + PADQ + (n + 1) * 128]

        def qslv(hfc, n):
            return av[:, hfc * WPV + PADV + n * 128:hfc * WPV + PADV + (n + 1) * 128]

        def pwt(hi):
            return powtf_v[:, hi * 128:(hi + 1) * 128]

        def pwu(hi):
            return powuf_v[:, hi * 128:(hi + 1) * 128]

        def xcf(hi):
            return xchf_v[:, hi * 256:(hi + 1) * 256]

        def xcb(hi):
            return xchb_v[:, hi * 256:(hi + 1) * 256]

        def delta_phase1(n, hg, sl):
            ph1 = {}
            for k2, h in enumerate((2 * hg, 2 * hg + 1)):
                hi = sl * 2 + k2
                f0, f1 = 2 * h, 2 * h + 1

                def fn_g(te, f0=f0, f1=f1, n=n):
                    te.matmul(ps_ms[:, 0:128], qslq(ak, f0, n), qslq(ak, f0, n),
                              start=True, stop=False)
                    return te.matmul(ps_ms[:, 0:128], qslq(ak, f1, n), qslq(ak, f1, n),
                                     start=False, stop=True)
                t_g = mk("tensor", fn_g, reads=[bf("ak", f0), bf("ak", f1)],
                         writes=[bf("ps_ms")], label=f"G{p_i}_{n}_{h}")

                def fn_a(v, hi=hi, h=h, n=n):
                    return v.scalar_tensor_tensor(abuf_v[:, hi * 128:(hi + 1) * 128],
                                                  ps_ms[:, 0:128],
                                                  beta_tm[:, n * H + h:n * H + h + 1],
                                                  lmask[:], op0=ALU.mult, op1=ALU.mult)
                t_a = mk("vector", fn_a, reads=[bf("ps_ms"), bf("beta_tm"), bf("lmask")],
                         writes=[bf("abuf", hi)],
                         fresh=("A6",) if (n == 0 and hg == 0 and k2 == 0) else (),
                         extra=[t_g, t_beta_fin], label=f"A{p_i}_{n}_{h}")

                t_att = mk("tensor", lambda te, hi=hi: te.transpose(
                    msb[:, 512:640], abuf_v[:, hi * 128:(hi + 1) * 128], eye[:]),
                    reads=[bf("abuf", hi), bf("eye")], writes=[bf("ps_ms")],
                    label=f"At{p_i}_{n}_{h}")

                def fn_p0(v, hi=hi):
                    v.tensor_copy(pwt(hi), msb[:, 512:640])
                    return v.tensor_copy(pwu(hi), abuf_v[:, hi * 128:(hi + 1) * 128])
                t_p0 = mk("vector", fn_p0, reads=[bf("ps_ms"), bf("abuf", hi)],
                          writes=[bf("pow_t", hi), bf("pow_u", hi)],
                          extra=[t_att], label=f"p0{p_i}_{n}_{h}")

                def fn_kv(te, f0=f0, f1=f1, n=n):
                    te.transpose(sub[:, 0:128], qslq(ak, f0, n), eye[:])
                    te.transpose(sub[:, 128:256], qslq(ak, f1, n), eye[:])
                    te.transpose(sub[:, 256:384], qslv(f0, n), eye[:])
                    return te.transpose(sub[:, 384:512], qslv(f1, n), eye[:])
                t_kv = mk("tensor", fn_kv, reads=[bf("ak", f0), bf("ak", f1),
                                                  bf("av", f0), bf("av", f1), bf("eye")],
                          writes=[bf("ps_su")], label=f"kv{p_i}_{n}_{h}")

                def fn_kve(v, hi=hi):
                    v.tensor_copy(ktm_v[:, hi * 256:(hi + 1) * 256], sub[:, 0:256])
                    return v.tensor_copy(vtm_v[:, hi * 256:(hi + 1) * 256], sub[:, 256:512])
                t_kve = mk("vector", fn_kve, reads=[bf("ps_su")],
                           writes=[bf("ktm", hi), bf("vtm", hi)], extra=[t_kv],
                           label=f"kve{p_i}_{n}_{h}")

                def fn_atn(te, f0=f0, f1=f1, n=n):
                    te.matmul(ps_ms[:, 256:384], qslq(ak, f0, n), qslq(aq_v, f0, n),
                              start=True, stop=False)
                    return te.matmul(ps_ms[:, 256:384], qslq(ak, f1, n), qslq(aq_v, f1, n),
                                     start=False, stop=True)
                t_atn = mk("tensor", fn_atn, reads=[bf("ak", f0), bf("ak", f1),
                                                    bf("aq", f0), bf("aq", f1)],
                           writes=[bf("ps_ms")], extra=[t_p0], label=f"atn{p_i}_{n}_{h}")

                t_atne = mk("vector", lambda v, hi=hi: v.tensor_tensor(
                    attnb_v[:, hi * 128:(hi + 1) * 128], ps_ms[:, 256:384], umaski[:],
                    op=ALU.mult),
                    reads=[bf("ps_ms"), bf("umaski")], writes=[bf("attnb", hi)],
                    extra=[t_atn], label=f"atne{p_i}_{n}_{h}")
                ph1[k2] = (t_p0, t_kve, t_atne)
            return ph1

        def delta_phase2(n, hg, sl, ph1):
            heads = (2 * hg, 2 * hg + 1)
            chain_prev = {}
            for k2, h in enumerate(heads):
                hi = sl * 2 + k2
                f0, f1 = 2 * h, 2 * h + 1

                def fn_ks(te, h=h, f0=f0, f1=f1, n=n):
                    te.matmul(ps_ch[:, 0:256], qslq(ak, f0, n),
                              s_bf[:, (2 * h) * 256:(2 * h + 1) * 256],
                              start=True, stop=False)
                    return te.matmul(ps_ch[:, 0:256], qslq(ak, f1, n),
                                     s_bf[:, (2 * h + 1) * 256:(2 * h + 2) * 256],
                                     start=False, stop=True)
                t_ks = mk("tensor", fn_ks, reads=[bf("ak", f0), bf("ak", f1), bf("s_bf")],
                          writes=[bf("ps_ch")], label=f"ks{p_i}_{n}_{h}")

                def fn_y(v, hi=hi, h=h, n=n):
                    v.scalar_tensor_tensor(xcf(hi), ps_ch[:, 0:256],
                                           -1.0, vtm_v[:, hi * 256:(hi + 1) * 256],
                                           op0=ALU.mult, op1=ALU.add)
                    return v.tensor_scalar(xcf(hi), xcf(hi),
                                           beta_tm[:, n * H + h:n * H + h + 1], None,
                                           op0=ALU.mult)
                chain_prev[k2] = mk("vector", fn_y,
                                    reads=[bf("ps_ch"), bf("vtm", hi), bf("beta_tm")],
                                    writes=[bf("xch", hi)], extra=[t_ks, ph1[k2][1]],
                                    label=f"Y{p_i}_{n}_{h}")

            pow_prev = {k2: ph1[k2][0] for k2 in range(2)}
            for step in range(7):
                for k2, h in enumerate(heads):
                    hi = sl * 2 + k2

                    def fn_cm2(te, hi=hi):
                        return te.matmul(ps_ch[:, 256:512], pwt(hi), xcf(hi),
                                         start=True, stop=True)
                    t_cm2 = mk("tensor", fn_cm2, reads=[bf("pow_t", hi), bf("xch", hi)],
                               writes=[bf("ps_ch")],
                               extra=[chain_prev[k2], pow_prev[k2]],
                               label=f"cm{p_i}_{n}_{h}_{step}")

                    def fn_cx(v, hi=hi, step=step):
                        r = v.tensor_tensor(xcf(hi), xcf(hi), ps_ch[:, 256:512],
                                            op=(ALU.subtract if step == 0 else ALU.add))
                        if step == 6:
                            r = v.tensor_copy(xcb(hi), xcf(hi))
                        return r
                    chain_prev[k2] = mk("vector", fn_cx,
                                        reads=[bf("ps_ch"), bf("xch", hi)],
                                        writes=[bf("xch", hi)]
                                        + ([bf("xchb", hi)] if step == 6 else []),
                                        extra=[t_cm2], label=f"cx{p_i}_{n}_{h}_{step}")
                if step == 6:
                    break
                for k2, h in enumerate(heads):
                    hi = sl * 2 + k2
                    off = 2 ** (step + 1)

                    def fn_sq2(te, hi=hi, off=off):
                        te.matmul(ps_pw[:, 0:128 - off], pwt(hi),
                                  pwu(hi)[:, 0:128 - off], start=True, stop=True)
                        return te.matmul(ps_pw[:, 128 + off:256], pwu(hi),
                                         pwt(hi)[:, off:128], start=True, stop=True)
                    t_s2 = mk("tensor", fn_sq2, reads=[bf("pow_u", hi), bf("pow_t", hi)],
                              writes=[bf("ps_pw")],
                              extra=[pow_prev[k2], chain_prev[k2]],
                              label=f"sq{p_i}_{n}_{h}_{step}")

                    def fn_se(v, hi=hi, off=off):
                        v.tensor_copy(pwu(hi)[:, 0:128 - off], ps_pw[:, 0:128 - off])
                        v.memset(pwu(hi)[:, 128 - off:128], 0.0)
                        v.tensor_copy(pwt(hi)[:, off:128], ps_pw[:, 128 + off:256])
                        return v.memset(pwt(hi)[:, 0:off], 0.0)
                    pow_prev[k2] = mk("vector", fn_se, reads=[bf("ps_pw")],
                                      writes=[bf("pow_u", hi), bf("pow_t", hi)],
                                      extra=[t_s2], label=f"se{p_i}_{n}_{h}_{step}")

            for k2, h in enumerate(heads):
                hi = sl * 2 + k2
                f0, f1 = 2 * h, 2 * h + 1

                def fn_o(te, hi=hi, h=h, f0=f0, f1=f1, n=n):
                    te.matmul(ps_ch[:, 0:256], qslq(aq_v, f0, n),
                              s_bf[:, (2 * h) * 256:(2 * h + 1) * 256],
                              start=True, stop=False)
                    te.matmul(ps_ch[:, 0:256], qslq(aq_v, f1, n),
                              s_bf[:, (2 * h + 1) * 256:(2 * h + 2) * 256],
                              start=False, stop=False)
                    return te.matmul(ps_ch[:, 0:256], attnb_v[:, hi * 128:(hi + 1) * 128],
                                     xcb(hi), start=False, stop=True)
                t_o = mk("tensor", fn_o, reads=[bf("aq", f0), bf("aq", f1), bf("s_bf"),
                                                bf("attnb", hi), bf("xchb", hi)],
                         writes=[bf("ps_ch")], extra=[chain_prev[k2], ph1[k2][2]],
                         label=f"o{p_i}_{n}_{h}")

                t_oe = mk("vector", lambda v, hi=hi: v.tensor_copy(
                    otm_v[:, hi * 256:(hi + 1) * 256], ps_ch[:, 0:256]),
                    reads=[bf("ps_ch")], writes=[bf("otm", hi)], extra=[t_o],
                    label=f"oe{p_i}_{n}_{h}")

                def fn_ot(te, hi=hi):
                    te.transpose(bcb[:, 0:128], otm_v[:, hi * 256:hi * 256 + 128], eye[:])
                    return te.transpose(bcb[:, 128:256],
                                        otm_v[:, hi * 256 + 128:hi * 256 + 256], eye[:])
                t_ot = mk("tensor", fn_ot, reads=[bf("otm", hi), bf("eye")],
                          writes=[bf("ps_bc")], extra=[t_oe],
                          label=f"ot{p_i}_{n}_{h}")

                def fn_ote(v, n=n, f0=f0, f1=f1):
                    v.tensor_copy(del_fm[:, f0 * 1024 + n * 128:f0 * 1024 + (n + 1) * 128],
                                  bcb[:, 0:128])
                    return v.tensor_copy(
                        del_fm[:, f1 * 1024 + n * 128:f1 * 1024 + (n + 1) * 128],
                        bcb[:, 128:256])
                mk("vector", fn_ote, reads=[bf("ps_bc")], writes=[bf("del_fm", n // 4)],
                   extra=[t_ot], label=f"ote{p_i}_{n}_{h}")

                def fn_su2(te, hi=hi):
                    te.matmul(ps_pw[:, 0:256],
                              ktm_v[:, hi * 256:hi * 256 + 128],
                              xcb(hi), start=True, stop=True)
                    return te.matmul(ps_pw[:, 256:512],
                                     ktm_v[:, hi * 256 + 128:hi * 256 + 256],
                                     xcb(hi), start=True, stop=True)
                t_su2 = mk("tensor", fn_su2, reads=[bf("ktm", hi), bf("xchb", hi)],
                           writes=[bf("ps_pw")], extra=[t_oe], label=f"su{p_i}_{n}_{h}")

                def fn_sue(v, h=h):
                    s0 = s_state[:, (2 * h) * 256:(2 * h + 1) * 256]
                    s1 = s_state[:, (2 * h + 1) * 256:(2 * h + 2) * 256]
                    v.tensor_tensor(s0, s0, ps_pw[:, 0:256], op=ALU.add)
                    v.tensor_tensor(s1, s1, ps_pw[:, 256:512], op=ALU.add)
                    v.tensor_copy(s_bf[:, (2 * h) * 256:(2 * h + 1) * 256], s0)
                    return v.tensor_copy(s_bf[:, (2 * h + 1) * 256:(2 * h + 2) * 256], s1)
                mk("vector", fn_sue, reads=[bf("ps_pw"), bf("s_state")],
                   writes=[bf("s_state"), bf("s_bf")], extra=[t_su2],
                   label=f"sue{p_i}_{n}_{h}")


        shuf_mask = [p - 4 if (p % 8) >= 4 else p for p in range(32)]
        outs4 = [("loc_h", loc_h, None), ("mid_h", mid_h, None), ("del_fm", del_fm, None),
                 ("av", av, None)]
        tail_ctx = {}

        def emit_tail_a(h2):
            conv_block("lo", av, "av",
                       lambda fc, s_: auxb[:, fc * CWW + LOCO + s_:fc * CWW + LOCO + s_ + 1], 7,
                       loc_h, lambda fc: fc * 512, "loc_h", None, WPV, PADV,
                       h2span=(h2, h2), extra_dep=t_vpad if h2 == 0 else t_vtail)
            yield
            conv_block("mi", av, "av", cw_ap(CWO["mid"]), 25,
                       mid_h, lambda fc: fc * 512, "mid_h", None, WPV, PADV,
                       h2span=(h2, h2), extra_dep=t_vpad if h2 == 0 else t_vtail)
            yield

            if p_i == 0 and h2 == 0:
                tap("t_loc", loc_h[:], reads=[bf("loc_h")])
                tap("t_mid", mid_h[:], reads=[bf("mid_h")])
            if p_i == 1 and h2 == 0:
                tap("t_loc1", loc_h[:], reads=[bf("loc_h")])
                tap("t_mid1", mid_h[:], reads=[bf("mid_h")])

            def osl(oi, fc, h2=h2):
                if oi == 0:
                    return loc_h[:, fc * 512:(fc + 1) * 512]
                if oi == 1:
                    return mid_h[:, fc * 512:(fc + 1) * 512]
                if oi == 2:
                    return del_fm[:, fc * 1024 + h2 * 512:fc * 1024 + (h2 + 1) * 512]
                return av[:, fc * WPV + PADV + h2 * 512:fc * WPV + PADV + (h2 + 1) * 512]

            def obuf(oi, fc):
                nm = outs4[oi][0]
                if nm == "del_fm":
                    return bf(nm, h2)
                return bf(nm, fc) if isinstance(BUFS[nm], list) else bf(nm)

            if p_i == 0 and h2 == 0 and "t_dcp" in d_dbg:
                t_cpd = mk("vector", lambda v, osl=osl: (v.tensor_copy(prod_v[0][:], osl(2, 0)),
                                                         v.tensor_copy(prod_v[1][:], osl(3, 0)))[-1],
                           reads=[obuf(2, 0), obuf(3, 0)],
                           writes=[bf("prod", 0), bf("prod", 1)],
                           fresh=("A8",), label="dbgcp")
                tap("t_dcp", prod_v[0][:], reads=[bf("prod", 0)], extra=[t_cpd])
                tap("t_vcp", prod_v[1][:], reads=[bf("prod", 1)], extra=[t_cpd])

            ft_last = None
            first_ft = True
            for fc in range(NFC):
                hh = fc // 2
                if fc:
                    yield
                for oi in (2, 3, 0, 1):
                    m = (2 * oi) * 4 + hh

                    def fn_fm(te, oi=oi, fc=fc, m=m, first=first_ft, osl=osl):
                        return te.matmul(ps_ft[0:56, 0:512], bigcol[:, 112 - m:168 - m][:, 0:56],
                                         osl(oi, fc), start=first, stop=False)
                    t_f = mk("tensor", fn_fm, reads=[obuf(oi, fc), bf("bigcol")],
                             writes=[bf("ps_ft")] if first_ft else [],
                             extra=[ft_last], label=f"ftm{p_i}_{h2}_{fc}_{oi}")
                    if not first_ft:
                        BUFS["ps_ft"].readers.append(t_f)
                    ft_last = t_f
                    first_ft = False
                for oi in range(4):
                    pi = oi % 2

                    def fn_sq3(sc, oi=oi, fc=fc, pi=pi, osl=osl):
                        return sc.activation(sq_v[pi][:], osl(oi, fc), AF.Square)
                    t_sq = mk("scalar", fn_sq3, reads=[obuf(oi, fc)], writes=[bf("sq", pi)],
                              fresh=("A8",) if (fc == 0 and oi == 0) else (),
                              label=f"fts{p_i}_{h2}_{fc}_{oi}")
                    m = (2 * oi + 1) * 4 + hh

                    def fn_fv(te, m=m, pi=pi):
                        return te.matmul(ps_ft[0:56, 0:512], bigcol[:, 112 - m:168 - m][:, 0:56],
                                         sq_v[pi][:], start=False, stop=False)
                    t_f = mk("tensor", fn_fv, reads=[bf("sq", pi), bf("bigcol")],
                             extra=[t_sq, ft_last], label=f"ftv{p_i}_{h2}_{fc}_{oi}")
                    BUFS["ps_ft"].readers.append(t_f)
                    ft_last = t_f
                cidx = 0
                for oi in range(4):
                    for oj in range(oi + 1, 4):
                        pi = cidx % 2

                        def fn_pr(v, oi=oi, oj=oj, fc=fc, pi=pi, osl=osl):
                            return v.tensor_tensor(prod_v[pi][:], osl(oi, fc), osl(oj, fc),
                                                   op=ALU.mult)
                        t_pr = mk("vector", fn_pr, reads=[obuf(oi, fc), obuf(oj, fc)],
                                  writes=[bf("prod", pi)],
                                  fresh=("A8",) if (fc == 0 and cidx == 0) else (),
                                  label=f"ftp{p_i}_{h2}_{fc}_{cidx}")
                        m = (8 + cidx) * 4 + hh
                        lastmm = (fc == NFC - 1 and cidx == 5)

                        def fn_fc2(te, m=m, pi=pi, lastmm=lastmm):
                            return te.matmul(ps_ft[0:56, 0:512],
                                             bigcol[:, 112 - m:168 - m][:, 0:56],
                                             prod_v[pi][:], start=False, stop=lastmm)
                        t_f = mk("tensor", fn_fc2, reads=[bf("prod", pi), bf("bigcol")],
                                 extra=[t_pr, ft_last], label=f"ftc{p_i}_{h2}_{fc}_{cidx}")
                        BUFS["ps_ft"].readers.append(t_f)
                        ft_last = t_f
                        cidx += 1

            def fn_ftp1(v):
                v.tensor_scalar(rf_tail[0:56, :], ps_ft[0:56, 0:512], 1.0 / DH, None,
                                op0=ALU.mult)
                return v.stream_shuffle(sq_v[0][:], rf_tail[:, 0:512], shuf_mask)
            t_ftp1 = mk("vector", fn_ftp1, reads=[bf("ps_ft")],
                        writes=[bf("rf_tail"), bf("sq", 0)],
                        extra=[ft_last], label=f"ftp1_{p_i}_{h2}")

            def fn_ftp2(v):
                v.tensor_tensor(prod_v[0][0:56, :], sq_v[0][0:56, :], sq_v[0][0:56, :],
                                op=ALU.mult)
                v.tensor_scalar(prod_v[1][0:56, :], ps_ft[0:56, 0:512],
                                1.0 / (DH - 1), None, op0=ALU.mult)
                v.scalar_tensor_tensor(prod_v[0][0:56, :], prod_v[0][0:56, :],
                                       -float(DH) / (DH - 1), prod_v[1][0:56, :],
                                       op0=ALU.mult, op1=ALU.add)
                v.tensor_scalar(rf_tail[0:56, :], rf_tail[0:56, :],
                                rowmask[0:56, 0:1], None, op0=ALU.mult)
                return v.scalar_tensor_tensor(rf_tail[0:56, :], prod_v[0][0:56, :],
                                              rowmask[0:56, 1:2], rf_tail[0:56, :],
                                              op0=ALU.mult, op1=ALU.add)
            t_ftp = mk("vector", fn_ftp2, reads=[bf("ps_ft"), bf("sq", 0), bf("rowmask")],
                       writes=[bf("rf_tail"), bf("prod", 0), bf("prod", 1)],
                       extra=[t_ftp1], label=f"ftp2_{p_i}_{h2}")

            if p_i == 0 and h2 == 0:
                tap("t_rf", rf_tail[:], reads=[bf("rf_tail")], extra=[t_ftp])

            lg_tasks = [None] * 4
            for mcg in range(9):
                yield
                mcs = [2 * mcg] + ([2 * mcg + 1] if 2 * mcg + 1 < 17 else [])
                dbi = mcg % 2
                mw = min(256, 2160 - mcg * 256)
                t_lr = mk("sync", lambda q, mcg=mcg, dbi=dbi: [
                    q.dma_start(out=rw1_v[dbi], in_=s_w["rw1"][mcg])],
                    writes=[bf("rw1", dbi)],
                    fresh=("A5",) if mcg == 0 else (),
                    extra=[pc["rw1"]], n_dma=1, label=f"lr{p_i}_{h2}_{mcg}")
                for mi, mc in enumerate(mcs):
                    rows = 128 if mc < 16 else 112
                    hbi = mc % 2

                    def fn_rmm(te, mc=mc, mi=mi, rows=rows, dbi=dbi, hbi=hbi, h2=h2):
                        o = []
                        for kc in range(8):
                            o.append(te.matmul(
                                ps_mm[hbi][0:rows, :],
                                rw1_v[dbi][:, kc * 256 + mi * 128:kc * 256 + mi * 128 + rows],
                                xn_fm[:, kc * 1024 + h2 * 512:kc * 1024 + (h2 + 1) * 512],
                                start=(kc == 0), stop=False))
                        o.append(te.matmul(
                            ps_mm[hbi][0:rows, :],
                            rw1_v[dbi][0:56, 8 * 256 + mi * 128:8 * 256 + mi * 128 + rows],
                            rf_tail[0:56, :], start=False, stop=True))
                        return o[-1]
                    t_rm = mk("tensor", fn_rmm, reads=[bf("rw1", dbi), bf("xn_fm"),
                                                       bf("rf_tail")],
                              writes=[bf(f"ps_mm{hbi}")], extra=[t_lr, t_ftp],
                              label=f"rt{p_i}_{h2}_{mc}")

                    def fn_rev(sc, mc=mc, rows=rows, hbi=hbi):
                        return sc.activation(h1t[0:rows, hbi * 512:(hbi + 1) * 512],
                                             ps_mm[hbi][0:rows, :], AF.Silu,
                                             bias=rb1_r[0:rows, mc:mc + 1])
                    t_re = mk("scalar", fn_rev, reads=[bf(f"ps_mm{hbi}"), bf("auxf")],
                              writes=[bf("h1t")], extra=[t_rm], label=f"rte{p_i}_{h2}_{mc}")

                    def fn_lg(te, mc=mc, hbi=hbi):
                        o = []
                        rows2 = 128 if mc < 16 else 112
                        for tcl in range(4):
                            o.append(te.matmul(
                                ps_ft[:, tcl * 128:tcl * 128 + 16],
                                h1t[0:rows2, hbi * 512 + tcl * 128:hbi * 512 + (tcl + 1) * 128],
                                rw2_r[0:rows2, mc * 16:(mc + 1) * 16],
                                start=(mc == 0 and tcl == 0), stop=(mc == 16)))
                        return o[-1]
                    t_lg = mk("tensor", fn_lg, reads=[bf("h1t"), bf("rw2_r")],
                              writes=[bf("ps_ft")] if mc == 0 else [],
                              extra=[t_re] + ([lg_tasks[0]] if mc > 0 else []),
                              label=f"lg{p_i}_{h2}_{mc}")
                    if mc != 0:
                        BUFS["ps_ft"].readers.append(t_lg)
                    lg_tasks[0] = t_lg
            t_lg_last = lg_tasks[0]

            pfm_done = []
            for tcl in range(4):
                yield
                def fn_sm1(v, tcl=tcl):
                    v.tensor_tensor(p_tm[:], ps_ft[:, tcl * 128:tcl * 128 + 16],
                                    rb2_bc[:], op=ALU.add)
                    v.tensor_copy(smallb[:, 0:2], smallb[:, 0:2])
                    v.reduce_sum(smallb[:, 12:16].unsqueeze(2),
                                 p_tm[:].rearrange("p (g i) -> p g i", i=4), axis=AX)
                    v.tensor_copy(smallb[:, 0:2], smallb[:, 0:2])
                    v.tensor_scalar(smallb[:, 12:16], smallb[:, 12:16], 0.25, None,
                                    op0=ALU.mult)
                    v.tensor_copy(smallb[:, 0:2], smallb[:, 0:2])
                    mb = smallb[:, 12:16].unsqueeze(2).broadcast_to([128, 4, 4])
                    return v.tensor_tensor(p_tm[:].rearrange("p (g i) -> p g i", i=4),
                                           p_tm[:].rearrange("p (g i) -> p g i", i=4), mb,
                                           op=ALU.subtract)
                t_s1 = mk("vector", fn_sm1, reads=[bf("ps_ft"), bf("auxf")],
                          writes=[bf("p_tm"), bf("smallb")],
                          extra=[t_lg_last], label=f"sm1{p_i}_{h2}_{tcl}")
                t_s2 = mk("scalar", lambda sc: sc.activation(p_tm[:], p_tm[:], AF.Exp),
                          writes=[bf("p_tm")], extra=[t_s1], label=f"sm2{p_i}_{h2}_{tcl}")

                def fn_sm3(v):
                    v.reduce_sum(smallb[:, 12:16].unsqueeze(2),
                                 p_tm[:].rearrange("p (g i) -> p g i", i=4), axis=AX)
                    v.tensor_copy(smallb[:, 0:2], smallb[:, 0:2])
                    v.reciprocal(smallb[:, 12:16], smallb[:, 12:16])
                    v.tensor_copy(smallb[:, 0:2], smallb[:, 0:2])
                    rb = smallb[:, 12:16].unsqueeze(2).broadcast_to([128, 4, 4])
                    v.tensor_tensor(p_tm[:].rearrange("p (g i) -> p g i", i=4),
                                    p_tm[:].rearrange("p (g i) -> p g i", i=4), rb,
                                    op=ALU.mult)
                    v.tensor_copy(smallb[:, 0:2], smallb[:, 0:2])
                    return v.tensor_scalar(p_tmb[:], p_tm[:], 0.96, 0.01,
                                           op0=ALU.mult, op1=ALU.add)
                t_s3 = mk("vector", fn_sm3, writes=[bf("p_tm"), bf("p_tmb"), bf("smallb")],
                          extra=[t_s2], label=f"sm3{p_i}_{h2}_{tcl}")
                t_pt = mk("tensor", lambda te: te.transpose(bcb[0:16, 256:384], p_tmb[:],
                                                            eye[:]),
                          reads=[bf("p_tmb"), bf("eye")], writes=[bf("ps_bc")],
                          extra=[t_s3], label=f"pt{p_i}_{h2}_{tcl}")
                pfm_done.append(mk("vector",
                                   lambda v, tcl=tcl: v.tensor_copy(
                                       p_fm[0:16, tcl * 128:(tcl + 1) * 128],
                                       bcb[0:16, 256:384]),
                                   reads=[bf("ps_bc")], writes=[bf("p_fm")],
                                   extra=[t_pt], label=f"pte{p_i}_{h2}_{tcl}"))

            if p_i == 0 and h2 == 0:
                tap("t_pfm", p_fm[:], reads=[bf("p_fm")], extra=pfm_done)

            tail_ctx[h2] = (osl, obuf, pfm_done)
            yield

        def emit_tail_b(h2):
            osl, obuf, pfm_done = tail_ctx[h2]
            rms_last = None
            for hh in range(H):
                yield
                pb_done = []
                for oi in range(4):
                    idx = hh * 4 + oi

                    def fn_pb(te, idx=idx):
                        return te.matmul(ps_bc[:], sel16[:, idx * 128:(idx + 1) * 128],
                                         p_fm[0:16, 0:512], start=True, stop=True)
                    t_pb = mk("tensor", fn_pb, reads=[bf("p_fm"), bf("sel16")],
                              writes=[bf("ps_bc")], extra=pfm_done, label=f"pb{p_i}_{h2}_{idx}")
                    pb_done.append(mk("vector",
                                      lambda v, oi=oi: v.tensor_copy(
                                          pb_v[:, oi * 512:(oi + 1) * 512], ps_bc[:]),
                                      reads=[bf("ps_bc")], writes=[bf("pb", oi)],
                                      extra=[t_pb], label=f"pbe{p_i}_{h2}_{idx}"))
                for fc in (2 * hh, 2 * hh + 1):
                    def fn_mix(v, fc=fc, osl=osl):
                        msl = mixed_v[:, fc * 512:(fc + 1) * 512]
                        v.tensor_tensor(msl, osl(0, fc), pb_v[:, 0:512], op=ALU.mult)
                        last = None
                        for oi in range(1, 4):
                            v.tensor_tensor(prod_v[0][:], osl(oi, fc),
                                            pb_v[:, oi * 512:(oi + 1) * 512], op=ALU.mult)
                            last = v.tensor_tensor(msl, msl, prod_v[0][:], op=ALU.add)
                        return last
                    t_mx = mk("vector", fn_mix,
                              reads=[obuf(oi, fc) for oi in range(4)]
                              + [bf("pb", oi) for oi in range(4)],
                              writes=[bf("mixed", fc), bf("prod", 0)],
                              extra=pb_done, label=f"mix{p_i}_{h2}_{fc}")

                    t_m2 = mk("scalar", lambda sc, fc=fc: sc.activation(
                        sq_v[0][:], mixed_v[:, fc * 512:(fc + 1) * 512], AF.Square),
                        reads=[bf("mixed", fc)], writes=[bf("sq", 0)], extra=[t_mx],
                        label=f"m2{p_i}_{h2}_{fc}")
                    t_m2w = mk("vector", lambda v, fc=fc: v.tensor_scalar(
                        sq_v[1][:], sq_v[0][:], w2col[:, fc:fc + 1], None, op0=ALU.mult),
                        reads=[bf("sq", 0), bf("w2col")], writes=[bf("sq", 1)], extra=[t_m2],
                        label=f"m2w{p_i}_{h2}_{fc}")

                    def fn_rmm(te, hh=hh, first=(fc == 0), last_=(fc == NFC - 1)):
                        te.matmul(ps_ms[0:4, 0:512], bigcol[:, 112 - hh:120 - hh][:, 0:4],
                                  sq_v[0][:], start=first, stop=last_)
                        return te.matmul(ps_su[0:4, 0:512], bigcol[:, 112 - hh:120 - hh][:, 0:4],
                                         sq_v[1][:], start=first, stop=last_)
                    t_rm = mk("tensor", fn_rmm, reads=[bf("sq", 0), bf("sq", 1), bf("bigcol")],
                              writes=[bf("ps_ms"), bf("ps_su")] if fc == 0 else [],
                              extra=[t_m2, t_m2w, rms_last], label=f"rmm{p_i}_{h2}_{fc}")
                    if fc != 0:
                        BUFS["ps_ms"].readers.append(t_rm)
                        BUFS["ps_su"].readers.append(t_rm)
                    rms_last = t_rm

            def fn_rsa(v):
                v.tensor_scalar(rsA[0:4, 0:512], ps_ms[0:4, 0:512], 1.0 / DH, 1e-5,
                                op0=ALU.mult, op1=ALU.add)
                v.reciprocal(rsB[0:4, 0:512], rsA[0:4, 0:512])
                v.tensor_scalar(rsB[0:4, 0:512], rsB[0:4, 0:512], 1.0 / DH, None, op0=ALU.mult)
                v.tensor_tensor(rsB[0:4, 0:512], rsB[0:4, 0:512], ps_su[0:4, 0:512],
                                op=ALU.mult)
                return v.tensor_scalar(rsB[0:4, 0:512], rsB[0:4, 0:512], 1.0, 1e-5,
                                       op0=ALU.mult, op1=ALU.add)
            t_rsa = mk("vector", fn_rsa, reads=[bf("ps_ms"), bf("ps_su")],
                       writes=[bf("rsA"), bf("rsB")], fresh=("A7",), extra=[rms_last],
                       label=f"rsa{p_i}_{h2}")

            def fn_rsb2(sc):
                sc.activation(rsA[0:4, 0:512], rsA[0:4, 0:512], AF.Ln)
                return sc.activation(rsB[0:4, 0:512], rsB[0:4, 0:512], AF.Ln)
            t_rsb = mk("scalar", fn_rsb2, writes=[bf("rsA"), bf("rsB")], extra=[t_rsa],
                       label=f"rsb{p_i}_{h2}")
            t_rsc = mk("vector", lambda v: v.tensor_tensor(rsA[0:4, 0:512], rsA[0:4, 0:512],
                                                           rsB[0:4, 0:512], op=ALU.add),
                       writes=[bf("rsA")], extra=[t_rsb], label=f"rsc{p_i}_{h2}")
            t_rsd = mk("scalar", lambda sc: sc.activation(rsA[0:4, 0:512], rsA[0:4, 0:512],
                                                          AF.Exp, scale=smallb[0:4, 3:4]),
                       reads=[bf("smallb")], writes=[bf("rsA")], extra=[t_rsc],
                       label=f"rsd{p_i}_{h2}")
            t_rse = mk("vector", lambda v: v.tensor_copy(rs_bf2[0:4, :], rsA[0:4, 0:512]),
                       reads=[bf("rsA")], writes=[bf("rs_bf2")], extra=[t_rsd],
                       label=f"rse{p_i}_{h2}")

            for hh in range(H):
                bank, bank_n = (ps_bc, "ps_bc") if hh % 2 == 0 else (ps_pw, "ps_pw")

                def fn_rb(te, hh=hh, bank=bank):
                    return te.matmul(bank[:, 0:512], sel8[0:4, hh * 128:(hh + 1) * 128],
                                     rs_bf2[0:4, :], start=True, stop=True)
                t_rb = mk("tensor", fn_rb, reads=[bf("rs_bf2"), bf("sel8")],
                          writes=[bf(bank_n)], extra=[t_rse], label=f"rb{p_i}_{h2}_{hh}")
                for fc in (2 * hh, 2 * hh + 1):
                    def fn_on(v, fc=fc, bank=bank):
                        msl = mixed_v[:, fc * 512:(fc + 1) * 512]
                        return v.scalar_tensor_tensor(msl, msl, g_col[:, fc:fc + 1],
                                                      bank[:, 0:512],
                                                      op0=ALU.mult, op1=ALU.mult)
                    t_on = mk("vector", fn_on, reads=[bf(bank_n), bf("g_col"),
                                                      bf("mixed", fc)],
                              writes=[bf("mixed", fc)], extra=[t_rb],
                              label=f"on{p_i}_{h2}_{fc}")

            if p_i == 0 and h2 == 0:
                tap("t_rs", rs_bf2[:], reads=[bf("rs_bf2")], extra=[t_rse])
                tap("t_mix", mixed_v, reads=[bf("mixed", i) for i in range(NFC)])

            last_lwo = mk("sync", lambda q: [q.dma_start(out=wo_v, in_=s_w["wo"][:])],
                          writes=[bf("wo_s")], fresh=("A5",), extra=[pc["wo"]], n_dma=1,
                          label=f"lwo{p_i}_{h2}")
            for tcl in range(4):
                yield
                tc_g = h2 * 4 + tcl
                t_xl2 = mk("sync", lambda q, s=seq, t0=tok0, tc_g=tc_g: [
                    q.dma_start(out=x_t[:], in_=dx[s, t0 + tc_g * 128:t0 + (tc_g + 1) * 128, :])],
                    writes=[bf("x_t")], n_dma=1, label=f"xl2{p_i}_{tc_g}")
                acc_evs = []
                for n2 in range(2):
                    bk = n2

                    def fn_wo(te, tcl=tcl, n2=n2, bk=bk, h2=h2):
                        o = []
                        for kc in range(NFC):
                            o.append(te.matmul(
                                ps_mm[bk][:],
                                mixed_v[:, kc * 512 + tcl * 128:kc * 512 + (tcl + 1) * 128],
                                wo_v[:, kc * 1024 + n2 * 512:kc * 1024 + (n2 + 1) * 512],
                                start=(kc == 0), stop=(kc == NFC - 1)))
                        return o[-1]
                    t_wo2 = mk("tensor", fn_wo,
                               reads=[bf("mixed", kc) for kc in range(NFC)] + [bf("wo_s")],
                               writes=[bf(f"ps_mm{bk}")], extra=[last_lwo],
                               label=f"wo{p_i}_{h2}_{tcl}_{n2}")

                    def fn_acc(v, n2=n2, bk=bk):
                        return v.tensor_tensor(ln_scr[:, n2 * 512:(n2 + 1) * 512], x_t[:, n2 * 512:(n2 + 1) * 512],
                                               ps_mm[bk][:], op=ALU.add)
                    acc_evs.append(mk("vector", fn_acc, reads=[bf(f"ps_mm{bk}"), bf("x_t")],
                                      writes=[bf("ln_scr")],
                                      fresh=("A7",) if (tcl == 0 and n2 == 0) else (),
                                      extra=[t_wo2, t_xl2],
                                      label=f"acc{p_i}_{h2}_{tcl}_{n2}"))
                mk("gpsimd", lambda q, tc_g=tc_g: [
                    q.dma_start(out=dacc[tc_g * 128:(tc_g + 1) * 128, :], in_=ln_scr[:, 0:1024])],
                    reads=[bf("ln_scr")], writes=[bf("dacc", tc_g)], extra=acc_evs,
                    n_dma=1, label=f"accd{p_i}_{tc_g}")
                if p_i == 0 and h2 == 0 and tcl == 0:
                    tap("t_acc", ln_scr[:, 0:1024], reads=[bf("ln_scr")], extra=acc_evs)

            yield


        def pump(g, k):
            if g is None:
                return None
            for _ in range(k):
                if next(g, "END") == "END":
                    return None
            return g

        gen_a0 = emit_tail_a(0)
        pend = None
        for gi, (n, hg) in enumerate([(n, hg) for n in range(NTC) for hg in range(2)]):
            st = delta_phase1(n, hg, gi % 2)
            if pend is not None:
                delta_phase2(*pend)
            pend = (n, hg, gi % 2, st)
            if gi >= 9:
                gen_a0 = pump(gen_a0, 3)
        delta_phase2(*pend)
        while gen_a0 is not None:
            gen_a0 = pump(gen_a0, 8)

        if p_i == 0:
            tap("t_del", del_fm[:], reads=[bf("del_fm", 0), bf("del_fm", 1)])
            tap("t_sst", s_state[:], reads=[bf("s_state")])
        if p_i == 1:
            tap("t_del1", del_fm[:], reads=[bf("del_fm", 0), bf("del_fm", 1)])

        for _ in emit_tail_b(0):
            pass
        for _ in emit_tail_a(1):
            pass
        for _ in emit_tail_b(1):
            pass

        for h2 in range(2):
            ffn1_done = []
            t_lw1 = {}
            for blk in range(4):
                dbi = blk % 2
                t_lw1[blk] = mk("sync", lambda q, blk=blk, dbi=dbi: [
                    q.dma_start(out=w1t_v[dbi], in_=s_w["w1t"][blk])],
                    writes=[bf("w1t", dbi)],
                    fresh=("A4",) if (h2 == 0 and blk == 0) else (),
                    extra=[pc["w1t"]], n_dma=1, label=f"lw1{p_i}_{h2}_{blk}")
                for hi in range(8):
                    hc = blk * 8 + hi
                    bk = hc % 2

                    def fn_f1(te, hi=hi, dbi=dbi, bk=bk, h2=h2):
                        o = []
                        for kc in range(NFC):
                            o.append(te.matmul(
                                ps_mm[bk][:],
                                w1t_v[dbi][:, kc * 1024 + hi * 128:kc * 1024 + (hi + 1) * 128],
                                xn_fm[:, kc * 1024 + h2 * 512:kc * 1024 + (h2 + 1) * 512],
                                start=(kc == 0), stop=(kc == NFC - 1)))
                        return o[-1]
                    t_f1 = mk("tensor", fn_f1, reads=[bf("w1t", dbi), bf("xn_fm")],
                              writes=[bf(f"ps_mm{bk}")], extra=[t_lw1[blk]],
                              label=f"f1{p_i}_{h2}_{hc}")

                    def fn_f1e(sc, hc=hc, bk=bk):
                        return sc.activation(hq_v[:, hc * 512:(hc + 1) * 512], ps_mm[bk][:],
                                             AF.Relu, bias=b1_r[:, hc:hc + 1])
                    ffn1_done.append(mk("scalar", fn_f1e, reads=[bf(f"ps_mm{bk}"), bf("auxf")],
                                        writes=[bf("hq")],
                                        fresh=("A4",) if (h2 == 0 and hc == 0) else (),
                                        extra=[t_f1],
                                        label=f"f1e{p_i}_{h2}_{hc}"))

            if p_i == 0 and h2 == 0:
                tap("t_hq", hq_v, reads=[bf("hq")], extra=[ffn1_done[-1]])

            f2_mm = {}
            for c2 in range(2):
                t_lw2 = mk("sync", lambda q, c2=c2: [
                    q.dma_start(out=w2t_v, in_=s_w["w2t"][c2])],
                    writes=[bf("w2t")], fresh=("A5",) if (h2 == 0 and c2 == 0) else (),
                    extra=[pc["w2t"]], n_dma=1, label=f"lw2{p_i}_{h2}_{c2}")
                for tcl in range(4):
                    for n2 in range(2):
                        bidx = tcl * 2 + n2
                        bank = PSH[PSB[bidx]]

                        def fn_f2(te, c2=c2, tcl=tcl, n2=n2, bank=bank):
                            o = []
                            for hi in range(16):
                                hc = c2 * 16 + hi
                                o.append(te.matmul(
                                    bank[:],
                                    hq_v[:, hc * 512 + tcl * 128:hc * 512 + (tcl + 1) * 128],
                                    w2t_v[:, hi * 1024 + n2 * 512:hi * 1024 + (n2 + 1) * 512],
                                    start=(c2 == 0 and hi == 0), stop=False))
                            if c2 == 1:
                                o.append(te.matmul(bank[:], ones32[:],
                                                   bq_row[0:1, n2 * 512:(n2 + 1) * 512],
                                                   start=False, stop=True))
                            return o[-1]
                        t_f2 = mk("tensor", fn_f2,
                                  reads=[bf("hq"), bf("w2t"), bf("ones32"), bf("bq_row")],
                                  writes=[bf(PSB[bidx])] if c2 == 0 else [],
                                  extra=[t_lw2, ffn1_done[-1], f2_mm.get(bidx)],
                                  label=f"f2{p_i}_{h2}_{tcl}_{n2}_{c2}")
                        if c2 == 1:
                            BUFS[PSB[bidx]].readers.append(t_f2)
                        f2_mm[bidx] = t_f2

            for tcl in range(4):
                tc_g = h2 * 4 + tcl
                t_al = mk("sync", lambda q, tc_g=tc_g: [
                    q.dma_start(out=x_t[:], in_=dacc[tc_g * 128:(tc_g + 1) * 128, :])],
                    reads=[bf("dacc", tc_g)], writes=[bf("x_t")], n_dma=1,
                    label=f"al{p_i}_{tc_g}")
                f2evs = []
                for n2 in range(2):
                    bidx = tcl * 2 + n2
                    bank = PSH[PSB[bidx]]

                    def fn_q16(v, n2=n2, bank=bank):
                        return v.tensor_copy(c16_v[n2][:], bank[:])
                    t_q16 = mk("vector", fn_q16, reads=[bf(PSB[bidx])], writes=[bf("c16", n2)],
                               fresh=("A8",) if (h2 == 0 and tcl == 0 and n2 == 0) else (),
                               extra=[f2_mm[bidx]], label=f"q16{p_i}_{h2}_{tcl}_{n2}")

                    def fn_fin(v, n2=n2):
                        return v.tensor_tensor(x_t[:, n2 * 512:(n2 + 1) * 512],
                                               x_t[:, n2 * 512:(n2 + 1) * 512],
                                               c16_v[n2][:], op=ALU.add)
                    f2evs.append(mk("vector", fn_fin, reads=[bf("c16", n2), bf("x_t")],
                                    writes=[bf("x_t")], extra=[t_q16, t_al],
                                    label=f"fin{p_i}_{h2}_{tcl}_{n2}"))
                final_dma.append(mk("gpsimd", lambda q, s=seq, t0=tok0, tc_g=tc_g: [
                    q.dma_start(out=dout[s, t0 + tc_g * 128:t0 + (tc_g + 1) * 128, :],
                                in_=x_t[:, 0:1024])],
                    reads=[bf("x_t")], extra=f2evs, n_dma=1, label=f"od{p_i}_{tc_g}"))

    mk("sync", lambda q: q.nop(), extra=final_dma + tap_tasks, label="fin")

    BD.finalize()
    with nc.Block() as blk:
        blk.tensor(BD.emitter("tensor", sems))
        blk.vector(BD.emitter("vector", sems))
        blk.scalar(BD.emitter("scalar", sems))
        blk.gpsimd(BD.emitter("gpsimd", sems))
        blk.sync(BD.emitter("sync", sems))
    est.close()
    return nc


_NC_CACHE = {}


def _get_program():
    if "nc" not in _NC_CACHE:
        _NC_CACHE["nc"] = build_program()
    return _NC_CACHE["nc"]


def pack_aux(g):
    aux = np.zeros((128, AUXW), np.float32)
    for fc in range(8):
        b = fc * 44
        aux[:, b + 0:b + 4] = g["conv_q_w"][fc * 128:(fc + 1) * 128, :]
        aux[:, b + 4:b + 8] = g["conv_k_w"][fc * 128:(fc + 1) * 128, :]
        aux[:, b + 8:b + 12] = g["conv_v_w"][fc * 128:(fc + 1) * 128, :]
        aux[:, b + 12:b + 37] = g["mid_w"][fc * 128:(fc + 1) * 128, :]
        aux[:, b + 37:b + 44] = g["local_w"][fc * 128:(fc + 1) * 128, :]
    aux[:, 352:384] = g["b1"].reshape(32, 128).T
    aux[:, 384:400] = g["r_b1"][0:2048].reshape(16, 128).T
    aux[0:112, 400] = g["r_b1"][2048:2160]
    aux[:, 401:409] = g["mix_w"].reshape(-1).reshape(8, 128).T
    aux[:, 409:411] = g["onorm_w"].reshape(2, 128).T
    aux[:, 411:427] = np.broadcast_to(g["r_b2"], (128, 16))
    for fc in range(8):
        aux[:, 427 + fc * 4:427 + (fc + 1) * 4] = g["Wb"][fc * 128:(fc + 1) * 128, :]
    return aux


def kernel(**inputs):
    nc = _get_program()
    consts = host_consts()
    x = np.ascontiguousarray(np.asarray(inputs["L0_attn_hidden_states"], dtype=np.float32))
    wmap = {
        "Wq": "L0_attn_Wq", "Wk": "L0_attn_Wk", "Wv": "L0_attn_Wv", "Wb": "L0_attn_Wb",
        "conv_q_w": "L0_attn_conv_q_w", "conv_k_w": "L0_attn_conv_k_w",
        "conv_v_w": "L0_attn_conv_v_w", "local_w": "L0_attn_local_w",
        "mid_w": "L0_attn_mid_w", "r_w1": "L0_attn_r_w1", "r_b1": "L0_attn_r_b1",
        "r_w2": "L0_attn_r_w2", "r_b2": "L0_attn_r_b2", "mix_w": "L0_attn_mix_w",
        "onorm_w": "L0_attn_onorm_w", "Wo": "L0_attn_Wo",
        "w1": "L0_ffn_w1", "b1": "L0_ffn_b1", "w2": "L0_ffn_w2", "b2": "L0_ffn_b2",
    }
    g = {k: np.ascontiguousarray(np.asarray(inputs[v], dtype=np.float32))
         for k, v in wmap.items()}
    base = {k: g[k] for k in ("Wq", "Wk", "Wv", "r_w1", "r_w2", "Wo", "w1", "w2", "b2")}
    base["aux"] = pack_aux(g)
    base.update(consts)
    in_maps = []
    for c in range(NCORES):
        m = dict(base)
        m["x"] = np.ascontiguousarray(x[c * SPC:(c + 1) * SPC])
        in_maps.append(m)
    res = run_bass_kernel_spmd(nc, in_maps, core_ids=list(range(NCORES)))
    return np.concatenate([r["out"] for r in res.results], axis=0)
